# version 11
# baseline (speedup 1.0000x reference)
import numpy as np
from contextlib import ExitStack

import concourse.bass as bass
import concourse.mybir as mybir
from concourse.bass_utils import run_bass_kernel_spmd

F32 = mybir.dt.float32
BF16 = mybir.dt.bfloat16
I32 = mybir.dt.int32
AF = mybir.ActivationFunctionType
ALU = mybir.AluOpType
AX = mybir.AxisListType

B, S, D, H, DH = 4, 4096, 1024, 16, 64
I0, I1 = 2816, 4096
NCH = D // 128
NPAIR = H // 2
NSLOT = 4
QT = 512
NKT = [8, 16, 24, 32]
SLOPES = [2.0 ** (-0.5 * (i + 1)) for i in range(H)]
NEG = -1.0e38
ASPL = 5
DEBUG = False


def _own_qtiles(half):
    return [1, 3, 5, 7] if half == 0 else [0, 2, 4, 6]


def _own_cols(half):
    cols = []
    for g in _own_qtiles(half):
        cols.extend(range(g * QT, (g + 1) * QT))
    return np.array(cols, dtype=np.int64)


def _real_nkt(half, s):
    return (_own_qtiles(half)[s] + 1) * QT // 128


def _host_consts(half):
    kl = np.arange(128)[:, None]
    ql = np.arange(QT)[None, :]
    strip = np.zeros((128, 8 * QT), dtype=np.float32)
    for rel in range(8):
        if half == 0:
            m = np.ones((128, QT)) if rel < 4 else (kl + 128 * (rel - 4) <= ql)
        else:
            m = (kl + 128 * rel <= ql) if rel < 4 else np.zeros((128, QT))
        strip[:, rel * QT:(rel + 1) * QT] = m.astype(np.float32)

    kvec0 = np.full((128, H * NSLOT * 32), NEG, dtype=np.float32)
    kvec1 = np.full((128, NSLOT * 32), NEG, dtype=np.float32)
    for s in range(NSLOT):
        q_base = _own_qtiles(half)[s] * QT
        realn = _real_nkt(half, s)
        for i in range(min(NKT[s], realn)):
            kvec1[:, s * 32 + i] = 0.0
            for h in range(H):
                r = q_base if h < ASPL else q_base + QT - 1
                v = SLOPES[h] * (128 * i + np.arange(128, dtype=np.float64) - r)
                kvec0[:, (h * NSLOT + s) * 32 + i] = v.astype(np.float32)

    qbias = np.zeros((128, ASPL * QT), dtype=np.float32)
    for h in range(ASPL):
        qbias[:, h * QT:(h + 1) * QT] = (
            -SLOPES[h] * np.arange(QT, dtype=np.float64)).astype(np.float32)[None, :]

    sel = np.zeros((1, NSLOT * 8), dtype=np.float32)
    for s in range(NSLOT):
        sel[0, s * 8 + NKT[s] // 4: s * 8 + 8] = 1.0

    return dict(
        maskstrip=strip,
        kvec0=kvec0, kvec1=kvec1, qbias=qbias,
        ident=np.eye(128, dtype=np.float32),
        onesM=np.full((128, 128), 1.0 / D, dtype=np.float32),
        onesD=np.ones((128, 128), dtype=np.float32),
        onesrow=np.ones((1, QT), dtype=np.float32),
        sel=sel)


ENGS = ("sync", "tensor", "vector", "scalar", "gpsimd")
OWNSEM = {"sync": None, "tensor": "tn", "vector": "vc", "scalar": "sc",
          "gpsimd": "gp"}
ALLSEMS = ("ld", "gd", "tn", "vc", "sc", "gp")


_PBMODE = ["strict"]
_REC = []
_DEP = {}
_DRAINS = set()
SEMBLK = 512


def _ap_ranges(opnd):
    try:
        memref = opnd.memref
        ap = [list(x) for x in opnd.ap]
        off = int(opnd.offset)
    except Exception:
        return None
    if not ap:
        return (memref, 0, 1 << 60)
    pstride = ap[0][0]
    span = 1
    for (st, cnt) in ap[1:]:
        span += abs(st) * (cnt - 1)
    if len(ap) == 1:
        span = 1
    col = off % pstride if pstride > 0 else off
    return (memref, col, col + span)


class PB:
    def __init__(self, nc):
        self.nc = nc
        self.q = {e: [] for e in ENGS}
        self.cnt = {s: 0 for s in ALLSEMS}
        self.lastw = {e: {s: 0 for s in ALLSEMS} for e in ENGS}
        self.need_drain = {e: False for e in ENGS}
        self.gidx = 0

    def _waits(self, eng):
        own = OWNSEM[eng]
        for s in ALLSEMS:
            if s == own:
                continue
            v = self.cnt[s]
            if v > self.lastw[eng][s]:
                self.q[eng].append(("w", s, v))
                self.lastw[eng][s] = v

    def _dep_waits(self, eng, idx):
        own = OWNSEM[eng]
        for (s, v) in _DEP.get(idx, ()):
            if s == own:
                continue
            if v > self.lastw[eng][s]:
                self.q[eng].append(("w", s, v))
                self.lastw[eng][s] = v

    def op(self, eng, fn, dma=False):
        idx = self.gidx
        self.gidx += 1
        mode = _PBMODE[0]
        if mode == "strict":
            self._waits(eng)
        elif mode == "dep":
            self._dep_waits(eng, idx)
        if eng in ("vector", "scalar", "gpsimd") and not dma:
            if mode == "dep":
                if idx in _DRAINS:
                    self.q[eng].append(("d",))
            elif self.need_drain[eng]:
                self.q[eng].append(("d",))
            self.need_drain[eng] = True
        if dma:
            sem = "ld" if eng == "sync" else "gd"
            n = 16
        else:
            sem, n = OWNSEM[eng], 1
            if sem is None:
                sem, n = "ld", 16
        self.cnt[sem] += n
        self.q[eng].append(("o", fn, sem, n, idx, eng, self.cnt[sem]))

    def finalize(self, semh, final_eng="gpsimd"):
        record = _PBMODE[0] == "record"
        self._waits(final_eng)
        own = OWNSEM[final_eng]
        if own is not None:
            self.q[final_eng].append(("w", own, self.cnt[own]))
        with self.nc.Block() as blk:
            for ename, meth in (("sync", blk.sync), ("tensor", blk.tensor),
                                ("vector", blk.vector), ("scalar", blk.scalar),
                                ("gpsimd", blk.gpsimd)):
                items = self.q[ename]
                if not items:
                    continue

                def body(e, items=items):
                    for it in items:
                        if it[0] == "w":
                            e.wait_ge(semh[it[1]], it[2])
                        elif it[0] == "d":
                            e.drain()
                        else:
                            _, fn, sem, n, idx, eng, cafter = it
                            ins = fn(e)
                            if record:
                                mi = getattr(ins, "ins", None)
                                rd, wr = [], []
                                if mi is not None:
                                    try:
                                        _i = mi.ins() if callable(mi.ins) else mi.ins
                                        _o = mi.outs() if callable(mi.outs) else mi.outs
                                        rd = [_ap_ranges(o) for o in _i]
                                        wr = [_ap_ranges(o) for o in _o]
                                    except Exception:
                                        rd, wr = [("ALL", 0, 1 << 60)],                                                  [("ALL", 0, 1 << 60)]
                                    nm = type(mi).__name__
                                    if "Collective" in nm:
                                        rd = [("ALL", 0, 1 << 60)]
                                        wr = [("ALL", 0, 1 << 60)]
                                _REC.append((idx, eng, sem, cafter,
                                             [x for x in rd if x],
                                             [x for x in wr if x]))
                            ins.then_inc(semh[sem], n)
                meth(body)


def _compute_deps(rec):
    rec = sorted(rec, key=lambda r: r[0])
    state = {}
    bar = {}
    deps = {}
    _DRAINS.clear()
    dirty = {}

    def keys_of(m, a, b):
        if m == "ALL":
            return None
        if b - a > 40960:
            return [(m, "W")]
        return [(m, i) for i in range(a // SEMBLK, b // SEMBLK + 1)]

    for (idx, eng, sem, cafter, rd, wr) in rec:
        need = dict(bar)

        def want(ps, pc):
            if ps == sem:
                return
            if need.get(ps, 0) < pc:
                need[ps] = pc

        def check_key(m, k, for_write):
            mm = state.get(m)
            if not mm:
                return
            tgts = list(mm.items()) if k == "W" else                 [(k, mm.get(k)), ("W", mm.get("W"))]
            for kk, st in tgts:
                if not st:
                    continue
                if st.get("w"):
                    want(*st["w"])
                if for_write:
                    for en, cn in st.get("r", {}).items():
                        want(en, cn)

        def upd_read(m, k):
            mm = state.setdefault(m, {})
            st = mm.setdefault(k, {"w": None, "r": {}})
            st["r"][sem] = cafter

        def upd_write(m, k):
            mm = state.setdefault(m, {})
            if k == "W":
                mm.clear()
            st = mm.setdefault(k, {"w": None, "r": {}})
            st["w"] = (sem, cafter)
            st["r"] = {}

        barrier = False
        rks, wks = [], []
        for (m, a, b) in rd:
            ks = keys_of(m, a, b)
            if ks is None:
                barrier = True
            else:
                rks += ks
        for (m, a, b) in wr:
            ks = keys_of(m, a, b)
            if ks is None:
                barrier = True
            else:
                wks += ks
        if barrier:
            for m, mm in state.items():
                for kk, st in mm.items():
                    if st.get("w"):
                        want(*st["w"])
                    for en, cn in st.get("r", {}).items():
                        want(en, cn)
            state.clear()
            bar = dict(bar)
            bar[sem] = cafter
        for (m, k) in rks:
            check_key(m, k, False)
        for (m, k) in wks:
            check_key(m, k, True)
        for (m, k) in rks:
            upd_read(m, k)
        for (m, k) in wks:
            upd_write(m, k)
        if sem not in ("ld", "gd"):
            dd = dirty.setdefault(sem, set())
            touched = set(rks) | set(wks)
            hit = False
            for (m, k) in touched:
                if (m, k) in dd or (m, "W") in dd:
                    hit = True
                    break
            if not hit:
                wholes = {m for (m, k) in touched if k == "W"}
                if wholes and any(m in wholes for (m, _) in dd):
                    hit = True
            if barrier or hit:
                _DRAINS.add(idx)
                dd.clear()
            dd.update(wks)
        if need:
            deps[idx] = sorted(need.items())
    return deps


def fm_dram_slab(src, c0, n):
    return src.rearrange("(c p) s -> p c s", p=128)[:, :, c0:c0 + n]


def sb_slab(t, off, n):
    return t[:, off:off + NCH * n].rearrange("p (c n) -> p c n", n=n)


class Alloc:

    def __init__(self, t, cols):
        self.t = t
        self.cols = cols
        self.off = 0
        self.marks = []

    def get(self, n):
        assert self.off + n <= self.cols, (self.off, n, self.cols)
        off = self.off
        self.off += n
        return off

    def push(self):
        self.marks.append(self.off)

    def pop(self):
        self.off = self.marks.pop()



def quant_fp4(P, scr, x_off, out_off, np_, nblk, prange=128):
    t = scr
    pr = slice(0, prange)
    ax_o = P._q4ax; ac_o = P._q4ac; qq_o = P._q4qq
    bm_o = P._q4bm; rs_o = P._q4rs
    seg = lambda off: t[pr, off:off + np_].rearrange("p (g e) -> p g e", e=16)
    bseg = lambda off: t[pr, off:off + nblk].rearrange(
        "p (g e) -> p g e", e=1).broadcast_to((prange, nblk, 16))
    X = t[pr, x_off:x_off + np_]
    AXs = t[pr, ax_o:ax_o + np_]
    AC = t[pr, ac_o:ac_o + np_]
    QQ = t[pr, qq_o:qq_o + np_]
    BM = t[pr, bm_o:bm_o + nblk]
    RS = t[pr, rs_o:rs_o + nblk]
    OUT = t[pr, out_off:out_off + np_]
    P.op("scalar", lambda e: e.activation(AXs, X, AF.Abs))
    P.op("vector", lambda e: e.tensor_reduce(BM, seg(ax_o), AX.X, ALU.max))
    P.op("vector", lambda e: e.tensor_scalar(BM, BM, 1.0 / 6.0, 1.0e-12,
                                             ALU.mult, ALU.max))
    P.op("vector", lambda e: e.reciprocal(RS, BM))
    P.op("vector", lambda e: e.tensor_tensor(seg(ax_o), seg(ax_o), bseg(rs_o),
                                             ALU.mult))
    P.op("vector", lambda e: e.tensor_scalar(AC, AXs, 0.25, None, ALU.is_gt))
    for mid in (0.75, 1.25, 1.75):
        P.op("vector", lambda e, mid=mid: e.scalar_tensor_tensor(
            AC, AXs, mid, AC, ALU.is_gt, ALU.add))
    P.op("vector", lambda e: e.tensor_scalar(QQ, AXs, 2.5, None, ALU.is_gt))
    for mid in (3.5, 5.0, 5.0):
        P.op("vector", lambda e, mid=mid: e.scalar_tensor_tensor(
            QQ, AXs, mid, QQ, ALU.is_gt, ALU.add))
    P.op("vector", lambda e: e.scalar_tensor_tensor(QQ, AC, 0.5, QQ,
                                                    ALU.mult, ALU.add))
    P.op("scalar", lambda e: e.activation(AC, X, AF.Sign))
    P.op("vector", lambda e: e.tensor_tensor(QQ, QQ, AC, ALU.mult))
    P.op("vector", lambda e: e.tensor_tensor(seg(out_off), seg(qq_o),
                                             bseg(bm_o), ALU.mult))



def build_l0(args_shapes_only=None):
    nc = bass.Bass("TRN2", target_bir_lowering=False)
    dram = lambda n, sh, dt=F32, kind="ExternalInput": nc.dram_tensor(
        n, sh, dt, kind=kind).ap()
    xg = dram("xg", [D, S])
    xo = dram("xo", [D, NSLOT * QT])
    w_in = dram("w_in", [D, 3 * D])
    b_in = dram("b_in", [3 * D, 1])
    w_out = dram("w_out", [D, D])
    b_out = dram("b_out", [128, NCH])
    w_gate = dram("w_gate", [I0, D])
    w_up = dram("w_up", [I0, D])
    w_down = dram("w_down", [D, I0])
    C = {k: dram("c_" + k, list(v_.shape)) for k, v_ in _host_consts(0).items()}
    t2own = dram("t2own", [D, NSLOT * QT], kind="ExternalOutput")
    yf_hbm = dram("yf_hbm", [D, NSLOT * QT],
                  kind="ExternalOutput" if DEBUG else "Internal")
    dbg = dram("dbg", [128, 4096], kind="ExternalOutput") if DEBUG else None
    dbg2 = dram("dbg2", [128, 4096], kind="ExternalOutput") if DEBUG else None
    wgqT = dram("wgqT", [D, I0], kind="Internal")
    wuqT = dram("wuqT", [D, I0], kind="Internal")
    wdqT = dram("wdqT", [I0, D], kind="Internal")

    with ExitStack() as ctx:
        SB = lambda n, sh, dt=F32: ctx.enter_context(nc.sbuf_tensor(n, sh, dt))
        scr_t = SB("scr", [128, 40960])
        cst = {k: SB("s_" + k, list(v_.shape)) for k, v_ in _host_consts(0).items()}
        bout_sb = SB("bout", [128, NCH])
        ps = [ctx.enter_context(nc.psum_tensor(f"pb{i}", [128, 512], F32))
              for i in range(8)]
        semh = {s: ctx.enter_context(nc.semaphore(s)) for s in ALLSEMS}
        P = PB(nc)
        al = Alloc(scr_t, 40960)

        for k in C:
            P.op("sync", lambda e, k=k: e.dma_start(out=cst[k][:], in_=C[k]),
                 dma=True)
        P.op("sync", lambda e: e.dma_start(
            out=bout_sb[:], in_=b_out), dma=True)

        al.push()
        acc_o = al.get(NCH * 2048)
        al.push()
        k_o = al.get(4096)
        q_o = al.get(2048)
        vtm_o = al.get(32 * 130)
        xt_o = al.get(4096)
        w3_o = al.get(3 * 1024)
        wo_o = al.get(1024)
        b3_o = al.get(4)
        vf_o = al.get(512)
        et_o = al.get(2 * 512)
        etmp_o = al.get(512)
        ao_o = al.get(512)
        z_o = al.get(512)
        rz_o = al.get(512)

        T = scr_t
        P.op("gpsimd", lambda e: e.memset(T[:, acc_o:acc_o + NCH * 2048], 0.0))
        P.op("gpsimd", lambda e: e.memset(
            T[:, vtm_o:vtm_o + 32 * 130].rearrange(
                "p (t c) -> p t c", c=130)[:, :, 64:65], 1.0))
        P.op("gpsimd", lambda e: e.memset(
            T[:, vtm_o:vtm_o + 32 * 130].rearrange(
                "p (t c) -> p t c", c=130)[:, :, 129:130], 1.0))

        def production(P, wsrc_cols, bias_rows, x_src, x_ncols, psum, dst_off,
                       wofs, add_bias=True):
            wt, wc0 = wsrc_cols
            P.op("sync", lambda e: e.dma_start(
                out=sb_slab(T, wofs, 128), in_=fm_dram_slab(wt, wc0, 128)),
                dma=True)
            if add_bias:
                P.op("sync", lambda e: e.dma_start(
                    out=T[:, b3_o + bias_rows[1]:b3_o + bias_rows[1] + 1],
                    in_=bias_rows[0][bias_rows[2]:bias_rows[2] + 128, 0:1]),
                    dma=True)
            ntt = x_ncols // 512
            for tt in range(ntt):
                P.op("sync", lambda e, tt=tt: e.dma_start(
                    out=sb_slab(T, xt_o, 512),
                    in_=fm_dram_slab(x_src, tt * 512, 512)), dma=True)
                for ch in range(NCH):
                    P.op("tensor", lambda e, ch=ch: e.matmul(
                        psum[:],
                        T[:, wofs + ch * 128: wofs + ch * 128 + 128],
                        T[:, xt_o + ch * 512: xt_o + ch * 512 + 512],
                        start=(ch == 0), stop=(ch == NCH - 1)))
                if add_bias:
                    P.op("vector", lambda e, tt=tt: e.tensor_scalar(
                        T[:, dst_off + tt * 512: dst_off + tt * 512 + 512],
                        psum[:], T[:, b3_o + bias_rows[1]:b3_o + bias_rows[1] + 1],
                        None, ALU.add))
                else:
                    P.op("vector", lambda e, tt=tt: e.tensor_copy(
                        T[:, dst_off + tt * 512: dst_off + tt * 512 + 512],
                        psum[:]))

        def make_ln(al2):
            lnscr = dict(m=al2.get(512), t=al2.get(512), rs=al2.get(512),
                         sq=al2.get(512), y=al2.get(512))

            def ln_tile(pre_off, out_fn):
                for oc in range(NCH):
                    src = T[:, pre_off + oc * 512: pre_off + oc * 512 + 512]
                    P.op("tensor", lambda e, src=src, oc=oc: e.matmul(
                        ps[6][:], cst["onesM"][:], src,
                        start=(oc == 0), stop=(oc == NCH - 1)))
                    P.op("scalar", lambda e, src=src: e.activation(
                        T[:, lnscr["sq"]:lnscr["sq"] + 512], src, AF.Square))
                    P.op("tensor", lambda e, oc=oc: e.matmul(
                        ps[7][:], cst["onesM"][:],
                        T[:, lnscr["sq"]:lnscr["sq"] + 512],
                        start=(oc == 0), stop=(oc == NCH - 1)))
                M_ = T[:, lnscr["m"]:lnscr["m"] + 512]
                T_ = T[:, lnscr["t"]:lnscr["t"] + 512]
                R_ = T[:, lnscr["rs"]:lnscr["rs"] + 512]
                SQ = T[:, lnscr["sq"]:lnscr["sq"] + 512]
                Y_ = T[:, lnscr["y"]:lnscr["y"] + 512]
                P.op("vector", lambda e: e.tensor_copy(M_, ps[6][:]))
                P.op("vector", lambda e: e.tensor_tensor(T_, ps[6][:], M_, ALU.mult))
                P.op("vector", lambda e: e.tensor_tensor(T_, ps[7][:], T_,
                                                         ALU.subtract))
                P.op("vector", lambda e: e.tensor_scalar(T_, T_, 1.0e-5, None,
                                                         ALU.add))
                P.op("scalar", lambda e: e.activation(R_, T_, AF.Sqrt))
                P.op("vector", lambda e: e.reciprocal(R_, R_))
                for oc in range(NCH):
                    src = T[:, pre_off + oc * 512: pre_off + oc * 512 + 512]
                    P.op("vector", lambda e, src=src: e.tensor_tensor(
                        SQ, src, M_, ALU.subtract))
                    P.op("vector", lambda e: e.tensor_tensor(Y_, SQ, R_, ALU.mult))
                    out_fn(oc)
            return ln_tile, lnscr

        for p in range(NPAIR):
            production(P, (w_in, 128 * p), (b_in, 0, 128 * p), xo,
                       NSLOT * QT, ps[0], q_o, w3_o)
            production(P, (w_in, D + 128 * p), (b_in, 1, D + 128 * p), xg,
                       S, ps[1], k_o, w3_o + 1024)
            P.op("sync", lambda e, p=p: e.dma_start(
                out=sb_slab(T, w3_o + 2048, 128),
                in_=fm_dram_slab(w_in, 2 * D + 128 * p, 128)), dma=True)
            P.op("sync", lambda e, p=p: e.dma_start(
                out=T[:, b3_o + 2:b3_o + 3],
                in_=b_in[2 * D + 128 * p:2 * D + 128 * p + 128, 0:1]), dma=True)
            for tt in range(8):
                P.op("sync", lambda e, tt=tt: e.dma_start(
                    out=sb_slab(T, xt_o, 512),
                    in_=fm_dram_slab(xg, tt * 512, 512)), dma=True)
                for ch in range(NCH):
                    P.op("tensor", lambda e, ch=ch: e.matmul(
                        ps[2][:],
                        T[:, w3_o + 2048 + ch * 128: w3_o + 2048 + ch * 128 + 128],
                        T[:, xt_o + ch * 512: xt_o + ch * 512 + 512],
                        start=(ch == 0), stop=(ch == NCH - 1)))
                P.op("vector", lambda e: e.tensor_scalar(
                    T[:, vf_o:vf_o + 512], ps[2][:],
                    T[:, b3_o + 2:b3_o + 3], None, ALU.add))
                for j in range(4):
                    P.op("tensor", lambda e, j=j: e.transpose(
                        ps[7][:, 0:128],
                        T[:, vf_o + j * 128: vf_o + j * 128 + 128],
                        cst["ident"][:]))
                    base = vtm_o + (tt * 4 + j) * 130
                    P.op("vector", lambda e, base=base: e.tensor_copy(
                        T[:, base:base + 64], ps[7][:, 0:64]))
                    P.op("vector", lambda e, base=base: e.tensor_copy(
                        T[:, base + 65:base + 129], ps[7][:, 64:128]))

            P.op("sync", lambda e, p=p: e.dma_start(
                out=T[:, wo_o:wo_o + 1024].rearrange("q (c n) -> q c n", n=128),
                in_=w_out[128 * p:128 * p + 128, :].rearrange(
                    "q (c n) -> q c n", n=128)), dma=True)
            for s in range(NSLOT):
                nkt = NKT[s]
                for hd in range(2):
                    gh = 2 * p + hd
                    off = 64 * hd
                    pav = ps[2 + hd]
                    pss = ps[hd]
                    for i in range(nkt):
                        P.op("tensor", lambda e, i=i, off=off, pss=pss, s=s:
                             e.matmul(
                                 pss[:],
                                 T[off:off + 64, k_o + 128 * i: k_o + 128 * i + 128],
                                 T[off:off + 64, q_o + s * 512: q_o + s * 512 + 512],
                                 start=True, stop=True,
                                 tile_position=(off, 0)))
                        kvidx = (gh * NSLOT + s) * 32 + i
                        if gh < ASPL:
                            P.op("vector", lambda e, pss=pss, gh=gh:
                                 e.scalar_tensor_tensor(
                                     T[:, etmp_o:etmp_o + 512], pss[:], 0.125,
                                     cst["qbias"][:, gh * QT:(gh + 1) * QT],
                                     ALU.mult, ALU.add))
                            P.op("vector", lambda e, kvidx=kvidx:
                                 e.tensor_scalar(
                                     T[:, etmp_o:etmp_o + 512],
                                     T[:, etmp_o:etmp_o + 512],
                                     cst["kvec0"][:, kvidx:kvidx + 1], 30.0,
                                     ALU.add, ALU.min))
                            P.op("scalar", lambda e, hd=hd:
                                 e.activation(
                                     T[:, et_o + hd * 512: et_o + hd * 512 + 512],
                                     T[:, etmp_o:etmp_o + 512], AF.Exp))
                        else:
                            P.op("scalar", lambda e, kvidx=kvidx, hd=hd, pss=pss:
                                 e.activation(
                                     T[:, et_o + hd * 512: et_o + hd * 512 + 512],
                                     pss[:], AF.Exp,
                                     bias=cst["kvec0"][:, kvidx:kvidx + 1],
                                     scale=0.125))
                        if i >= nkt - 8:
                            rel = i - (nkt - 8)
                            P.op("vector", lambda e, rel=rel, hd=hd:
                                 e.tensor_tensor(
                                     T[:, et_o + hd * 512: et_o + hd * 512 + 512],
                                     T[:, et_o + hd * 512: et_o + hd * 512 + 512],
                                     cst["maskstrip"][:, rel * QT:(rel + 1) * QT],
                                     ALU.mult))
                        vbase = vtm_o + i * 130 + 65 * hd
                        P.op("tensor", lambda e, vbase=vbase, i=i, hd=hd,
                             pav=pav, nkt=nkt: e.matmul(
                                 pav[0:65, :], T[:, vbase:vbase + 65],
                                 T[:, et_o + hd * 512: et_o + hd * 512 + 512],
                                 start=(i == 0), stop=(i == nkt - 1)))
                    P.op("vector", lambda e, pav=pav: e.tensor_copy(
                        T[0:1, z_o:z_o + 512], pav[64:65, :]))
                    P.op("tensor", lambda e: e.matmul(
                        ps[4][0:64, :], cst["onesrow"][0:1, 0:64],
                        T[0:1, z_o:z_o + 512], start=True, stop=True))
                    P.op("vector", lambda e: e.reciprocal(
                        T[0:64, rz_o:rz_o + 512], ps[4][0:64, :]))
                    P.op("vector", lambda e, pav=pav, off=off: e.tensor_tensor(
                        T[off:off + 64, ao_o:ao_o + 512], pav[0:64, :],
                        T[0:64, rz_o:rz_o + 512], ALU.mult))
                if DEBUG and p == 0 and s == 0:
                    for nm, ofs in (("q", q_o), ("k", k_o), ("e", et_o),
                                    ("ao", ao_o)):
                        pass
                    P.op("gpsimd", lambda e: e.dma_start(
                        out=dbg[:, 0:512], in_=T[:, q_o:q_o + 512]), dma=True)
                    P.op("gpsimd", lambda e: e.dma_start(
                        out=dbg[:, 512:1024], in_=T[:, k_o:k_o + 512]), dma=True)
                    P.op("gpsimd", lambda e: e.dma_start(
                        out=dbg[:, 1024:1154], in_=T[:, vtm_o:vtm_o + 130]),
                        dma=True)
                    P.op("gpsimd", lambda e: e.dma_start(
                        out=dbg[:, 1536:2048], in_=T[:, et_o:et_o + 512]),
                        dma=True)
                    P.op("gpsimd", lambda e: e.dma_start(
                        out=dbg[:, 2048:2560], in_=T[:, ao_o:ao_o + 512]),
                        dma=True)
                    P.op("vector", lambda e: e.tensor_copy(
                        T[:, etmp_o:etmp_o + 512], ps[4][0:128, :]))
                    P.op("gpsimd", lambda e: e.dma_start(
                        out=dbg[:, 2560:3072], in_=T[:, etmp_o:etmp_o + 512]),
                        dma=True)
                for oc in range(NCH):
                    P.op("tensor", lambda e, oc=oc: e.matmul(
                        ps[5][:], T[:, wo_o + oc * 128: wo_o + oc * 128 + 128],
                        T[:, ao_o:ao_o + 512], start=True, stop=True))
                    dst = acc_o + oc * 2048 + s * 512
                    P.op("vector", lambda e, dst=dst: e.tensor_tensor(
                        T[:, dst:dst + 512], T[:, dst:dst + 512], ps[5][:],
                        ALU.add))

        al.pop()
        al.push()
        pre_o = al.get(4096)
        xr_o = al.get(4096)
        ln1, lnscr1 = make_ln(al)

        for s in range(NSLOT):
            P.op("sync", lambda e, s=s: e.dma_start(
                out=sb_slab(T, xr_o, 512), in_=fm_dram_slab(xo, s * 512, 512)),
                dma=True)
            for oc in range(NCH):
                P.op("vector", lambda e, s=s, oc=oc: e.scalar_tensor_tensor(
                    T[:, pre_o + oc * 512: pre_o + oc * 512 + 512],
                    T[:, acc_o + oc * 2048 + s * 512:
                      acc_o + oc * 2048 + s * 512 + 512],
                    bout_sb[:, oc:oc + 1],
                    T[:, xr_o + oc * 512: xr_o + oc * 512 + 512],
                    ALU.add, ALU.add))
            if DEBUG and s == 0:
                P.op("gpsimd", lambda e: e.dma_start(
                    out=dbg[:, 3072:3584], in_=T[:, acc_o:acc_o + 512]),
                    dma=True)
                P.op("gpsimd", lambda e: e.dma_start(
                    out=dbg[:, 3584:4096], in_=T[:, pre_o:pre_o + 512]),
                    dma=True)

            def wr(oc, s=s):
                yo = lnscr1["y"]
                P.op("gpsimd", lambda e, oc=oc, s=s, yo=yo: e.dma_start(
                    out=yf_hbm.rearrange("(c p) s -> p c s", p=128)[
                        :, oc:oc + 1, s * 512:(s + 1) * 512].squeeze(1),
                    in_=T[:, yo:yo + 512]), dma=True)
            ln1(pre_o, wr)
        al.pop()
        al.pop()

        al.push()
        wt_o = al.get(2816)
        wq_o = al.get(2816)
        P._q4ax = al.get(2816)
        P._q4ac = al.get(2816)
        P._q4qq = al.get(2816)
        P._q4bm = al.get(176)
        P._q4rs = al.get(176)
        tb_o = al.get(128)
        for (wsrc, wdst, rows, cols) in ((w_gate, wgqT, I0, D),
                                         (w_up, wuqT, I0, D),
                                         (w_down, wdqT, D, I0)):
            nrt = rows // 128
            for rt in range(nrt):
                P.op("sync", lambda e, wsrc=wsrc, rt=rt, cols=cols: e.dma_start(
                    out=T[:, wt_o:wt_o + cols],
                    in_=wsrc[rt * 128:(rt + 1) * 128, :]), dma=True)
                quant_fp4(P, T, wt_o, wq_o, cols, cols // 16)
                for j in range(cols // 128):
                    P.op("tensor", lambda e, j=j: e.transpose(
                        ps[7][:, 0:128],
                        T[:, wq_o + j * 128: wq_o + j * 128 + 128],
                        cst["ident"][:]))
                    P.op("vector", lambda e: e.tensor_copy(
                        T[:, tb_o:tb_o + 128], ps[7][:, 0:128]))
                    P.op("gpsimd", lambda e, wdst=wdst, j=j, rt=rt: e.dma_start(
                        out=wdst[j * 128:(j + 1) * 128,
                                 rt * 128:(rt + 1) * 128],
                        in_=T[:, tb_o:tb_o + 128]), dma=True)
        al.pop()

        al.push()
        yf_o = al.get(4096)
        xtm_o = al.get(1024)
        xqtm_o = al.get(1024)
        xq_o = al.get(4096)
        hA_o = al.get(22 * 512)
        hB_o = al.get(11 * 128)
        hBq_o = al.get(11 * 128)
        wg2_o = al.get(1024)
        wu2_o = al.get(1024)
        wd2_o = al.get(1024)
        sg_o = al.get(512)
        P._q4ax = al.get(1408)
        P._q4ac = al.get(1408)
        P._q4qq = al.get(1408)
        P._q4bm = al.get(88)
        P._q4rs = al.get(88)
        pre2_o = al.get(4096)
        ln2, lnscr2 = make_ln(al)
        assert al.off <= 40960, al.off

        for s in range(NSLOT):
            P.op("sync", lambda e, s=s: e.dma_start(
                out=sb_slab(T, yf_o, 512), in_=fm_dram_slab(yf_hbm, s * 512, 512)),
                dma=True)
            for sub in range(4):
                for ch in range(NCH):
                    P.op("tensor", lambda e, ch=ch, sub=sub: e.transpose(
                        ps[7][:, 0:128],
                        T[:, yf_o + ch * 512 + sub * 128:
                          yf_o + ch * 512 + sub * 128 + 128],
                        cst["ident"][:]))
                    P.op("vector", lambda e, ch=ch: e.tensor_copy(
                        T[:, xtm_o + ch * 128: xtm_o + ch * 128 + 128],
                        ps[7][:, 0:128]))
                quant_fp4(P, T, xtm_o, xqtm_o, 1024, 64)
                for ch in range(NCH):
                    P.op("tensor", lambda e, ch=ch: e.transpose(
                        ps[7][:, 0:128],
                        T[:, xqtm_o + ch * 128: xqtm_o + ch * 128 + 128],
                        cst["ident"][:]))
                    P.op("vector", lambda e, ch=ch, sub=sub: e.tensor_copy(
                        T[:, xq_o + ch * 512 + sub * 128:
                          xq_o + ch * 512 + sub * 128 + 128],
                        ps[7][:, 0:128]))
            if DEBUG and s == 0:
                P.op("gpsimd", lambda e: e.dma_start(
                    out=dbg2[:, 0:512], in_=T[:, xq_o:xq_o + 512]), dma=True)
                P.op("gpsimd", lambda e: e.dma_start(
                    out=dbg2[:, 2048:3072], in_=T[:, yf_o:yf_o + 1024]), dma=True)
            for ic in range(22):
                P.op("sync", lambda e, ic=ic: e.dma_start(
                    out=sb_slab(T, wg2_o, 128),
                    in_=fm_dram_slab(wgqT, ic * 128, 128)), dma=True)
                P.op("sync", lambda e, ic=ic: e.dma_start(
                    out=sb_slab(T, wu2_o, 128),
                    in_=fm_dram_slab(wuqT, ic * 128, 128)), dma=True)
                for ch in range(NCH):
                    P.op("tensor", lambda e, ch=ch: e.matmul(
                        ps[0][:], T[:, wg2_o + ch * 128: wg2_o + ch * 128 + 128],
                        T[:, xq_o + ch * 512: xq_o + ch * 512 + 512],
                        start=(ch == 0), stop=(ch == NCH - 1)))
                for ch in range(NCH):
                    P.op("tensor", lambda e, ch=ch: e.matmul(
                        ps[1][:], T[:, wu2_o + ch * 128: wu2_o + ch * 128 + 128],
                        T[:, xq_o + ch * 512: xq_o + ch * 512 + 512],
                        start=(ch == 0), stop=(ch == NCH - 1)))
                P.op("scalar", lambda e: e.activation(
                    T[:, sg_o:sg_o + 512], ps[0][:], AF.Silu))
                P.op("vector", lambda e, ic=ic: e.tensor_tensor(
                    T[:, hA_o + ic * 512: hA_o + ic * 512 + 512],
                    T[:, sg_o:sg_o + 512], ps[1][:], ALU.mult))
            if DEBUG and s == 0:
                P.op("gpsimd", lambda e: e.dma_start(
                    out=dbg2[:, 512:1024], in_=T[:, hA_o:hA_o + 512]), dma=True)
            for sub in range(4):
                for g in range(2):
                    for jj in range(11):
                        ic = g * 11 + jj
                        P.op("tensor", lambda e, ic=ic, sub=sub: e.transpose(
                            ps[7][:, 0:128],
                            T[:, hA_o + ic * 512 + sub * 128:
                              hA_o + ic * 512 + sub * 128 + 128],
                            cst["ident"][:]))
                        P.op("vector", lambda e, jj=jj: e.tensor_copy(
                            T[:, hB_o + jj * 128: hB_o + jj * 128 + 128],
                            ps[7][:, 0:128]))
                    quant_fp4(P, T, hB_o, hBq_o, 1408, 88)
                    for jj in range(11):
                        ic = g * 11 + jj
                        P.op("tensor", lambda e, jj=jj: e.transpose(
                            ps[7][:, 0:128],
                            T[:, hBq_o + jj * 128: hBq_o + jj * 128 + 128],
                            cst["ident"][:]))
                        P.op("vector", lambda e, ic=ic, sub=sub: e.tensor_copy(
                            T[:, hA_o + ic * 512 + sub * 128:
                              hA_o + ic * 512 + sub * 128 + 128],
                            ps[7][:, 0:128]))
            if DEBUG and s == 0:
                P.op("gpsimd", lambda e: e.dma_start(
                    out=dbg2[:, 1024:1536], in_=T[:, hA_o:hA_o + 512]), dma=True)
                P.op("gpsimd", lambda e: e.dma_start(
                    out=dbg2[:, 3072:4096], in_=T[:, wg2_o:wg2_o + 1024]), dma=True)
            for ic in range(22):
                P.op("sync", lambda e, ic=ic: e.dma_start(
                    out=T[:, wd2_o:wd2_o + 1024],
                    in_=wdqT[ic * 128:(ic + 1) * 128, :]), dma=True)
                for hc in range(NCH):
                    P.op("tensor", lambda e, ic=ic, hc=hc: e.matmul(
                        ps[hc][:],
                        T[:, wd2_o + hc * 128: wd2_o + hc * 128 + 128],
                        T[:, hA_o + ic * 512: hA_o + ic * 512 + 512],
                        start=(ic == 0), stop=(ic == 21)))
            for hc in range(NCH):
                P.op("vector", lambda e, hc=hc: e.tensor_tensor(
                    T[:, pre2_o + hc * 512: pre2_o + hc * 512 + 512],
                    ps[hc][:], T[:, yf_o + hc * 512: yf_o + hc * 512 + 512],
                    ALU.add))
            if DEBUG and s == 0:
                P.op("gpsimd", lambda e: e.dma_start(
                    out=dbg2[:, 1536:2048], in_=T[:, pre2_o:pre2_o + 512]),
                    dma=True)

            def wr2(oc, s=s):
                yo = lnscr2["y"]
                P.op("gpsimd", lambda e, oc=oc, s=s, yo=yo: e.dma_start(
                    out=t2own.rearrange("(c p) s -> p c s", p=128)[
                        :, oc:oc + 1, s * 512:(s + 1) * 512].squeeze(1),
                    in_=T[:, yo:yo + 512]), dma=True)
            ln2(pre2_o, wr2)
        al.pop()
        P.finalize(semh)
    return nc



def build_l1a():
    nc = bass.Bass("TRN2", target_bir_lowering=False)
    dram = lambda n, sh, dt=F32, kind="ExternalInput": nc.dram_tensor(
        n, sh, dt, kind=kind).ap()
    xg = dram("xg", [D, S])
    xo = dram("xo", [D, NSLOT * QT])
    wqT = dram("wqT", [D, D])
    wkT = dram("wkT", [D, D])
    wvT = dram("wvT", [D, D])
    woT = dram("woT", [D, D])
    bq = dram("bq", [D, 1])
    bk = dram("bk", [D, 1])
    bv = dram("bv", [D, 1])
    bo = dram("bo", [128, NCH])
    C = {k: dram("c_" + k, list(v_.shape)) for k, v_ in _host_consts(0).items()
         if k in ("maskstrip", "kvec1", "ident", "onesM", "onesD", "onesrow", "sel")}
    t3own = dram("t3own", [D, NSLOT * QT], kind="ExternalOutput")

    with ExitStack() as ctx:
        SB = lambda n, sh, dt=F32: ctx.enter_context(nc.sbuf_tensor(n, sh, dt))
        scr_t = SB("scr", [128, 45056])
        cst = {k: SB("s_" + k, list(v_.shape)) for k, v_ in _host_consts(0).items()
               if k in ("maskstrip", "kvec1", "ident", "onesM", "onesD", "onesrow", "sel")}
        bo_sb = SB("bo_sb", [128, NCH])
        ps = [ctx.enter_context(nc.psum_tensor(f"pb{i}", [128, 512], F32))
              for i in range(8)]
        semh = {s: ctx.enter_context(nc.semaphore(s)) for s in ALLSEMS}
        P = PB(nc)
        al = Alloc(scr_t, 45056)
        T = scr_t

        for k in C:
            P.op("sync", lambda e, k=k: e.dma_start(out=cst[k][:], in_=C[k]),
                 dma=True)
        P.op("sync", lambda e: e.dma_start(
            out=bo_sb[:], in_=bo), dma=True)

        acc_o = al.get(NCH * 2048)
        al.push()
        k_o = al.get(4096)
        q_o = al.get(2048)
        vtm_o = al.get(32 * 130)
        xt_o = al.get(4096)
        w3_o = al.get(3 * 1024)
        wo_o = al.get(1024)
        b3_o = al.get(4)
        vf_o = al.get(512)
        e1_o = al.get(8 * 512)
        pw_o = al.get(512)
        rd_o = al.get(2 * 512)
        ao_o = al.get(512)
        z_o = al.get(512)
        rz_o = al.get(512)
        bv_o = al.get(8 * 130)
        sfx_o = al.get(4 * 130)

        P.op("gpsimd", lambda e: e.memset(T[:, acc_o:acc_o + NCH * 2048], 0.0))
        P.op("gpsimd", lambda e: e.memset(
            T[:, vtm_o:vtm_o + 32 * 130].rearrange(
                "p (t c) -> p t c", c=130)[:, :, 64:65], 1.0))
        P.op("gpsimd", lambda e: e.memset(
            T[:, vtm_o:vtm_o + 32 * 130].rearrange(
                "p (t c) -> p t c", c=130)[:, :, 129:130], 1.0))

        def prod(wsrc, bsrc, col0, x_src, x_ncols, psum, dst_off, wofs, bslot):
            P.op("sync", lambda e: e.dma_start(
                out=sb_slab(T, wofs, 128), in_=fm_dram_slab(wsrc, col0, 128)),
                dma=True)
            P.op("sync", lambda e: e.dma_start(
                out=T[:, b3_o + bslot:b3_o + bslot + 1],
                in_=bsrc[col0:col0 + 128, 0:1]), dma=True)
            for tt in range(x_ncols // 512):
                P.op("sync", lambda e, tt=tt: e.dma_start(
                    out=sb_slab(T, xt_o, 512),
                    in_=fm_dram_slab(x_src, tt * 512, 512)), dma=True)
                for ch in range(NCH):
                    P.op("tensor", lambda e, ch=ch: e.matmul(
                        psum[:], T[:, wofs + ch * 128: wofs + ch * 128 + 128],
                        T[:, xt_o + ch * 512: xt_o + ch * 512 + 512],
                        start=(ch == 0), stop=(ch == NCH - 1)))
                if dst_off is not None:
                    P.op("vector", lambda e, tt=tt: e.tensor_scalar(
                        T[:, dst_off + tt * 512: dst_off + tt * 512 + 512],
                        psum[:], T[:, b3_o + bslot:b3_o + bslot + 1], None,
                        ALU.add))
                else:
                    P.op("vector", lambda e: e.tensor_scalar(
                        T[:, vf_o:vf_o + 512], psum[:],
                        T[:, b3_o + bslot:b3_o + bslot + 1], None, ALU.add))
                    for j in range(4):
                        P.op("tensor", lambda e, j=j: e.transpose(
                            ps[7][:, 0:128],
                            T[:, vf_o + j * 128: vf_o + j * 128 + 128],
                            cst["ident"][:]))
                        base = vtm_o + (tt * 4 + j) * 130
                        P.op("vector", lambda e, base=base: e.tensor_copy(
                            T[:, base:base + 64], ps[7][:, 0:64]))
                        P.op("vector", lambda e, base=base: e.tensor_copy(
                            T[:, base + 65:base + 129], ps[7][:, 64:128]))

        for p in range(NPAIR):
            prod(wqT, bq, 128 * p, xo, NSLOT * QT, ps[0], q_o, w3_o, 0)
            prod(wkT, bk, 128 * p, xg, S, ps[1], k_o, w3_o + 1024, 1)
            prod(wvT, bv, 128 * p, xg, S, ps[2], None, w3_o + 2048, 2)
            for blk in range(8):
                for t4 in range(4):
                    P.op("tensor", lambda e, blk=blk, t4=t4: e.matmul(
                        ps[6][0:1, 0:130], cst["onesD"][:, 0:1],
                        T[:, vtm_o + (blk * 4 + t4) * 130:
                          vtm_o + (blk * 4 + t4) * 130 + 130],
                        start=(t4 == 0), stop=(t4 == 3)))
                P.op("vector", lambda e, blk=blk: e.tensor_copy(
                    T[0:1, bv_o + blk * 130: bv_o + blk * 130 + 130],
                    ps[6][0:1, 0:130]))
            for s in range(NSLOT):
                for b in range(8):
                    sap = cst["sel"][0:1, s * 8 + b:s * 8 + b + 1]
                    if b == 0:
                        P.op("vector", lambda e, s=s, b=b, sap=sap: e.tensor_scalar(
                            T[0:1, sfx_o + s * 130: sfx_o + s * 130 + 130],
                            T[0:1, bv_o + b * 130: bv_o + b * 130 + 130],
                            sap, None, ALU.mult))
                    else:
                        P.op("vector", lambda e, s=s, b=b, sap=sap:
                             e.scalar_tensor_tensor(
                                 T[0:1, sfx_o + s * 130: sfx_o + s * 130 + 130],
                                 T[0:1, bv_o + b * 130: bv_o + b * 130 + 130],
                                 sap,
                                 T[0:1, sfx_o + s * 130: sfx_o + s * 130 + 130],
                                 ALU.mult, ALU.add))
            P.op("sync", lambda e, p=p: e.dma_start(
                out=T[:, wo_o:wo_o + 1024].rearrange("q (c n) -> q c n", n=128),
                in_=woT[128 * p:128 * p + 128, :].rearrange(
                    "q (c n) -> q c n", n=128)), dma=True)

            for s in range(NSLOT):
                nkt = NKT[s]
                for blk in range(nkt // 4):
                    for j4 in range(4):
                        i = blk * 4 + j4
                        for hd in range(2):
                            off = 64 * hd
                            P.op("tensor", lambda e, i=i, off=off, hd=hd, s=s:
                                 e.matmul(
                                     ps[hd][:],
                                     T[off:off + 64,
                                       k_o + 128 * i: k_o + 128 * i + 128],
                                     T[off:off + 64,
                                       q_o + s * 512: q_o + s * 512 + 512],
                                     start=True, stop=True,
                                     tile_position=(off, 0)))
                            kvidx = s * 32 + i
                            eoff = e1_o + (j4 * 2 + hd) * 512
                            P.op("scalar", lambda e, kvidx=kvidx, eoff=eoff,
                                 hd=hd: e.activation(
                                     T[:, eoff:eoff + 512], ps[hd][:], AF.Exp,
                                     bias=cst["kvec1"][:, kvidx:kvidx + 1],
                                     scale=0.125))
                            if i >= nkt - 8:
                                rel = i - (nkt - 8)
                                P.op("vector", lambda e, rel=rel, eoff=eoff:
                                     e.tensor_tensor(
                                         T[:, eoff:eoff + 512],
                                         T[:, eoff:eoff + 512],
                                         cst["maskstrip"][:, rel * QT:(rel + 1) * QT],
                                         ALU.mult))
                            P.op("tensor", lambda e, eoff=eoff, hd=hd, j4=j4:
                                 e.matmul(
                                     ps[2 + hd][:], cst["onesD"][:],
                                     T[:, eoff:eoff + 512],
                                     start=(j4 == 0), stop=(j4 == 3)))
                    for hd in range(2):
                        P.op("vector", lambda e, hd=hd: e.tensor_scalar(
                            T[:, rd_o + hd * 512: rd_o + hd * 512 + 512],
                            ps[2 + hd][:], 1.0e-30, None, ALU.max))
                        P.op("vector", lambda e, hd=hd: e.reciprocal(
                            T[:, rd_o + hd * 512: rd_o + hd * 512 + 512],
                            T[:, rd_o + hd * 512: rd_o + hd * 512 + 512]))
                        for j4 in range(4):
                            i = blk * 4 + j4
                            eoff = e1_o + (j4 * 2 + hd) * 512
                            P.op("vector", lambda e, eoff=eoff, hd=hd:
                                 e.tensor_tensor(
                                     T[:, pw_o:pw_o + 512], T[:, eoff:eoff + 512],
                                     T[:, rd_o + hd * 512: rd_o + hd * 512 + 512],
                                     ALU.mult))
                            P.op("scalar", lambda e, eoff=eoff: e.activation(
                                T[:, eoff:eoff + 512], T[:, pw_o:pw_o + 512],
                                AF.Exp))
                            vbase = vtm_o + i * 130 + 65 * hd
                            P.op("tensor", lambda e, vbase=vbase, eoff=eoff,
                                 hd=hd, i=i: e.matmul(
                                     ps[4 + hd][0:65, :],
                                     T[:, vbase:vbase + 65],
                                     T[:, eoff:eoff + 512],
                                     start=(i == 0), stop=False))
                for hd in range(2):
                    off = 64 * hd
                    P.op("tensor", lambda e, hd=hd, s=s: e.matmul(
                        ps[4 + hd][0:65, :],
                        T[0:1, sfx_o + s * 130 + 65 * hd:
                          sfx_o + s * 130 + 65 * hd + 65],
                        cst["onesrow"][0:1, :], start=False, stop=True))
                    P.op("vector", lambda e, hd=hd: e.tensor_copy(
                        T[0:1, z_o:z_o + 512], ps[4 + hd][64:65, :]))
                    P.op("tensor", lambda e: e.matmul(
                        ps[6][0:64, :], cst["onesrow"][0:1, 0:64],
                        T[0:1, z_o:z_o + 512], start=True, stop=True))
                    P.op("vector", lambda e: e.reciprocal(
                        T[0:64, rz_o:rz_o + 512], ps[6][0:64, :]))
                    P.op("vector", lambda e, hd=hd, off=off: e.tensor_tensor(
                        T[off:off + 64, ao_o:ao_o + 512], ps[4 + hd][0:64, :],
                        T[0:64, rz_o:rz_o + 512], ALU.mult))
                for oc in range(NCH):
                    P.op("tensor", lambda e, oc=oc: e.matmul(
                        ps[6][:], T[:, wo_o + oc * 128: wo_o + oc * 128 + 128],
                        T[:, ao_o:ao_o + 512], start=True, stop=True))
                    dst = acc_o + oc * 2048 + s * 512
                    P.op("vector", lambda e, dst=dst: e.tensor_tensor(
                        T[:, dst:dst + 512], T[:, dst:dst + 512], ps[6][:],
                        ALU.add))
        al.pop()

        al.push()
        pre_o = al.get(4096)
        xr_o = al.get(4096)
        lnm_o = al.get(512)
        lnt_o = al.get(512)
        lnr_o = al.get(512)
        lnq_o = al.get(512)
        lny_o = al.get(512)

        def ln_tile(pre_off, out_fn):
            for oc in range(NCH):
                src = T[:, pre_off + oc * 512: pre_off + oc * 512 + 512]
                P.op("tensor", lambda e, src=src, oc=oc: e.matmul(
                    ps[6][:], cst["onesM"][:], src,
                    start=(oc == 0), stop=(oc == NCH - 1)))
                P.op("scalar", lambda e, src=src: e.activation(
                    T[:, lnq_o:lnq_o + 512], src, AF.Square))
                P.op("tensor", lambda e, oc=oc: e.matmul(
                    ps[7][:], cst["onesM"][:], T[:, lnq_o:lnq_o + 512],
                    start=(oc == 0), stop=(oc == NCH - 1)))
            P.op("vector", lambda e: e.tensor_copy(T[:, lnm_o:lnm_o + 512], ps[6][:]))
            P.op("vector", lambda e: e.tensor_tensor(
                T[:, lnt_o:lnt_o + 512], ps[6][:], T[:, lnm_o:lnm_o + 512], ALU.mult))
            P.op("vector", lambda e: e.tensor_tensor(
                T[:, lnt_o:lnt_o + 512], ps[7][:], T[:, lnt_o:lnt_o + 512],
                ALU.subtract))
            P.op("vector", lambda e: e.tensor_scalar(
                T[:, lnt_o:lnt_o + 512], T[:, lnt_o:lnt_o + 512], 1.0e-5, None,
                ALU.add))
            P.op("scalar", lambda e: e.activation(
                T[:, lnr_o:lnr_o + 512], T[:, lnt_o:lnt_o + 512], AF.Sqrt))
            P.op("vector", lambda e: e.reciprocal(
                T[:, lnr_o:lnr_o + 512], T[:, lnr_o:lnr_o + 512]))
            for oc in range(NCH):
                src = T[:, pre_off + oc * 512: pre_off + oc * 512 + 512]
                P.op("vector", lambda e, src=src: e.tensor_tensor(
                    T[:, lnq_o:lnq_o + 512], src, T[:, lnm_o:lnm_o + 512],
                    ALU.subtract))
                P.op("vector", lambda e: e.tensor_tensor(
                    T[:, lny_o:lny_o + 512], T[:, lnq_o:lnq_o + 512],
                    T[:, lnr_o:lnr_o + 512], ALU.mult))
                out_fn(oc)

        for s in range(NSLOT):
            P.op("sync", lambda e, s=s: e.dma_start(
                out=sb_slab(T, xr_o, 512), in_=fm_dram_slab(xo, s * 512, 512)),
                dma=True)
            for oc in range(NCH):
                P.op("vector", lambda e, s=s, oc=oc: e.scalar_tensor_tensor(
                    T[:, pre_o + oc * 512: pre_o + oc * 512 + 512],
                    T[:, acc_o + oc * 2048 + s * 512:
                      acc_o + oc * 2048 + s * 512 + 512],
                    bo_sb[:, oc:oc + 1],
                    T[:, xr_o + oc * 512: xr_o + oc * 512 + 512],
                    ALU.add, ALU.add))

            def wr(oc, s=s):
                P.op("gpsimd", lambda e, oc=oc, s=s: e.dma_start(
                    out=t3own.rearrange("(c p) s -> p c s", p=128)[
                        :, oc:oc + 1, s * 512:(s + 1) * 512].squeeze(1),
                    in_=T[:, lny_o:lny_o + 512]), dma=True)
            ln_tile(pre_o, wr)
        al.pop()
        P.finalize(semh)
    return nc



def build_l1c():
    nc = bass.Bass("TRN2", target_bir_lowering=False)
    dram = lambda n, sh, dt=F32, kind="ExternalInput": nc.dram_tensor(
        n, sh, dt, kind=kind).ap()
    xo = dram("xo", [D, NSLOT * QT])
    xe = dram("xe", [D, 8])
    w1T = dram("w1T", [D, I1])
    w2T = dram("w2T", [I1, D])
    convw = dram("convw", [I1, 3])
    convb = dram("convb", [128, 32])
    C = {k: dram("c_" + k, list(v_.shape))
         for k, v_ in _host_consts(0).items() if k in ("ident", "onesM", "onesD")}
    out = dram("out", [D, NSLOT * QT], kind="ExternalOutput")
    w1q_hbm = dram("w1q_hbm", [D, I1], BF16, kind="Internal")
    w2q_hbm = dram("w2q_hbm", [I1, D], BF16, kind="Internal")

    with ExitStack() as ctx:
        SB = lambda n, sh, dt=F32: ctx.enter_context(nc.sbuf_tensor(n, sh, dt))
        T = SB("scr", [128, 36864])
        TB = SB("scrb", [128, 22656], BF16)
        cst = {k: SB("s_" + k, list(v_.shape)) for k, v_ in _host_consts(0).items()
               if k in ("ident", "onesM", "onesD")}
        identb = SB("identb", [128, 128], BF16)
        cw_sb = SB("cw_sb", [128, 32 * 3])
        cb_sb = SB("cb_sb", [128, 32])
        ps = [ctx.enter_context(nc.psum_tensor(f"pb{i}", [128, 512], F32))
              for i in range(8)]
        semh = {s: ctx.enter_context(nc.semaphore(s)) for s in ALLSEMS}
        P = PB(nc)
        al = Alloc(T, 36864)
        alb = Alloc(TB, 22656)

        for k in cst:
            P.op("sync", lambda e, k=k: e.dma_start(out=cst[k][:], in_=C[k]),
                 dma=True)
        P.op("vector", lambda e: e.tensor_copy(identb[:], cst["ident"][:]))
        P.op("sync", lambda e: e.dma_start(
            out=cw_sb[:].rearrange("p (c k) -> p c k", k=3),
            in_=convw.rearrange("(c p) k -> p c k", p=128)), dma=True)
        P.op("sync", lambda e: e.dma_start(
            out=cb_sb[:], in_=convb), dma=True)

        cs_o = al.get(8)
        c1_o = al.get(1)
        c2_o = al.get(1)
        al.push()
        wt_o = al.get(4096)
        wi_o = al.get(4096)
        alb.push()
        wtb_o = alb.get(4096)
        wi32 = ctx.enter_context(nc.sbuf_tensor("wi32", [128, 2048], I32))

        def ternary(wsrc, wdst, rows, cols, c_off):
            nrt = rows // 128
            P.op("gpsimd", lambda e: e.memset(T[:, cs_o:cs_o + 1], 0.0))
            for rt in range(nrt):
                P.op("sync", lambda e, rt=rt: e.dma_start(
                    out=T[:, wt_o:wt_o + cols],
                    in_=wsrc[rt * 128:(rt + 1) * 128, :]), dma=True)
                P.op("scalar", lambda e: e.activation(
                    T[:, wi_o:wi_o + cols], T[:, wt_o:wt_o + cols], AF.Abs))
                P.op("vector", lambda e: e.tensor_reduce(
                    T[:, cs_o + 1:cs_o + 2], T[:, wi_o:wi_o + cols], AX.X,
                    ALU.add))
                P.op("vector", lambda e: e.tensor_tensor(
                    T[:, cs_o:cs_o + 1], T[:, cs_o:cs_o + 1],
                    T[:, cs_o + 1:cs_o + 2], ALU.add))
            P.op("tensor", lambda e: e.matmul(
                ps[7][0:1, 0:1], T[:, cs_o:cs_o + 1], cst["onesD"][:, 0:1],
                start=True, stop=True))
            P.op("vector", lambda e: e.tensor_copy(T[0:1, cs_o + 2:cs_o + 3],
                                                   ps[7][0:1, 0:1]))
            P.op("tensor", lambda e: e.matmul(
                ps[7][0:128, 0:1], cst["onesD"][0:1, :],
                T[0:1, cs_o + 2:cs_o + 3], start=True, stop=True))
            P.op("vector", lambda e, rows=rows, cols=cols: e.tensor_scalar(
                T[:, c_off:c_off + 1], ps[7][0:128, 0:1],
                1.0 / float(rows * cols), 1.0e-5, ALU.mult, ALU.max))
            P.op("vector", lambda e: e.reciprocal(T[:, cs_o + 3:cs_o + 4],
                                                  T[:, c_off:c_off + 1]))
            for rt in range(nrt):
                P.op("sync", lambda e, rt=rt: e.dma_start(
                    out=T[:, wt_o:wt_o + cols],
                    in_=wsrc[rt * 128:(rt + 1) * 128, :]), dma=True)
                P.op("vector", lambda e: e.tensor_scalar(
                    T[:, wi_o:wi_o + cols], T[:, wt_o:wt_o + cols],
                    T[:, cs_o + 3:cs_o + 4], None, ALU.mult))
                for c0 in range(0, cols, 2048):
                    cn = min(2048, cols - c0)
                    P.op("vector", lambda e, c0=c0, cn=cn: e.tensor_copy(
                        wi32[:, 0:cn], T[:, wi_o + c0:wi_o + c0 + cn]))
                    P.op("vector", lambda e, c0=c0, cn=cn: e.tensor_copy(
                        T[:, wi_o + c0:wi_o + c0 + cn], wi32[:, 0:cn]))
                P.op("vector", lambda e: e.tensor_scalar(
                    T[:, wi_o:wi_o + cols], T[:, wi_o:wi_o + cols], 1.0, -1.0,
                    ALU.min, ALU.max))
                P.op("vector", lambda e: e.tensor_copy(
                    TB[:, wtb_o:wtb_o + cols], T[:, wi_o:wi_o + cols]))
                P.op("gpsimd", lambda e, rt=rt: e.dma_start(
                    out=wdst[rt * 128:(rt + 1) * 128, :],
                    in_=TB[:, wtb_o:wtb_o + cols]), dma=True)

        ternary(w1T, w1q_hbm, D, I1, c1_o)
        ternary(w2T, w2q_hbm, I1, D, c2_o)
        al.pop()
        alb.pop()

        te_o = al.get(64)
        etm_o = al.get(1024)
        erow_o = al.get(16)
        xqe_o = alb.get(64)
        qi_o = al.get(4096)
        febc_o = al.get(8)
        P.op("sync", lambda e: e.dma_start(
            out=sb_slab(T, te_o, 8), in_=fm_dram_slab(xe, 0, 8)), dma=True)
        for ch in range(NCH):
            P.op("tensor", lambda e, ch=ch: e.transpose(
                ps[7][0:8, 0:128], T[:, te_o + ch * 8: te_o + ch * 8 + 8],
                cst["ident"][:]))
            P.op("vector", lambda e, ch=ch: e.tensor_copy(
                T[0:8, etm_o + ch * 128: etm_o + ch * 128 + 128],
                ps[7][0:8, 0:128]))
        P.op("vector", lambda e: e.tensor_reduce(
            T[0:8, erow_o:erow_o + 1], T[0:8, etm_o:etm_o + 1024], AX.X,
            ALU.max, apply_absolute_value=True))
        P.op("vector", lambda e: e.tensor_scalar(
            T[0:8, erow_o:erow_o + 1], T[0:8, erow_o:erow_o + 1], 1.0e-5, None,
            ALU.max))
        P.op("vector", lambda e: e.reciprocal(
            T[0:8, erow_o + 1:erow_o + 2], T[0:8, erow_o:erow_o + 1]))
        P.op("vector", lambda e: e.tensor_scalar(
            T[0:8, etm_o:etm_o + 1024], T[0:8, etm_o:etm_o + 1024],
            T[0:8, erow_o + 1:erow_o + 2], 127.0, ALU.mult, ALU.mult))
        ei32 = wi32
        P.op("vector", lambda e: e.tensor_copy(ei32[0:8, 0:1024],
                                               T[0:8, etm_o:etm_o + 1024]))
        P.op("vector", lambda e: e.tensor_copy(
            T[0:8, etm_o:etm_o + 1024], ei32[0:8, 0:1024]))
        for ch in range(NCH):
            P.op("tensor", lambda e, ch=ch: e.transpose(
                ps[7][0:128, 0:8],
                T[0:8, etm_o + ch * 128: etm_o + ch * 128 + 128],
                cst["ident"][0:8, 0:8]))
            P.op("vector", lambda e, ch=ch: e.tensor_copy(
                TB[:, xqe_o + ch * 8: xqe_o + ch * 8 + 8], ps[7][0:128, 0:8]))
        P.op("vector", lambda e: e.tensor_scalar(
            T[0:8, erow_o + 2:erow_o + 3], T[0:8, erow_o:erow_o + 1],
            T[0:8, c1_o:c1_o + 1], 1.0 / 127.0, ALU.mult, ALU.mult))
        P.op("tensor", lambda e: e.transpose(
            ps[7][0:1, 0:8],
            T[0:8, erow_o + 2:erow_o + 3], cst["ident"][0:8, 0:8]))
        P.op("vector", lambda e: e.tensor_copy(T[0:1, erow_o + 4:erow_o + 12],
                                               ps[7][0:1, 0:8]))
        P.op("tensor", lambda e: e.matmul(
            ps[7][0:128, 0:8], cst["onesD"][0:1, :],
            T[0:1, erow_o + 4:erow_o + 12], start=True, stop=True))
        P.op("vector", lambda e: e.tensor_copy(T[:, febc_o:febc_o + 8],
                                               ps[7][0:128, 0:8]))

        xt_o = al.get(4096)
        xtm_o = al.get(1024)
        ctok_o = al.get(16)
        f_o = al.get(4)
        frow_o = al.get(512)
        fbc_o = al.get(512)
        fbc2_o = al.get(512)
        stm_o = al.get(4 * 4096)
        hx_o = al.get(514)
        pre_o = al.get(4096)
        tmp_o = al.get(512)
        lnm_o = al.get(512)
        lnt_o = al.get(512)
        lnr_o = al.get(512)
        lnq_o = al.get(512)
        lny_o = al.get(512)
        xqs_o = alb.get(4096)
        w1t_o = alb.get(1024)
        w2t_o = alb.get(1024)
        xq2_o = alb.get(32 * 512)
        assert al.off <= 36864, al.off
        assert alb.off <= 22656, alb.off

        def int8_quant_tm(src_off, qi_off, n, sub, rbuf_off, c_scale_off):
            P.op("vector", lambda e: e.tensor_reduce(
                T[:, ctok_o:ctok_o + 1], T[:, src_off:src_off + n], AX.X,
                ALU.max, apply_absolute_value=True))
            P.op("vector", lambda e: e.tensor_scalar(
                T[:, ctok_o:ctok_o + 1], T[:, ctok_o:ctok_o + 1], 1.0e-5, None,
                ALU.max))
            P.op("vector", lambda e: e.reciprocal(
                T[:, ctok_o + 1:ctok_o + 2], T[:, ctok_o:ctok_o + 1]))
            P.op("vector", lambda e, n=n: e.tensor_scalar(
                T[:, rbuf_off:rbuf_off + n], T[:, src_off:src_off + n],
                T[:, ctok_o + 1:ctok_o + 2], 127.0, ALU.mult, ALU.mult))
            for c0 in range(0, n, 2048):
                cn = min(2048, n - c0)
                P.op("vector", lambda e, c0=c0, cn=cn: e.tensor_copy(
                    wi32[:, 0:cn], T[:, rbuf_off + c0:rbuf_off + c0 + cn]))
                P.op("vector", lambda e, c0=c0, cn=cn: e.tensor_copy(
                    T[:, qi_off + c0:qi_off + c0 + cn], wi32[:, 0:cn]))
            P.op("vector", lambda e: e.tensor_scalar(
                T[:, f_o + sub:f_o + sub + 1], T[:, ctok_o:ctok_o + 1],
                T[:, c_scale_off:c_scale_off + 1], 1.0 / 127.0,
                ALU.mult, ALU.mult))

        def factor_bcast(dst_off):
            for sub in range(4):
                P.op("tensor", lambda e, sub=sub: e.transpose(
                    ps[7][0:1, 0:128],
                    T[:, f_o + sub:f_o + sub + 1], cst["ident"][:]))
                P.op("vector", lambda e, sub=sub: e.tensor_copy(
                    T[0:1, frow_o + sub * 128: frow_o + sub * 128 + 128],
                    ps[7][0:1, 0:128]))
            P.op("tensor", lambda e: e.matmul(
                ps[7][0:128, 0:512], cst["onesD"][0:1, :],
                T[0:1, frow_o:frow_o + 512], start=True, stop=True))
            P.op("vector", lambda e, dst_off=dst_off: e.tensor_copy(
                T[:, dst_off:dst_off + 512], ps[7][0:128, 0:512]))

        for s in range(NSLOT):
            P.op("sync", lambda e, s=s: e.dma_start(
                out=sb_slab(T, xt_o, 512), in_=fm_dram_slab(xo, s * 512, 512)),
                dma=True)
            for sub in range(4):
                for ch in range(NCH):
                    P.op("tensor", lambda e, ch=ch, sub=sub: e.transpose(
                        ps[7][:, 0:128],
                        T[:, xt_o + ch * 512 + sub * 128:
                          xt_o + ch * 512 + sub * 128 + 128],
                        cst["ident"][:]))
                    P.op("vector", lambda e, ch=ch: e.tensor_copy(
                        T[:, xtm_o + ch * 128: xtm_o + ch * 128 + 128],
                        ps[7][:, 0:128]))
                int8_quant_tm(xtm_o, qi_o, 1024, sub, pre_o, c1_o)
                for ch in range(NCH):
                    P.op("tensor", lambda e, ch=ch: e.transpose(
                        ps[7][:, 0:128],
                        T[:, qi_o + ch * 128: qi_o + ch * 128 + 128],
                        cst["ident"][:]))
                    P.op("vector", lambda e, ch=ch, sub=sub: e.tensor_copy(
                        TB[:, xqs_o + ch * 512 + sub * 128:
                           xqs_o + ch * 512 + sub * 128 + 128],
                        ps[7][:, 0:128]))
            factor_bcast(fbc_o)
            for ich in range(32):
                P.op("sync", lambda e, ich=ich: e.dma_start(
                    out=sb_slab(TB, w1t_o, 128),
                    in_=fm_dram_slab(w1q_hbm, ich * 128, 128)), dma=True)
                for ch in range(NCH):
                    P.op("tensor", lambda e, ch=ch: e.matmul(
                        ps[0][:], TB[:, w1t_o + ch * 128: w1t_o + ch * 128 + 128],
                        TB[:, xqs_o + ch * 512: xqs_o + ch * 512 + 512],
                        start=(ch == 0), stop=(ch == NCH - 1)))
                for ch in range(NCH):
                    P.op("tensor", lambda e, ch=ch, s=s: e.matmul(
                        ps[1][:, 0:2],
                        TB[:, w1t_o + ch * 128: w1t_o + ch * 128 + 128],
                        TB[:, xqe_o + ch * 8 + 2 * s: xqe_o + ch * 8 + 2 * s + 2],
                        start=(ch == 0), stop=(ch == NCH - 1)))
                P.op("vector", lambda e: e.tensor_tensor(
                    T[:, hx_o + 1:hx_o + 513], ps[0][:], T[:, fbc_o:fbc_o + 512],
                    ALU.mult))
                P.op("vector", lambda e, s=s: e.tensor_scalar(
                    T[:, hx_o:hx_o + 1], ps[1][:, 0:1],
                    T[:, febc_o + 2 * s:febc_o + 2 * s + 1], None, ALU.mult))
                P.op("vector", lambda e, s=s: e.tensor_scalar(
                    T[:, hx_o + 513:hx_o + 514], ps[1][:, 1:2],
                    T[:, febc_o + 2 * s + 1:febc_o + 2 * s + 2], None, ALU.mult))
                P.op("vector", lambda e, ich=ich: e.tensor_scalar(
                    T[:, tmp_o:tmp_o + 512], T[:, hx_o:hx_o + 512],
                    cw_sb[:, ich * 3:ich * 3 + 1], cb_sb[:, ich:ich + 1],
                    ALU.mult, ALU.add))
                P.op("vector", lambda e, ich=ich: e.scalar_tensor_tensor(
                    T[:, tmp_o:tmp_o + 512], T[:, hx_o + 1:hx_o + 513],
                    cw_sb[:, ich * 3 + 1:ich * 3 + 2], T[:, tmp_o:tmp_o + 512],
                    ALU.mult, ALU.add))
                P.op("vector", lambda e, ich=ich: e.scalar_tensor_tensor(
                    T[:, tmp_o:tmp_o + 512], T[:, hx_o + 2:hx_o + 514],
                    cw_sb[:, ich * 3 + 2:ich * 3 + 3], T[:, tmp_o:tmp_o + 512],
                    ALU.mult, ALU.add))
                P.op("scalar", lambda e: e.activation(
                    T[:, tmp_o:tmp_o + 512], T[:, tmp_o:tmp_o + 512], AF.Silu))
                for sub in range(4):
                    P.op("tensor", lambda e, sub=sub: e.transpose(
                        ps[7][:, 0:128],
                        T[:, tmp_o + sub * 128: tmp_o + sub * 128 + 128],
                        cst["ident"][:]))
                    P.op("vector", lambda e, sub=sub, ich=ich: e.tensor_copy(
                        T[:, stm_o + sub * 4096 + ich * 128:
                          stm_o + sub * 4096 + ich * 128 + 128],
                        ps[7][:, 0:128]))
            for sub in range(4):
                int8_quant_tm(stm_o + sub * 4096, qi_o, 4096, sub, pre_o, c2_o)
                for ich in range(32):
                    P.op("tensor", lambda e, ich=ich: e.transpose(
                        ps[7][:, 0:128],
                        T[:, qi_o + ich * 128: qi_o + ich * 128 + 128],
                        cst["ident"][:]))
                    P.op("vector", lambda e, ich=ich, sub=sub: e.tensor_copy(
                        TB[:, xq2_o + ich * 512 + sub * 128:
                           xq2_o + ich * 512 + sub * 128 + 128],
                        ps[7][:, 0:128]))
            factor_bcast(fbc2_o)
            for ich in range(32):
                P.op("sync", lambda e, ich=ich: e.dma_start(
                    out=TB[:, w2t_o:w2t_o + 1024],
                    in_=w2q_hbm[ich * 128:(ich + 1) * 128, :]), dma=True)
                for hc in range(NCH):
                    P.op("tensor", lambda e, ich=ich, hc=hc: e.matmul(
                        ps[hc][:],
                        TB[:, w2t_o + hc * 128: w2t_o + hc * 128 + 128],
                        TB[:, xq2_o + ich * 512: xq2_o + ich * 512 + 512],
                        start=(ich == 0), stop=(ich == 31)))
            for hc in range(NCH):
                P.op("vector", lambda e, hc=hc: e.tensor_tensor(
                    T[:, tmp_o:tmp_o + 512], ps[hc][:], T[:, fbc2_o:fbc2_o + 512],
                    ALU.mult))
                P.op("vector", lambda e, hc=hc: e.tensor_tensor(
                    T[:, pre_o + hc * 512: pre_o + hc * 512 + 512],
                    T[:, tmp_o:tmp_o + 512],
                    T[:, xt_o + hc * 512: xt_o + hc * 512 + 512], ALU.add))
            for oc in range(NCH):
                src = T[:, pre_o + oc * 512: pre_o + oc * 512 + 512]
                P.op("tensor", lambda e, src=src, oc=oc: e.matmul(
                    ps[6][:], cst["onesM"][:], src,
                    start=(oc == 0), stop=(oc == NCH - 1)))
                P.op("scalar", lambda e, src=src: e.activation(
                    T[:, lnq_o:lnq_o + 512], src, AF.Square))
                P.op("tensor", lambda e, oc=oc: e.matmul(
                    ps[7][:], cst["onesM"][:], T[:, lnq_o:lnq_o + 512],
                    start=(oc == 0), stop=(oc == NCH - 1)))
            P.op("vector", lambda e: e.tensor_copy(T[:, lnm_o:lnm_o + 512], ps[6][:]))
            P.op("vector", lambda e: e.tensor_tensor(
                T[:, lnt_o:lnt_o + 512], ps[6][:], T[:, lnm_o:lnm_o + 512],
                ALU.mult))
            P.op("vector", lambda e: e.tensor_tensor(
                T[:, lnt_o:lnt_o + 512], ps[7][:], T[:, lnt_o:lnt_o + 512],
                ALU.subtract))
            P.op("vector", lambda e: e.tensor_scalar(
                T[:, lnt_o:lnt_o + 512], T[:, lnt_o:lnt_o + 512], 1.0e-5, None,
                ALU.add))
            P.op("scalar", lambda e: e.activation(
                T[:, lnr_o:lnr_o + 512], T[:, lnt_o:lnt_o + 512], AF.Sqrt))
            P.op("vector", lambda e: e.reciprocal(
                T[:, lnr_o:lnr_o + 512], T[:, lnr_o:lnr_o + 512]))
            for oc in range(NCH):
                src = T[:, pre_o + oc * 512: pre_o + oc * 512 + 512]
                P.op("vector", lambda e, src=src: e.tensor_tensor(
                    T[:, lnq_o:lnq_o + 512], src, T[:, lnm_o:lnm_o + 512],
                    ALU.subtract))
                P.op("vector", lambda e: e.tensor_tensor(
                    T[:, lny_o:lny_o + 512], T[:, lnq_o:lnq_o + 512],
                    T[:, lnr_o:lnr_o + 512], ALU.mult))
                P.op("gpsimd", lambda e, oc=oc, s=s: e.dma_start(
                    out=out.rearrange("(c p) s -> p c s", p=128)[
                        :, oc:oc + 1, s * 512:(s + 1) * 512].squeeze(1),
                    in_=T[:, lny_o:lny_o + 512]), dma=True)
        P.finalize(semh)
    return nc


_PROGS = {}


def _get_prog(name, builder):
    if name not in _PROGS:
        _PROGS[name] = builder()
    return _PROGS[name]


def kernel(**inp):
    f32 = lambda a: np.ascontiguousarray(a, dtype=np.float32)
    cols = {h: _own_cols(h) for h in (0, 1)}
    consts = {h: _host_consts(h) for h in (0, 1)}
    xs = [f32(inp["L0_attn_x"][b].T) for b in range(B)]

    maps = []
    for c in range(8):
        b, h = c % 4, c // 4
        m = {("c_" + k): v for k, v in consts[h].items()}
        m["c_selE"] = _host_selE(h)
        m.update(dict(
            xg=xs[b], xo=np.ascontiguousarray(xs[b][:, cols[h]]),
            w_in=f32(inp["L0_attn_w_in"]),
            b_in=f32(inp["L0_attn_b_in"]).reshape(3 * D, 1),
            w_out=f32(inp["L0_attn_w_out"]),
            b_out=f32(inp["L0_attn_b_out"]).reshape(NCH, 128).T.copy(),
            w_gate=f32(inp["L0_ffn_w_gate"]), w_up=f32(inp["L0_ffn_w_up"]),
            w_down=f32(inp["L0_ffn_w_down"]),
            wqT=f32(inp["L1_attn_Wq_w"].T), wkT=f32(inp["L1_attn_Wk_w"].T),
            wvT=f32(inp["L1_attn_Wv_w"].T), woT=f32(inp["L1_attn_Wo_w"].T),
            bq=f32(inp["L1_attn_Wq_b"]).reshape(D, 1),
            bk=f32(inp["L1_attn_Wk_b"]).reshape(D, 1),
            bv=f32(inp["L1_attn_Wv_b"]).reshape(D, 1),
            bo=f32(inp["L1_attn_Wo_b"]).reshape(NCH, 128).T.copy(),
            w1T=f32(inp["L1_ffn_w1"].T), w2T=f32(inp["L1_ffn_w2"].T),
            convw=f32(inp["L1_ffn_conv_w"]).reshape(I1, 3),
            convb=f32(inp["L1_ffn_conv_b"]).reshape(32, 128).T.copy()))
        maps.append(m)

    nc = _get_prog("fused", build_fused)
    r = run_bass_kernel_spmd(nc, maps, core_ids=list(range(8)))

    dt0 = np.asarray(r.results[0]["out"]).dtype
    outp = np.zeros((B, S, D), dtype=dt0)
    for c in range(8):
        b, h = c % 4, c // 4
        outp[b][cols[h], :] = np.asarray(r.results[c]["out"]).T
    return outp.astype(np.float32) if dt0 == np.float32 else outp


CC_GROUPS = [[0, 4], [1, 5], [2, 6], [3, 7]]


def _host_selE(half):
    selE = np.zeros((1, 8 * 16), dtype=np.float32)
    for s in range(NSLOT):
        g = _own_qtiles(half)[s]
        if g - 1 >= 0:
            blk = 0 if ((g - 1) % 2 == 1) else 1
            s2 = (g - 1) // 2
            selE[0, (2 * s) * 16 + blk * 8 + 2 * s2 + 1] = 1.0
        if g + 1 < S // QT:
            blk = 0 if ((g + 1) % 2 == 1) else 1
            s2 = (g + 1) // 2
            selE[0, (2 * s + 1) * 16 + blk * 8 + 2 * s2] = 1.0
    return np.tile(selE, (128, 1))


def build_fused():
    global _DEP
    _PBMODE[0] = "record"
    _REC.clear()
    _DEP = {}
    _build_fused_once()
    _DEP = _compute_deps(_REC)
    _PBMODE[0] = "dep"
    try:
        nc = _build_fused_once()
    finally:
        _PBMODE[0] = "strict"
    return nc


def _build_fused_once():
    nc = bass.Bass("TRN2", target_bir_lowering=False)
    dram = lambda n, sh, dt=F32, kind="ExternalInput": nc.dram_tensor(
        n, sh, dt, kind=kind).ap()
    xg = dram("xg", [D, S])
    xo = dram("xo", [D, NSLOT * QT])
    w_in = dram("w_in", [D, 3 * D])
    b_in = dram("b_in", [3 * D, 1])
    w_out = dram("w_out", [D, D])
    b_out = dram("b_out", [128, NCH])
    w_gate = dram("w_gate", [I0, D])
    w_up = dram("w_up", [I0, D])
    w_down = dram("w_down", [D, I0])
    wqT = dram("wqT", [D, D])
    wkT = dram("wkT", [D, D])
    wvT = dram("wvT", [D, D])
    woT = dram("woT", [D, D])
    bq = dram("bq", [D, 1])
    bk = dram("bk", [D, 1])
    bv = dram("bv", [D, 1])
    bo = dram("bo", [128, NCH])
    w1T = dram("w1T", [D, I1])
    w2T = dram("w2T", [I1, D])
    convw = dram("convw", [I1, 3])
    convb = dram("convb", [128, 32])
    CC0 = _host_consts(0)
    RES_CONSTS = ("maskstrip", "kvec1", "ident", "onesM", "onesD", "onesrow",
                  "sel")
    C = {k: dram("c_" + k, list(CC0[k].shape)) for k in RES_CONSTS}
    C["kvec0"] = dram("c_kvec0", list(CC0["kvec0"].shape))
    C["qbias"] = dram("c_qbias", list(CC0["qbias"].shape))
    C["selE"] = dram("c_selE", [128, 128])
    out = dram("out", [D, NSLOT * QT], kind="ExternalOutput")
    IK = "ExternalOutput" if DEBUG else "Internal"
    yf_hbm = dram("yf_hbm", [D, NSLOT * QT], kind=IK)
    ao_hbm = dram("ao_hbm", [D, NSLOT * QT], kind=IK)
    t2o_hbm = dram("t2o_hbm", [D, NSLOT * QT], kind=IK)
    t3o_hbm = dram("t3o_hbm", [D, NSLOT * QT], kind=IK)
    cc_in = dram("cc_in", [D, NSLOT * QT], kind="Internal")
    cc_out = dram("cc_out", [2 * D, NSLOT * QT], kind="Internal")
    ccE_in = dram("ccE_in", [D, 8], kind="Internal")
    ccE_out = dram("ccE_out", [2 * D, 8], kind="Internal")
    wgqT = dram("wgqT", [D, I0], kind="Internal")
    wuqT = dram("wuqT", [D, I0], kind="Internal")
    wdqT = dram("wdqT", [I0, D], kind="Internal")
    w1q_hbm = dram("w1q_hbm", [D, I1], BF16, kind="Internal")
    w2q_hbm = dram("w2q_hbm", [I1, D], BF16, kind="Internal")

    with ExitStack() as ctx:
        SB = lambda n, sh, dt=F32: ctx.enter_context(nc.sbuf_tensor(n, sh, dt))
        T = SB("scr", [128, 35072])
        TB = SB("scrb", [128, 22656], BF16)
        wi32 = ctx.enter_context(nc.sbuf_tensor("wi32", [128, 1024], I32))
        cst = {k: SB("s_" + k, list(CC0[k].shape)) for k in RES_CONSTS}
        cst["selE"] = SB("s_selE", [128, 128])
        bout_sb = SB("bout", [128, NCH])
        bo_sb = SB("bo_sb", [128, NCH])
        cw_sb = SB("cw_sb", [128, 32 * 3])
        cb_sb = SB("cb_sb", [128, 32])
        ps = [ctx.enter_context(nc.psum_tensor(f"pb{i}", [128, 512], F32))
              for i in range(8)]
        semh = {s: ctx.enter_context(nc.semaphore(s)) for s in ALLSEMS}
        P = PB(nc)
        al = Alloc(T, 35072)
        alb = Alloc(TB, 22656)

        for k in RES_CONSTS:
            P.op("sync", lambda e, k=k: e.dma_start(out=cst[k][:], in_=C[k]),
                 dma=True)
        P.op("sync", lambda e: e.dma_start(out=cst["selE"][:], in_=C["selE"]),
             dma=True)
        P.op("sync", lambda e: e.dma_start(out=bout_sb[:], in_=b_out), dma=True)
        P.op("sync", lambda e: e.dma_start(out=bo_sb[:], in_=bo), dma=True)
        P.op("sync", lambda e: e.dma_start(
            out=cw_sb[:].rearrange("p (c k) -> p c k", k=3),
            in_=convw.rearrange("(c p) k -> p c k", p=128)), dma=True)
        P.op("sync", lambda e: e.dma_start(out=cb_sb[:], in_=convb), dma=True)

        att = {}

        def alloc_attn(l0=True):
            att["k"] = al.get(4096)
            att["q"] = al.get(2048)
            att["vtm"] = al.get(32 * 130)
            att["xt"] = al.get(8192)
            att["w3"] = al.get(2048)
            att["b3"] = al.get(8)
            att["pc"] = 0
            att["vf"] = al.get(1024)
            att["ao"] = al.get(512)
            att["z"] = al.get(512)
            att["rz"] = al.get(512)
            if l0:
                att["et"] = al.get(4 * 512)
                att["etmp"] = al.get(1024)
                att["kv0"] = al.get(256)
                att["qb"] = al.get(1024)

        def memset_vtm():
            P.op("gpsimd", lambda e: e.memset(
                T[:, att["vtm"]:att["vtm"] + 32 * 130].rearrange(
                    "p (t c) -> p t c", c=130)[:, :, 64:65], 1.0))
            P.op("gpsimd", lambda e: e.memset(
                T[:, att["vtm"]:att["vtm"] + 32 * 130].rearrange(
                    "p (t c) -> p t c", c=130)[:, :, 129:130], 1.0))

        def production(wsrc, col0, bsrc, brow0, bslot, src_fn, ntt, dst, kind):
            par = att["pc"] % 2
            att["pc"] += 1
            w3 = att["w3"] + 1024 * par
            b3 = att["b3"] + 4 * par
            vf, vtm = att["vf"], att["vtm"]
            P.op("sync", lambda e: e.dma_start(
                out=sb_slab(T, w3, 128), in_=fm_dram_slab(wsrc, col0, 128)),
                dma=True)
            P.op("sync", lambda e: e.dma_start(
                out=T[:, b3 + bslot:b3 + bslot + 1],
                in_=bsrc[brow0:brow0 + 128, 0:1]), dma=True)
            for tt in range(ntt):
                xt = att["xt"] + 4096 * (tt % 2)
                pp = ps[0] if tt % 2 == 0 else ps[5]
                P.op("sync", lambda e, tt=tt, xt=xt: e.dma_start(
                    out=sb_slab(T, xt, 512), in_=src_fn(tt)), dma=True)
                for ch in range(NCH):
                    P.op("tensor", lambda e, ch=ch, xt=xt, pp=pp: e.matmul(
                        pp[:], T[:, w3 + ch * 128: w3 + ch * 128 + 128],
                        T[:, xt + ch * 512: xt + ch * 512 + 512],
                        start=(ch == 0), stop=(ch == NCH - 1)))
                if kind == "fm":
                    P.op("vector", lambda e, tt=tt, pp=pp: e.tensor_scalar(
                        T[:, dst + tt * 512: dst + tt * 512 + 512], pp[:],
                        T[:, b3 + bslot:b3 + bslot + 1], None, ALU.add))
                else:
                    vfo = vf + 512 * (tt % 2)
                    P.op("vector", lambda e, pp=pp, vfo=vfo: e.tensor_scalar(
                        T[:, vfo:vfo + 512], pp[:],
                        T[:, b3 + bslot:b3 + bslot + 1], None, ALU.add))
                    for j in range(4):
                        tp = ps[6] if j % 2 == 0 else ps[7]
                        P.op("tensor", lambda e, j=j, vfo=vfo, tp=tp: e.transpose(
                            tp[:, 0:128],
                            T[:, vfo + j * 128: vfo + j * 128 + 128],
                            cst["ident"][:]))
                        base = vtm + (tt * 4 + j) * 130
                        P.op("vector", lambda e, base=base, tp=tp: e.tensor_copy(
                            T[:, base:base + 64], tp[:, 0:64]))
                        P.op("vector", lambda e, base=base, tp=tp: e.tensor_copy(
                            T[:, base + 65:base + 129], tp[:, 64:128]))

        def attn_l0(p, filler=None):
            k_o, q_o, vtm_o = att["k"], att["q"], att["vtm"]
            et_o, etmp_o, ao_o, z_o, rz_o = (att["et"], att["etmp"], att["ao"],
                                             att["z"], att["rz"])
            kv0, qb = att["kv0"], att["qb"]
            SBANK = {(0, 0): ps[0], (0, 1): ps[6], (1, 0): ps[1], (1, 1): ps[7]}
            P.op("sync", lambda e, p=p: e.dma_start(
                out=T[:, kv0:kv0 + 256],
                in_=C["kvec0"][:, 2 * p * NSLOT * 32:(2 * p + 2) * NSLOT * 32]),
                dma=True)
            if 2 * p < ASPL:
                n_qb = min(ASPL - 2 * p, 2)
                P.op("sync", lambda e, p=p, n_qb=n_qb: e.dma_start(
                    out=T[:, qb:qb + n_qb * 512],
                    in_=C["qbias"][:, 2 * p * QT:(2 * p + n_qb) * QT]), dma=True)
            for s in range(NSLOT):
                nkt = NKT[s]

                def emit_av(i):
                    for hd in range(2):
                        pav = ps[2 + hd]
                        eo = et_o + (hd * 2 + i % 2) * 512
                        vbase = vtm_o + i * 130 + 65 * hd
                        P.op("tensor", lambda e, vbase=vbase, eo=eo, pav=pav,
                             i=i, nkt=nkt: e.matmul(
                                 pav[0:65, :], T[:, vbase:vbase + 65],
                                 T[:, eo:eo + 512],
                                 start=(i == 0), stop=(i == nkt - 1)))

                for i in range(nkt):
                    for hd in range(2):
                        off = 64 * hd
                        pss = SBANK[(hd, i % 2)]
                        P.op("tensor", lambda e, i=i, off=off, pss=pss, s=s:
                             e.matmul(
                                 pss[:],
                                 T[off:off + 64, k_o + 128 * i: k_o + 128 * i + 128],
                                 T[off:off + 64, q_o + s * 512: q_o + s * 512 + 512],
                                 start=True, stop=True, tile_position=(off, 0)))
                    if i > 0:
                        emit_av(i - 1)
                    for hd in range(2):
                        gh = 2 * p + hd
                        pss = SBANK[(hd, i % 2)]
                        eo = et_o + (hd * 2 + i % 2) * 512
                        em = etmp_o + hd * 512
                        kvidx = kv0 + (hd * NSLOT + s) * 32 + i
                        if gh < ASPL:
                            P.op("vector", lambda e, pss=pss, hd=hd, em=em:
                                 e.scalar_tensor_tensor(
                                     T[:, em:em + 512], pss[:], 0.125,
                                     T[:, qb + hd * QT: qb + (hd + 1) * QT],
                                     ALU.mult, ALU.add))
                            P.op("vector", lambda e, kvidx=kvidx, em=em:
                                 e.tensor_scalar(
                                     T[:, em:em + 512], T[:, em:em + 512],
                                     T[:, kvidx:kvidx + 1], 30.0,
                                     ALU.add, ALU.min))
                            P.op("scalar", lambda e, eo=eo, em=em: e.activation(
                                T[:, eo:eo + 512], T[:, em:em + 512], AF.Exp))
                        else:
                            P.op("scalar", lambda e, kvidx=kvidx, eo=eo, pss=pss:
                                 e.activation(
                                     T[:, eo:eo + 512], pss[:], AF.Exp,
                                     bias=T[:, kvidx:kvidx + 1], scale=0.125))
                        if i >= nkt - 8:
                            rel = i - (nkt - 8)
                            P.op("vector", lambda e, rel=rel, eo=eo:
                                 e.tensor_tensor(
                                     T[:, eo:eo + 512], T[:, eo:eo + 512],
                                     cst["maskstrip"][:, rel * QT:(rel + 1) * QT],
                                     ALU.mult))
                emit_av(nkt - 1)
                for hd in range(2):
                    off = 64 * hd
                    pav = ps[2 + hd]
                    P.op("vector", lambda e, pav=pav: e.tensor_copy(
                        T[0:1, z_o:z_o + 512], pav[64:65, :]))
                    P.op("tensor", lambda e: e.matmul(
                        ps[6][0:64, :], cst["onesrow"][0:1, 0:64],
                        T[0:1, z_o:z_o + 512], start=True, stop=True))
                    P.op("vector", lambda e: e.reciprocal(
                        T[0:64, rz_o:rz_o + 512], ps[6][0:64, :]))
                    P.op("vector", lambda e, pav=pav, off=off: e.tensor_tensor(
                        T[off:off + 64, ao_o:ao_o + 512], pav[0:64, :],
                        T[0:64, rz_o:rz_o + 512], ALU.mult))
                P.op("gpsimd", lambda e, p=p, s=s: e.dma_start(
                    out=ao_hbm[128 * p:128 * p + 128, s * 512:(s + 1) * 512],
                    in_=T[:, att["ao"]:att["ao"] + 512]), dma=True)
                if filler is not None:
                    filler()

        def attn_l1(p):
            k_o, q_o, vtm_o = att["k"], att["q"], att["vtm"]
            et_o = att["e1"]
            pw_o, rd_o, ao_o, z_o, rz_o = (att["pw"], att["rd"], att["ao"],
                                           att["z"], att["rz"])
            bv_o, sfx_o = att["bv"], att["sfx"]
            SBANK = {(0, 0): ps[0], (0, 1): ps[6], (1, 0): ps[1], (1, 1): ps[7]}
            for blk in range(8):
                for t4 in range(4):
                    P.op("tensor", lambda e, blk=blk, t4=t4: e.matmul(
                        ps[6][0:1, 0:130], cst["onesD"][:, 0:1],
                        T[:, vtm_o + (blk * 4 + t4) * 130:
                          vtm_o + (blk * 4 + t4) * 130 + 130],
                        start=(t4 == 0), stop=(t4 == 3)))
                P.op("vector", lambda e, blk=blk: e.tensor_copy(
                    T[0:1, bv_o + blk * 130: bv_o + blk * 130 + 130],
                    ps[6][0:1, 0:130]))
            for s in range(NSLOT):
                for b in range(8):
                    sap = cst["sel"][0:1, s * 8 + b:s * 8 + b + 1]
                    if b == 0:
                        P.op("vector", lambda e, s=s, sap=sap, b=b:
                             e.tensor_scalar(
                                 T[0:1, sfx_o + s * 130: sfx_o + s * 130 + 130],
                                 T[0:1, bv_o + b * 130: bv_o + b * 130 + 130],
                                 sap, None, ALU.mult))
                    else:
                        P.op("vector", lambda e, s=s, sap=sap, b=b:
                             e.scalar_tensor_tensor(
                                 T[0:1, sfx_o + s * 130: sfx_o + s * 130 + 130],
                                 T[0:1, bv_o + b * 130: bv_o + b * 130 + 130],
                                 sap,
                                 T[0:1, sfx_o + s * 130: sfx_o + s * 130 + 130],
                                 ALU.mult, ALU.add))
            for s in range(NSLOT):
                nkt = NKT[s]
                def emit_d(j4):
                    for hd in range(2):
                        eoff = et_o + (j4 * 2 + hd) * 512
                        P.op("tensor", lambda e, eoff=eoff, hd=hd, j4=j4:
                             e.matmul(
                                 ps[2 + hd][:], cst["onesD"][:],
                                 T[:, eoff:eoff + 512],
                                 start=(j4 == 0), stop=(j4 == 3)))

                def emit_av1(j4, blk):
                    i = blk * 4 + j4
                    for hd in range(2):
                        eoff = et_o + (j4 * 2 + hd) * 512
                        vbase = vtm_o + i * 130 + 65 * hd
                        P.op("tensor", lambda e, vbase=vbase, eoff=eoff,
                             hd=hd, i=i: e.matmul(
                                 ps[4 + hd][0:65, :],
                                 T[:, vbase:vbase + 65],
                                 T[:, eoff:eoff + 512],
                                 start=(i == 0), stop=False))

                for blk in range(nkt // 4):
                    for j4 in range(4):
                        i = blk * 4 + j4
                        for hd in range(2):
                            off = 64 * hd
                            pss = SBANK[(hd, j4 % 2)]
                            P.op("tensor", lambda e, i=i, off=off, s=s, pss=pss:
                                 e.matmul(
                                     pss[:],
                                     T[off:off + 64,
                                       k_o + 128 * i: k_o + 128 * i + 128],
                                     T[off:off + 64,
                                       q_o + s * 512: q_o + s * 512 + 512],
                                     start=True, stop=True,
                                     tile_position=(off, 0)))
                        if j4 > 0:
                            emit_d(j4 - 1)
                        for hd in range(2):
                            off = 64 * hd
                            pss = SBANK[(hd, j4 % 2)]
                            kvidx = s * 32 + i
                            eoff = et_o + (j4 * 2 + hd) * 512
                            P.op("scalar", lambda e, kvidx=kvidx, eoff=eoff,
                                 pss=pss: e.activation(
                                     T[:, eoff:eoff + 512], pss[:], AF.Exp,
                                     bias=cst["kvec1"][:, kvidx:kvidx + 1],
                                     scale=0.125))
                            if i >= nkt - 8:
                                rel = i - (nkt - 8)
                                P.op("vector", lambda e, rel=rel, eoff=eoff:
                                     e.tensor_tensor(
                                         T[:, eoff:eoff + 512],
                                         T[:, eoff:eoff + 512],
                                         cst["maskstrip"][:, rel * QT:(rel + 1) * QT],
                                         ALU.mult))
                    emit_d(3)
                    for hd in range(2):
                        P.op("vector", lambda e, hd=hd: e.tensor_scalar(
                            T[:, rd_o + hd * 512: rd_o + hd * 512 + 512],
                            ps[2 + hd][:], 1.0e-30, None, ALU.max))
                        P.op("vector", lambda e, hd=hd: e.reciprocal(
                            T[:, rd_o + hd * 512: rd_o + hd * 512 + 512],
                            T[:, rd_o + hd * 512: rd_o + hd * 512 + 512]))
                    for j4 in range(4):
                        for hd in range(2):
                            eoff = et_o + (j4 * 2 + hd) * 512
                            pw = pw_o + hd * 512
                            P.op("vector", lambda e, eoff=eoff, hd=hd, pw=pw:
                                 e.tensor_tensor(
                                     T[:, pw:pw + 512], T[:, eoff:eoff + 512],
                                     T[:, rd_o + hd * 512: rd_o + hd * 512 + 512],
                                     ALU.mult))
                            P.op("scalar", lambda e, eoff=eoff, pw=pw:
                                 e.activation(
                                     T[:, eoff:eoff + 512], T[:, pw:pw + 512],
                                     AF.Exp))
                        if j4 > 0:
                            emit_av1(j4 - 1, blk)
                    emit_av1(3, blk)
                for hd in range(2):
                    off = 64 * hd
                    P.op("tensor", lambda e, hd=hd, s=s: e.matmul(
                        ps[4 + hd][0:65, :],
                        T[0:1, sfx_o + s * 130 + 65 * hd:
                          sfx_o + s * 130 + 65 * hd + 65],
                        cst["onesrow"][0:1, :], start=False, stop=True))
                    P.op("vector", lambda e, hd=hd: e.tensor_copy(
                        T[0:1, z_o:z_o + 512], ps[4 + hd][64:65, :]))
                    P.op("tensor", lambda e: e.matmul(
                        ps[6][0:64, :], cst["onesrow"][0:1, 0:64],
                        T[0:1, z_o:z_o + 512], start=True, stop=True))
                    P.op("vector", lambda e: e.reciprocal(
                        T[0:64, rz_o:rz_o + 512], ps[6][0:64, :]))
                    P.op("vector", lambda e, hd=hd, off=off: e.tensor_tensor(
                        T[off:off + 64, ao_o:ao_o + 512], ps[4 + hd][0:64, :],
                        T[0:64, rz_o:rz_o + 512], ALU.mult))
                P.op("gpsimd", lambda e, p=p, s=s: e.dma_start(
                    out=ao_hbm[128 * p:128 * p + 128, s * 512:(s + 1) * 512],
                    in_=T[:, att["ao"]:att["ao"] + 512]), dma=True)

        P._fused = dict(att=att, alloc_attn=alloc_attn, memset_vtm=memset_vtm,
                        production=production, attn_l0=attn_l0, attn_l1=attn_l1)
        _fused_part2(nc, P, al, alb, T, TB, wi32, cst, ps, semh, locals())
    return nc


def _fused_part2(nc, P, al, alb, T, TB, wi32, cst, ps, semh, env):
    att = P._fused["att"]
    alloc_attn = P._fused["alloc_attn"]
    memset_vtm = P._fused["memset_vtm"]
    production = P._fused["production"]
    attn_l0 = P._fused["attn_l0"]
    attn_l1 = P._fused["attn_l1"]
    g = lambda n: env[n]
    xg, xo, w_in, b_in, w_out = g("xg"), g("xo"), g("w_in"), g("b_in"), g("w_out")
    w_gate, w_up, w_down = g("w_gate"), g("w_up"), g("w_down")
    wqT, wkT, wvT, woT = g("wqT"), g("wkT"), g("wvT"), g("woT")
    bq, bk, bv = g("bq"), g("bk"), g("bv")
    w1T, w2T = g("w1T"), g("w2T")
    bout_sb, bo_sb = g("bout_sb"), g("bo_sb")
    yf_hbm, ao_hbm = g("yf_hbm"), g("ao_hbm")
    t2o_hbm, t3o_hbm = g("t2o_hbm"), g("t3o_hbm")
    cc_in, cc_out, ccE_in, ccE_out = (g("cc_in"), g("cc_out"), g("ccE_in"),
                                      g("ccE_out"))
    wgqT, wuqT, wdqT = g("wgqT"), g("wuqT"), g("wdqT")

    def make_ln():
        lnscr = dict(m=al.get(512), t=al.get(512), rs=al.get(512),
                     sq=al.get(512), y=al.get(512))

        def ln_tile(pre_off, out_fn):
            for oc in range(NCH):
                src = T[:, pre_off + oc * 512: pre_off + oc * 512 + 512]
                P.op("tensor", lambda e, src=src, oc=oc: e.matmul(
                    ps[6][:], cst["onesM"][:], src,
                    start=(oc == 0), stop=(oc == NCH - 1)))
                P.op("scalar", lambda e, src=src: e.activation(
                    T[:, lnscr["sq"]:lnscr["sq"] + 512], src, AF.Square))
                P.op("tensor", lambda e, oc=oc: e.matmul(
                    ps[7][:], cst["onesM"][:],
                    T[:, lnscr["sq"]:lnscr["sq"] + 512],
                    start=(oc == 0), stop=(oc == NCH - 1)))
            M_ = T[:, lnscr["m"]:lnscr["m"] + 512]
            T_ = T[:, lnscr["t"]:lnscr["t"] + 512]
            R_ = T[:, lnscr["rs"]:lnscr["rs"] + 512]
            SQ = T[:, lnscr["sq"]:lnscr["sq"] + 512]
            Y_ = T[:, lnscr["y"]:lnscr["y"] + 512]
            P.op("vector", lambda e: e.tensor_copy(M_, ps[6][:]))
            P.op("vector", lambda e: e.tensor_tensor(T_, ps[6][:], M_, ALU.mult))
            P.op("vector", lambda e: e.tensor_tensor(T_, ps[7][:], T_,
                                                     ALU.subtract))
            P.op("vector", lambda e: e.tensor_scalar(T_, T_, 1.0e-5, None,
                                                     ALU.add))
            P.op("scalar", lambda e: e.activation(R_, T_, AF.Sqrt))
            P.op("vector", lambda e: e.reciprocal(R_, R_))
            for oc in range(NCH):
                src = T[:, pre_off + oc * 512: pre_off + oc * 512 + 512]
                P.op("vector", lambda e, src=src: e.tensor_tensor(
                    SQ, src, M_, ALU.subtract))
                P.op("vector", lambda e: e.tensor_tensor(Y_, SQ, R_, ALU.mult))
                out_fn(oc)
        return ln_tile, lnscr

    def attn_epilogue(wsrc, bias_sb, resid_src, write_fn):
        aos = al.get(4096)
        wo2_o = al.get(2048)
        pre = al.get(4096)
        xr = al.get(4096)
        ln_tile, lnscr = make_ln()
        for s in range(NSLOT):
            P.op("sync", lambda e, s=s: e.dma_start(
                out=sb_slab(T, aos, 512), in_=fm_dram_slab(ao_hbm, s * 512, 512)),
                dma=True)
            P.op("sync", lambda e, s=s: e.dma_start(
                out=sb_slab(T, xr, 512),
                in_=fm_dram_slab(resid_src, s * 512, 512)), dma=True)
            for oc in range(NCH):
                wo2 = wo2_o + 1024 * (oc % 2)
                P.op("sync", lambda e, oc=oc, wo2=wo2: e.dma_start(
                    out=sb_slab(T, wo2, 128),
                    in_=fm_dram_slab(wsrc, oc * 128, 128)), dma=True)
                for ch in range(NCH):
                    P.op("tensor", lambda e, ch=ch, wo2=wo2: e.matmul(
                        ps[5][:], T[:, wo2 + ch * 128: wo2 + ch * 128 + 128],
                        T[:, aos + ch * 512: aos + ch * 512 + 512],
                        start=(ch == 0), stop=(ch == NCH - 1)))
                P.op("vector", lambda e, oc=oc: e.scalar_tensor_tensor(
                    T[:, pre + oc * 512: pre + oc * 512 + 512], ps[5][:],
                    bias_sb[:, oc:oc + 1],
                    T[:, xr + oc * 512: xr + oc * 512 + 512],
                    ALU.add, ALU.add))
            write_fn(s, ln_tile, lnscr, pre)

    al.push()
    alloc_attn()
    memset_vtm()
    wt_o = al.get(1024)
    wq_o = al.get(1024)
    P._q4ax = al.get(1024)
    P._q4ac = al.get(1024)
    P._q4qq = al.get(1024)
    P._q4bm = al.get(64)
    P._q4rs = al.get(64)
    tb_o = al.get(1024)

    def emit_qunit(wsrc, wdst, rt, c0, w):
        P.op("sync", lambda e: e.dma_start(
            out=T[:, wt_o:wt_o + w],
            in_=wsrc[rt * 128:(rt + 1) * 128, c0:c0 + w]), dma=True)
        quant_fp4(P, T, wt_o, wq_o, w, w // 16)
        nj = w // 128
        for j in range(nj):
            pslot = 128 * (j % 4)
            P.op("tensor", lambda e, j=j, pslot=pslot: e.transpose(
                ps[4][:, pslot:pslot + 128],
                T[:, wq_o + j * 128: wq_o + j * 128 + 128],
                cst["ident"][:]))
            P.op("vector", lambda e, j=j, pslot=pslot: e.tensor_copy(
                T[:, tb_o + j * 128: tb_o + j * 128 + 128],
                ps[4][:, pslot:pslot + 128]))
        P.op("gpsimd", lambda e: e.dma_start(
            out=wdst[c0:c0 + w, rt * 128:(rt + 1) * 128].rearrange(
                "(j p) c -> p j c", p=128),
            in_=T[:, tb_o:tb_o + w].rearrange("p (j c) -> p j c", c=128)),
            dma=True)

    qunits = []
    for (wsrc, wdst, rows, cols) in ((w_gate, wgqT, I0, D), (w_up, wuqT, I0, D),
                                     (w_down, wdqT, D, I0)):
        for rt in range(rows // 128):
            for c0 in range(0, cols, 1024):
                qunits.append((wsrc, wdst, rt, c0, min(1024, cols - c0)))
    qstate = {"pts": NPAIR * 4 + (NPAIR - 3) * NSLOT}

    def drip():
        if qunits:
            k = -(-len(qunits) // max(1, qstate["pts"]))
            for _ in range(min(k, len(qunits))):
                emit_qunit(*qunits.pop(0))
        qstate["pts"] -= 1

    for p in range(NPAIR):
        production(w_in, 128 * p, b_in, 128 * p, 0,
                   lambda tt: fm_dram_slab(xo, tt * 512, 512), NSLOT,
                   att["q"], "fm")
        drip()
        production(w_in, D + 128 * p, b_in, D + 128 * p, 1,
                   lambda tt: fm_dram_slab(xg, tt * 512, 512), 8,
                   att["k"], "fm")
        drip()
        production(w_in, 2 * D + 128 * p, b_in, 2 * D + 128 * p, 2,
                   lambda tt: fm_dram_slab(xg, tt * 512, 512), 8,
                   None, "v")
        drip()
        attn_l0(p, filler=(drip if p >= 3 else None))
        drip()
    while qunits:
        emit_qunit(*qunits.pop(0))
    al.pop()
    al.push()

    def wr_t1(s, ln_tile, lnscr, pre):
        def wfn(oc, s=s):
            yo = lnscr["y"]
            P.op("gpsimd", lambda e, oc=oc, s=s, yo=yo: e.dma_start(
                out=yf_hbm.rearrange("(c p) s -> p c s", p=128)[
                    :, oc:oc + 1, s * 512:(s + 1) * 512].squeeze(1),
                in_=T[:, yo:yo + 512]), dma=True)
        ln_tile(pre, wfn)
    attn_epilogue(w_out, bout_sb, xo, wr_t1)
    al.pop()

    al.push()
    yf_o = al.get(4096)
    xq_o = al.get(4096)
    hA_o = al.get(22 * 512)
    hB_o = al.get(11 * 128)
    hBq_o = al.get(11 * 128)
    wg2_o = al.get(2048)
    wu2_o = al.get(2048)
    sg_o = al.get(1024)
    yfr_o = sg_o
    P._q4ax = al.get(1408)
    P._q4ac = al.get(1408)
    P._q4qq = al.get(1408)
    P._q4bm = al.get(88)
    P._q4rs = al.get(88)
    ln2, lnscr2 = make_ln()
    xqtm_o = hB_o
    xtm_o = hBq_o
    pre2_o = hA_o
    assert al.off <= 35072, al.off

    def emit_xquant(s):
        P.op("sync", lambda e, s=s: e.dma_start(
            out=sb_slab(T, yf_o, 512), in_=fm_dram_slab(yf_hbm, s * 512, 512)),
            dma=True)
        for sub in range(4):
            for ch in range(NCH):
                P.op("tensor", lambda e, ch=ch, sub=sub: e.transpose(
                    ps[7][:, 0:128],
                    T[:, yf_o + ch * 512 + sub * 128:
                      yf_o + ch * 512 + sub * 128 + 128], cst["ident"][:]))
                P.op("vector", lambda e, ch=ch: e.tensor_copy(
                    T[:, xtm_o + ch * 128: xtm_o + ch * 128 + 128],
                    ps[7][:, 0:128]))
            quant_fp4(P, T, xtm_o, xqtm_o, 1024, 64)
            for ch in range(NCH):
                P.op("tensor", lambda e, ch=ch: e.transpose(
                    ps[7][:, 0:128],
                    T[:, xqtm_o + ch * 128: xqtm_o + ch * 128 + 128],
                    cst["ident"][:]))
                P.op("vector", lambda e, ch=ch, sub=sub: e.tensor_copy(
                    T[:, xq_o + ch * 512 + sub * 128:
                      xq_o + ch * 512 + sub * 128 + 128], ps[7][:, 0:128]))

    emit_xquant(0)
    for s in range(NSLOT):
        for ic in range(22):
            wg2 = wg2_o + 1024 * (ic % 2)
            wu2 = wu2_o + 1024 * (ic % 2)
            P.op("sync", lambda e, ic=ic, wg2=wg2: e.dma_start(
                out=sb_slab(T, wg2, 128),
                in_=fm_dram_slab(wgqT, ic * 128, 128)), dma=True)
            P.op("sync", lambda e, ic=ic, wu2=wu2: e.dma_start(
                out=sb_slab(T, wu2, 128),
                in_=fm_dram_slab(wuqT, ic * 128, 128)), dma=True)
            pg = ps[0] if ic % 2 == 0 else ps[2]
            pu = ps[1] if ic % 2 == 0 else ps[3]
            sg2 = sg_o + 512 * (ic % 2)
            for ch in range(NCH):
                P.op("tensor", lambda e, ch=ch, pg=pg, wg2=wg2: e.matmul(
                    pg[:], T[:, wg2 + ch * 128: wg2 + ch * 128 + 128],
                    T[:, xq_o + ch * 512: xq_o + ch * 512 + 512],
                    start=(ch == 0), stop=(ch == NCH - 1)))
            for ch in range(NCH):
                P.op("tensor", lambda e, ch=ch, pu=pu, wu2=wu2: e.matmul(
                    pu[:], T[:, wu2 + ch * 128: wu2 + ch * 128 + 128],
                    T[:, xq_o + ch * 512: xq_o + ch * 512 + 512],
                    start=(ch == 0), stop=(ch == NCH - 1)))
            P.op("scalar", lambda e, pg=pg, sg2=sg2: e.activation(
                T[:, sg2:sg2 + 512], pg[:], AF.Silu))
            P.op("vector", lambda e, ic=ic, pu=pu, sg2=sg2: e.tensor_tensor(
                T[:, hA_o + ic * 512: hA_o + ic * 512 + 512],
                T[:, sg2:sg2 + 512], pu[:], ALU.mult))
        for gg in range(2):
            for sub in range(4):
                for jj in range(11):
                    ic = gg * 11 + jj
                    P.op("tensor", lambda e, ic=ic, sub=sub: e.transpose(
                        ps[7][:, 0:128],
                        T[:, hA_o + ic * 512 + sub * 128:
                          hA_o + ic * 512 + sub * 128 + 128], cst["ident"][:]))
                    P.op("vector", lambda e, jj=jj: e.tensor_copy(
                        T[:, hB_o + jj * 128: hB_o + jj * 128 + 128],
                        ps[7][:, 0:128]))
                quant_fp4(P, T, hB_o, hBq_o, 1408, 88)
                for jj in range(11):
                    ic = gg * 11 + jj
                    P.op("tensor", lambda e, jj=jj: e.transpose(
                        ps[7][:, 0:128],
                        T[:, hBq_o + jj * 128: hBq_o + jj * 128 + 128],
                        cst["ident"][:]))
                    P.op("vector", lambda e, ic=ic, sub=sub: e.tensor_copy(
                        T[:, hA_o + ic * 512 + sub * 128:
                          hA_o + ic * 512 + sub * 128 + 128], ps[7][:, 0:128]))
        if s + 1 < NSLOT:
            emit_xquant(s + 1)
        for ic in range(22):
            wd2 = wg2_o + 1024 * (ic % 4)
            P.op("sync", lambda e, ic=ic, wd2=wd2: e.dma_start(
                out=T[:, wd2:wd2 + 1024],
                in_=wdqT[ic * 128:(ic + 1) * 128, :]), dma=True)
            for hc in range(NCH):
                P.op("tensor", lambda e, ic=ic, hc=hc, wd2=wd2: e.matmul(
                    ps[hc][:], T[:, wd2 + hc * 128: wd2 + hc * 128 + 128],
                    T[:, hA_o + ic * 512: hA_o + ic * 512 + 512],
                    start=(ic == 0), stop=(ic == 21)))
        for hc in range(NCH):
            yfr = yfr_o + 512 * (hc % 2)
            P.op("sync", lambda e, hc=hc, s=s, yfr=yfr: e.dma_start(
                out=T[:, yfr:yfr + 512],
                in_=fm_dram_slab(yf_hbm, s * 512, 512)[
                    :, hc:hc + 1, :].squeeze(1)), dma=True)
            P.op("vector", lambda e, hc=hc, yfr=yfr: e.tensor_tensor(
                T[:, pre2_o + hc * 512: pre2_o + hc * 512 + 512], ps[hc][:],
                T[:, yfr:yfr + 512], ALU.add))

        def wr2(oc, s=s):
            yo = lnscr2["y"]
            for dst in (t2o_hbm, cc_in):
                P.op("gpsimd", lambda e, oc=oc, s=s, yo=yo, dst=dst:
                     e.dma_start(
                         out=dst.rearrange("(c p) s -> p c s", p=128)[
                             :, oc:oc + 1, s * 512:(s + 1) * 512].squeeze(1),
                         in_=T[:, yo:yo + 512]), dma=True)
        ln2(pre2_o, wr2)
    al.pop()

    P.op("gpsimd", lambda e: e.collective_compute(
        "AllGather", ALU.bypass, replica_groups=CC_GROUPS,
        ins=[cc_in], outs=[cc_out]))

    def l1_src(tt):
        blk = 1 - (tt & 1)
        sc = tt // 2
        return cc_out.rearrange("(b c p) s -> p b c s", p=128, c=NCH)[
            :, blk, :, sc * 512:(sc + 1) * 512]

    al.push()
    alloc_attn(l0=False)
    att["e1"] = al.get(8 * 512)
    att["pw"] = al.get(1024)
    att["rd"] = al.get(1024)
    att["bv"] = al.get(8 * 130)
    att["sfx"] = al.get(4 * 130)
    memset_vtm()
    for p in range(NPAIR):
        production(wqT, 128 * p, bq, 128 * p, 0,
                   lambda tt: fm_dram_slab(t2o_hbm, tt * 512, 512), NSLOT,
                   att["q"], "fm")
        production(wkT, 128 * p, bk, 128 * p, 1, l1_src, 8, att["k"], "fm")
        production(wvT, 128 * p, bv, 128 * p, 2, l1_src, 8, None, "v")
        attn_l1(p)
    al.pop()
    al.push()

    def wr_t3(s, ln_tile, lnscr, pre):
        def wfn(oc, s=s):
            yo = lnscr["y"]
            P.op("gpsimd", lambda e, oc=oc, s=s, yo=yo: e.dma_start(
                out=t3o_hbm.rearrange("(c p) s -> p c s", p=128)[
                    :, oc:oc + 1, s * 512:(s + 1) * 512].squeeze(1),
                in_=T[:, yo:yo + 512]), dma=True)
            P.op("gpsimd", lambda e, oc=oc, s=s, yo=yo: e.dma_start(
                out=ccE_in[oc * 128:(oc + 1) * 128, 2 * s:2 * s + 1],
                in_=T[:, yo:yo + 1],
                allow_slow_non_contiguous=True), dma=True)
            P.op("gpsimd", lambda e, oc=oc, s=s, yo=yo: e.dma_start(
                out=ccE_in[oc * 128:(oc + 1) * 128, 2 * s + 1:2 * s + 2],
                in_=T[:, yo + 511:yo + 512],
                allow_slow_non_contiguous=True), dma=True)
        ln_tile(pre, wfn)
    attn_epilogue(woT, bo_sb, t2o_hbm, wr_t3)
    al.pop()

    P.op("gpsimd", lambda e: e.collective_compute(
        "AllGather", ALU.bypass, replica_groups=CC_GROUPS,
        ins=[ccE_in], outs=[ccE_out]))

    _fused_l1c(nc, P, al, alb, T, TB, wi32, cst, ps, semh, env)


def _fused_l1c(nc, P, al, alb, T, TB, wi32, cst, ps, semh, env):
    g = lambda n: env[n]
    w1T, w2T = g("w1T"), g("w2T")
    cw_sb, cb_sb = g("cw_sb"), g("cb_sb")
    t3o_hbm, ccE_out = g("t3o_hbm"), g("ccE_out")
    w1q_hbm, w2q_hbm = g("w1q_hbm"), g("w2q_hbm")
    out = g("out")
    selE = cst["selE"]

    al.push()
    alb.push()
    cs_o = al.get(8)
    c1_o = al.get(1)
    c2_o = al.get(1)
    al.push()
    wt_o = al.get(4096)
    wi_o = al.get(4096)
    alb.push()
    wtb_o = alb.get(4096)

    def i32rt(buf_off, n, pr=128):
        for c0 in range(0, n, 1024):
            cn = min(1024, n - c0)
            P.op("vector", lambda e, c0=c0, cn=cn, pr=pr: e.tensor_copy(
                wi32[0:pr, 0:cn], T[0:pr, buf_off + c0:buf_off + c0 + cn]))
            P.op("vector", lambda e, c0=c0, cn=cn, pr=pr: e.tensor_copy(
                T[0:pr, buf_off + c0:buf_off + c0 + cn], wi32[0:pr, 0:cn]))

    def ternary(wsrc, wdst, rows, cols, c_off):
        nrt = rows // 128
        P.op("gpsimd", lambda e: e.memset(T[:, cs_o:cs_o + 1], 0.0))
        for rt in range(nrt):
            P.op("sync", lambda e, rt=rt, wsrc=wsrc, cols=cols: e.dma_start(
                out=T[:, wt_o:wt_o + cols],
                in_=wsrc[rt * 128:(rt + 1) * 128, :]), dma=True)
            P.op("scalar", lambda e, cols=cols: e.activation(
                T[:, wi_o:wi_o + cols], T[:, wt_o:wt_o + cols], AF.Abs))
            P.op("vector", lambda e, cols=cols: e.tensor_reduce(
                T[:, cs_o + 1:cs_o + 2], T[:, wi_o:wi_o + cols], AX.X, ALU.add))
            P.op("vector", lambda e: e.tensor_tensor(
                T[:, cs_o:cs_o + 1], T[:, cs_o:cs_o + 1],
                T[:, cs_o + 1:cs_o + 2], ALU.add))
        P.op("tensor", lambda e: e.matmul(
            ps[7][0:1, 0:1], T[:, cs_o:cs_o + 1], cst["onesD"][:, 0:1],
            start=True, stop=True))
        P.op("vector", lambda e: e.tensor_copy(T[0:1, cs_o + 2:cs_o + 3],
                                               ps[7][0:1, 0:1]))
        P.op("tensor", lambda e: e.matmul(
            ps[7][0:128, 0:1], cst["onesD"][0:1, :], T[0:1, cs_o + 2:cs_o + 3],
            start=True, stop=True))
        P.op("vector", lambda e, rows=rows, cols=cols: e.tensor_scalar(
            T[:, c_off:c_off + 1], ps[7][0:128, 0:1],
            1.0 / float(rows * cols), 1.0e-5, ALU.mult, ALU.max))
        P.op("vector", lambda e: e.reciprocal(T[:, cs_o + 3:cs_o + 4],
                                              T[:, c_off:c_off + 1]))
        for rt in range(nrt):
            P.op("sync", lambda e, rt=rt, wsrc=wsrc, cols=cols: e.dma_start(
                out=T[:, wt_o:wt_o + cols],
                in_=wsrc[rt * 128:(rt + 1) * 128, :]), dma=True)
            P.op("vector", lambda e, cols=cols: e.tensor_scalar(
                T[:, wi_o:wi_o + cols], T[:, wt_o:wt_o + cols],
                T[:, cs_o + 3:cs_o + 4], None, ALU.mult))
            i32rt(wi_o, cols)
            P.op("vector", lambda e, cols=cols: e.tensor_scalar(
                T[:, wi_o:wi_o + cols], T[:, wi_o:wi_o + cols], 1.0, -1.0,
                ALU.min, ALU.max))
            P.op("vector", lambda e, cols=cols: e.tensor_copy(
                TB[:, wtb_o:wtb_o + cols], T[:, wi_o:wi_o + cols]))
            P.op("gpsimd", lambda e, rt=rt, wdst=wdst, cols=cols: e.dma_start(
                out=wdst[rt * 128:(rt + 1) * 128, :],
                in_=TB[:, wtb_o:wtb_o + cols]), dma=True)

    ternary(w1T, w1q_hbm, D, I1, c1_o)
    ternary(w2T, w2q_hbm, I1, D, c2_o)
    al.pop()
    alb.pop()

    cand_o = al.get(128)
    te_o = al.get(64)
    etm_o = al.get(1024)
    erow_o = al.get(16)
    febc_o = al.get(8)
    xqe_o = alb.get(64)
    for blk in range(2):
        P.op("sync", lambda e, blk=blk: e.dma_start(
            out=T[:, cand_o:cand_o + 128].rearrange(
                "p (c j) -> p c j", j=16)[:, :, blk * 8:(blk + 1) * 8],
            in_=ccE_out.rearrange("(b c p) s -> p b c s", p=128, c=NCH)[
                :, blk, :, :]), dma=True)
    for ed in range(8):
        for j in range(16):
            sap = selE[:, ed * 16 + j:ed * 16 + j + 1]
            src = T[:, cand_o:cand_o + 128].rearrange(
                "p (c j) -> p c j", j=16)[:, :, j:j + 1]
            dst = T[:, te_o:te_o + 64].rearrange(
                "p (c x) -> p c x", x=8)[:, :, ed:ed + 1]
            if j == 0:
                P.op("vector", lambda e, src=src, dst=dst, sap=sap:
                     e.tensor_scalar(dst, src, sap, None, ALU.mult))
            else:
                P.op("vector", lambda e, src=src, dst=dst, sap=sap:
                     e.scalar_tensor_tensor(dst, src, sap, dst,
                                            ALU.mult, ALU.add))
    for ch in range(NCH):
        P.op("tensor", lambda e, ch=ch: e.transpose(
            ps[7][0:8, 0:128], T[:, te_o + ch * 8: te_o + ch * 8 + 8],
            cst["ident"][:]))
        P.op("vector", lambda e, ch=ch: e.tensor_copy(
            T[0:8, etm_o + ch * 128: etm_o + ch * 128 + 128], ps[7][0:8, 0:128]))
    P.op("vector", lambda e: e.tensor_reduce(
        T[0:8, erow_o:erow_o + 1], T[0:8, etm_o:etm_o + 1024], AX.X, ALU.max,
        apply_absolute_value=True))
    P.op("vector", lambda e: e.tensor_scalar(
        T[0:8, erow_o:erow_o + 1], T[0:8, erow_o:erow_o + 1], 1.0e-5, None,
        ALU.max))
    P.op("vector", lambda e: e.reciprocal(
        T[0:8, erow_o + 1:erow_o + 2], T[0:8, erow_o:erow_o + 1]))
    P.op("vector", lambda e: e.tensor_scalar(
        T[0:8, etm_o:etm_o + 1024], T[0:8, etm_o:etm_o + 1024],
        T[0:8, erow_o + 1:erow_o + 2], 127.0, ALU.mult, ALU.mult))
    i32rt(etm_o, 1024, pr=8)
    for ch in range(NCH):
        P.op("tensor", lambda e, ch=ch: e.transpose(
            ps[7][0:128, 0:8], T[0:8, etm_o + ch * 128: etm_o + ch * 128 + 128],
            cst["ident"][0:8, 0:8]))
        P.op("vector", lambda e, ch=ch: e.tensor_copy(
            TB[:, xqe_o + ch * 8: xqe_o + ch * 8 + 8], ps[7][0:128, 0:8]))
    P.op("vector", lambda e: e.tensor_scalar(
        T[0:8, erow_o + 2:erow_o + 3], T[0:8, erow_o:erow_o + 1],
        T[0:8, c1_o:c1_o + 1], 1.0 / 127.0, ALU.mult, ALU.mult))
    P.op("tensor", lambda e: e.transpose(
        ps[7][0:1, 0:8], T[0:8, erow_o + 2:erow_o + 3], cst["ident"][0:8, 0:8]))
    P.op("vector", lambda e: e.tensor_copy(T[0:1, erow_o + 4:erow_o + 12],
                                           ps[7][0:1, 0:8]))
    P.op("tensor", lambda e: e.matmul(
        ps[7][0:128, 0:8], cst["onesD"][0:1, :], T[0:1, erow_o + 4:erow_o + 12],
        start=True, stop=True))
    P.op("vector", lambda e: e.tensor_copy(T[:, febc_o:febc_o + 8],
                                           ps[7][0:128, 0:8]))

    xt_o = al.get(4096)
    xtm_o = al.get(1024)
    ctok_o = al.get(16)
    f_o = al.get(4)
    frow_o = al.get(512)
    fbc_o = al.get(512)
    fbc2_o = al.get(512)
    stm_o = al.get(4 * 4096)
    hx_o = al.get(514)
    pre_o = al.get(4096)
    tmp_o = al.get(512)
    lnm_o = al.get(512)
    lnt_o = al.get(512)
    lnr_o = al.get(512)
    lnq_o = al.get(512)
    lny_o = al.get(512)
    xqs_o = alb.get(4096)
    w1t_o = alb.get(1024)
    w2t_o = alb.get(1024)
    xq2_o = alb.get(32 * 512)
    assert al.off <= 35072, al.off
    assert alb.off <= 22656, alb.off

    def int8_quant_tm(src_off, n, sub, c_scale_off):
        P.op("vector", lambda e, n=n: e.tensor_reduce(
            T[:, ctok_o:ctok_o + 1], T[:, src_off:src_off + n], AX.X, ALU.max,
            apply_absolute_value=True))
        P.op("vector", lambda e: e.tensor_scalar(
            T[:, ctok_o:ctok_o + 1], T[:, ctok_o:ctok_o + 1], 1.0e-5, None,
            ALU.max))
        P.op("vector", lambda e: e.reciprocal(
            T[:, ctok_o + 1:ctok_o + 2], T[:, ctok_o:ctok_o + 1]))
        P.op("vector", lambda e, n=n: e.tensor_scalar(
            T[:, pre_o:pre_o + n], T[:, src_off:src_off + n],
            T[:, ctok_o + 1:ctok_o + 2], 127.0, ALU.mult, ALU.mult))
        i32rt(pre_o, n)
        P.op("vector", lambda e, sub=sub: e.tensor_scalar(
            T[:, f_o + sub:f_o + sub + 1], T[:, ctok_o:ctok_o + 1],
            T[:, c_scale_off:c_scale_off + 1], 1.0 / 127.0, ALU.mult, ALU.mult))

    def factor_bcast(dst_off):
        for sub in range(4):
            P.op("tensor", lambda e, sub=sub: e.transpose(
                ps[7][0:1, 0:128], T[:, f_o + sub:f_o + sub + 1],
                cst["ident"][:]))
            P.op("vector", lambda e, sub=sub: e.tensor_copy(
                T[0:1, frow_o + sub * 128: frow_o + sub * 128 + 128],
                ps[7][0:1, 0:128]))
        P.op("tensor", lambda e: e.matmul(
            ps[7][0:128, 0:512], cst["onesD"][0:1, :],
            T[0:1, frow_o:frow_o + 512], start=True, stop=True))
        P.op("vector", lambda e, dst_off=dst_off: e.tensor_copy(
            T[:, dst_off:dst_off + 512], ps[7][0:128, 0:512]))

    for s in range(NSLOT):
        P.op("sync", lambda e, s=s: e.dma_start(
            out=sb_slab(T, xt_o, 512), in_=fm_dram_slab(t3o_hbm, s * 512, 512)),
            dma=True)
        for sub in range(4):
            for ch in range(NCH):
                P.op("tensor", lambda e, ch=ch, sub=sub: e.transpose(
                    ps[7][:, 0:128],
                    T[:, xt_o + ch * 512 + sub * 128:
                      xt_o + ch * 512 + sub * 128 + 128], cst["ident"][:]))
                P.op("vector", lambda e, ch=ch: e.tensor_copy(
                    T[:, xtm_o + ch * 128: xtm_o + ch * 128 + 128],
                    ps[7][:, 0:128]))
            int8_quant_tm(xtm_o, 1024, sub, c1_o)
            for ch in range(NCH):
                P.op("tensor", lambda e, ch=ch: e.transpose(
                    ps[7][:, 0:128],
                    T[:, pre_o + ch * 128: pre_o + ch * 128 + 128],
                    cst["ident"][:]))
                P.op("vector", lambda e, ch=ch, sub=sub: e.tensor_copy(
                    TB[:, xqs_o + ch * 512 + sub * 128:
                       xqs_o + ch * 512 + sub * 128 + 128], ps[7][:, 0:128]))
        factor_bcast(fbc_o)
        for ich in range(32):
            P.op("sync", lambda e, ich=ich: e.dma_start(
                out=sb_slab(TB, w1t_o, 128),
                in_=fm_dram_slab(w1q_hbm, ich * 128, 128)), dma=True)
            ph = ps[0] if ich % 2 == 0 else ps[2]
            pe2 = ps[1] if ich % 2 == 0 else ps[3]
            for ch in range(NCH):
                P.op("tensor", lambda e, ch=ch, ph=ph: e.matmul(
                    ph[:], TB[:, w1t_o + ch * 128: w1t_o + ch * 128 + 128],
                    TB[:, xqs_o + ch * 512: xqs_o + ch * 512 + 512],
                    start=(ch == 0), stop=(ch == NCH - 1)))
            for ch in range(NCH):
                P.op("tensor", lambda e, ch=ch, s=s, pe2=pe2: e.matmul(
                    pe2[:, 0:2],
                    TB[:, w1t_o + ch * 128: w1t_o + ch * 128 + 128],
                    TB[:, xqe_o + ch * 8 + 2 * s: xqe_o + ch * 8 + 2 * s + 2],
                    start=(ch == 0), stop=(ch == NCH - 1)))
            P.op("vector", lambda e, ph=ph: e.tensor_tensor(
                T[:, hx_o + 1:hx_o + 513], ph[:], T[:, fbc_o:fbc_o + 512],
                ALU.mult))
            P.op("vector", lambda e, s=s, pe2=pe2: e.tensor_scalar(
                T[:, hx_o:hx_o + 1], pe2[:, 0:1],
                T[:, febc_o + 2 * s:febc_o + 2 * s + 1], None, ALU.mult))
            P.op("vector", lambda e, s=s, pe2=pe2: e.tensor_scalar(
                T[:, hx_o + 513:hx_o + 514], pe2[:, 1:2],
                T[:, febc_o + 2 * s + 1:febc_o + 2 * s + 2], None, ALU.mult))
            P.op("vector", lambda e, ich=ich: e.tensor_scalar(
                T[:, tmp_o:tmp_o + 512], T[:, hx_o:hx_o + 512],
                cw_sb[:, ich * 3:ich * 3 + 1], cb_sb[:, ich:ich + 1],
                ALU.mult, ALU.add))
            P.op("vector", lambda e, ich=ich: e.scalar_tensor_tensor(
                T[:, tmp_o:tmp_o + 512], T[:, hx_o + 1:hx_o + 513],
                cw_sb[:, ich * 3 + 1:ich * 3 + 2], T[:, tmp_o:tmp_o + 512],
                ALU.mult, ALU.add))
            P.op("vector", lambda e, ich=ich: e.scalar_tensor_tensor(
                T[:, tmp_o:tmp_o + 512], T[:, hx_o + 2:hx_o + 514],
                cw_sb[:, ich * 3 + 2:ich * 3 + 3], T[:, tmp_o:tmp_o + 512],
                ALU.mult, ALU.add))
            P.op("scalar", lambda e: e.activation(
                T[:, tmp_o:tmp_o + 512], T[:, tmp_o:tmp_o + 512], AF.Silu))
            for sub in range(4):
                P.op("tensor", lambda e, sub=sub: e.transpose(
                    ps[7][:, 0:128],
                    T[:, tmp_o + sub * 128: tmp_o + sub * 128 + 128],
                    cst["ident"][:]))
                P.op("vector", lambda e, sub=sub, ich=ich: e.tensor_copy(
                    T[:, stm_o + sub * 4096 + ich * 128:
                      stm_o + sub * 4096 + ich * 128 + 128], ps[7][:, 0:128]))
        for sub in range(4):
            int8_quant_tm(stm_o + sub * 4096, 4096, sub, c2_o)
            for ich in range(32):
                P.op("tensor", lambda e, ich=ich: e.transpose(
                    ps[7][:, 0:128],
                    T[:, pre_o + ich * 128: pre_o + ich * 128 + 128],
                    cst["ident"][:]))
                P.op("vector", lambda e, ich=ich, sub=sub: e.tensor_copy(
                    TB[:, xq2_o + ich * 512 + sub * 128:
                       xq2_o + ich * 512 + sub * 128 + 128], ps[7][:, 0:128]))
        factor_bcast(fbc2_o)
        for ich in range(32):
            P.op("sync", lambda e, ich=ich: e.dma_start(
                out=TB[:, w2t_o:w2t_o + 1024],
                in_=w2q_hbm[ich * 128:(ich + 1) * 128, :]), dma=True)
            for hc in range(NCH):
                P.op("tensor", lambda e, ich=ich, hc=hc: e.matmul(
                    ps[hc][:], TB[:, w2t_o + hc * 128: w2t_o + hc * 128 + 128],
                    TB[:, xq2_o + ich * 512: xq2_o + ich * 512 + 512],
                    start=(ich == 0), stop=(ich == 31)))
        for hc in range(NCH):
            P.op("vector", lambda e, hc=hc: e.tensor_tensor(
                T[:, tmp_o:tmp_o + 512], ps[hc][:], T[:, fbc2_o:fbc2_o + 512],
                ALU.mult))
            P.op("vector", lambda e, hc=hc: e.tensor_tensor(
                T[:, pre_o + hc * 512: pre_o + hc * 512 + 512],
                T[:, tmp_o:tmp_o + 512],
                T[:, xt_o + hc * 512: xt_o + hc * 512 + 512], ALU.add))
        for oc in range(NCH):
            src = T[:, pre_o + oc * 512: pre_o + oc * 512 + 512]
            P.op("tensor", lambda e, src=src, oc=oc: e.matmul(
                ps[6][:], cst["onesM"][:], src,
                start=(oc == 0), stop=(oc == NCH - 1)))
            P.op("scalar", lambda e, src=src: e.activation(
                T[:, lnq_o:lnq_o + 512], src, AF.Square))
            P.op("tensor", lambda e, oc=oc: e.matmul(
                ps[7][:], cst["onesM"][:], T[:, lnq_o:lnq_o + 512],
                start=(oc == 0), stop=(oc == NCH - 1)))
        P.op("vector", lambda e: e.tensor_copy(T[:, lnm_o:lnm_o + 512], ps[6][:]))
        P.op("vector", lambda e: e.tensor_tensor(
            T[:, lnt_o:lnt_o + 512], ps[6][:], T[:, lnm_o:lnm_o + 512], ALU.mult))
        P.op("vector", lambda e: e.tensor_tensor(
            T[:, lnt_o:lnt_o + 512], ps[7][:], T[:, lnt_o:lnt_o + 512],
            ALU.subtract))
        P.op("vector", lambda e: e.tensor_scalar(
            T[:, lnt_o:lnt_o + 512], T[:, lnt_o:lnt_o + 512], 1.0e-5, None,
            ALU.add))
        P.op("scalar", lambda e: e.activation(
            T[:, lnr_o:lnr_o + 512], T[:, lnt_o:lnt_o + 512], AF.Sqrt))
        P.op("vector", lambda e: e.reciprocal(
            T[:, lnr_o:lnr_o + 512], T[:, lnr_o:lnr_o + 512]))
        for oc in range(NCH):
            src = T[:, pre_o + oc * 512: pre_o + oc * 512 + 512]
            P.op("vector", lambda e, src=src: e.tensor_tensor(
                T[:, lnq_o:lnq_o + 512], src, T[:, lnm_o:lnm_o + 512],
                ALU.subtract))
            P.op("vector", lambda e: e.tensor_tensor(
                T[:, lny_o:lny_o + 512], T[:, lnq_o:lnq_o + 512],
                T[:, lnr_o:lnr_o + 512], ALU.mult))
            P.op("gpsimd", lambda e, oc=oc, s=s: e.dma_start(
                out=out.rearrange("(c p) s -> p c s", p=128)[
                    :, oc:oc + 1, s * 512:(s + 1) * 512].squeeze(1),
                in_=T[:, lny_o:lny_o + 512]), dma=True)
    al.pop()
    alb.pop()
    P.finalize(semh)


# version 12
# speedup vs baseline: 1.0329x; 1.0329x over previous
import numpy as np
from contextlib import ExitStack

import concourse.bass as bass
import concourse.mybir as mybir
from concourse.bass_utils import run_bass_kernel_spmd

F32 = mybir.dt.float32
BF16 = mybir.dt.bfloat16
I32 = mybir.dt.int32
AF = mybir.ActivationFunctionType
ALU = mybir.AluOpType
AX = mybir.AxisListType

B, S, D, H, DH = 4, 4096, 1024, 16, 64
I0, I1 = 2816, 4096
NCH = D // 128
NPAIR = H // 2
NSLOT = 4
QT = 512
NKT = [8, 16, 24, 32]
SLOPES = [2.0 ** (-0.5 * (i + 1)) for i in range(H)]
NEG = -1.0e38
ASPL = 5
DEBUG = False


def _own_qtiles(half):
    return [1, 3, 5, 7] if half == 0 else [0, 2, 4, 6]


def _own_cols(half):
    cols = []
    for g in _own_qtiles(half):
        cols.extend(range(g * QT, (g + 1) * QT))
    return np.array(cols, dtype=np.int64)


def _real_nkt(half, s):
    return (_own_qtiles(half)[s] + 1) * QT // 128


def _host_consts(half):
    kl = np.arange(128)[:, None]
    ql = np.arange(QT)[None, :]
    strip = np.zeros((128, 8 * QT), dtype=np.float32)
    for rel in range(8):
        if half == 0:
            m = np.ones((128, QT)) if rel < 4 else (kl + 128 * (rel - 4) <= ql)
        else:
            m = (kl + 128 * rel <= ql) if rel < 4 else np.zeros((128, QT))
        strip[:, rel * QT:(rel + 1) * QT] = m.astype(np.float32)

    kvec0 = np.full((128, H * NSLOT * 32), NEG, dtype=np.float32)
    kvec1 = np.full((128, NSLOT * 32), NEG, dtype=np.float32)
    for s in range(NSLOT):
        q_base = _own_qtiles(half)[s] * QT
        realn = _real_nkt(half, s)
        for i in range(min(NKT[s], realn)):
            kvec1[:, s * 32 + i] = 0.0
            for h in range(H):
                r = q_base if h < ASPL else q_base + QT - 1
                v = SLOPES[h] * (128 * i + np.arange(128, dtype=np.float64) - r)
                kvec0[:, (h * NSLOT + s) * 32 + i] = v.astype(np.float32)

    qbias = np.zeros((128, ASPL * QT), dtype=np.float32)
    for h in range(ASPL):
        qbias[:, h * QT:(h + 1) * QT] = (
            -SLOPES[h] * np.arange(QT, dtype=np.float64)).astype(np.float32)[None, :]

    sel = np.zeros((1, NSLOT * 8), dtype=np.float32)
    for s in range(NSLOT):
        sel[0, s * 8 + NKT[s] // 4: s * 8 + 8] = 1.0

    return dict(
        maskstrip=strip,
        kvec0=kvec0, kvec1=kvec1, qbias=qbias,
        ident=np.eye(128, dtype=np.float32),
        onesM=np.full((128, 128), 1.0 / D, dtype=np.float32),
        onesD=np.ones((128, 128), dtype=np.float32),
        onesrow=np.ones((1, QT), dtype=np.float32),
        sel=sel)


ENGS = ("sync", "tensor", "vector", "scalar", "gpsimd")
OWNSEM = {"sync": None, "tensor": "tn", "vector": "vc", "scalar": "sc",
          "gpsimd": "gp"}
ALLSEMS = ("ld", "gd", "tn", "vc", "sc", "gp")


_PBMODE = ["strict"]
_REC = []
_DEP = {}
_DRAINS = set()
SEMBLK = 512


def _ap_ranges(opnd):
    try:
        memref = opnd.memref
        ap = [list(x) for x in opnd.ap]
        off = int(opnd.offset)
    except Exception:
        return None
    if not ap:
        return (memref, 0, 1 << 60)
    pstride = ap[0][0]
    span = 1
    for (st, cnt) in ap[1:]:
        span += abs(st) * (cnt - 1)
    if len(ap) == 1:
        span = 1
    col = off % pstride if pstride > 0 else off
    return (memref, col, col + span)


class PB:
    def __init__(self, nc):
        self.nc = nc
        self.q = {e: [] for e in ENGS}
        self.cnt = {s: 0 for s in ALLSEMS}
        self.lastw = {e: {s: 0 for s in ALLSEMS} for e in ENGS}
        self.need_drain = {e: False for e in ENGS}
        self.gidx = 0

    def _waits(self, eng):
        own = OWNSEM[eng]
        for s in ALLSEMS:
            if s == own:
                continue
            v = self.cnt[s]
            if v > self.lastw[eng][s]:
                self.q[eng].append(("w", s, v))
                self.lastw[eng][s] = v

    def _dep_waits(self, eng, idx):
        own = OWNSEM[eng]
        for (s, v) in _DEP.get(idx, ()):
            if s == own:
                continue
            if v > self.lastw[eng][s]:
                self.q[eng].append(("w", s, v))
                self.lastw[eng][s] = v

    def op(self, eng, fn, dma=False):
        idx = self.gidx
        self.gidx += 1
        mode = _PBMODE[0]
        if mode == "strict":
            self._waits(eng)
        elif mode == "dep":
            self._dep_waits(eng, idx)
        if eng in ("vector", "scalar", "gpsimd") and not dma:
            if mode == "dep":
                if idx in _DRAINS:
                    self.q[eng].append(("d",))
            elif self.need_drain[eng]:
                self.q[eng].append(("d",))
            self.need_drain[eng] = True
        if dma:
            sem = "ld" if eng == "sync" else "gd"
            n = 16
        else:
            sem, n = OWNSEM[eng], 1
            if sem is None:
                sem, n = "ld", 16
        self.cnt[sem] += n
        self.q[eng].append(("o", fn, sem, n, idx, eng, self.cnt[sem]))

    def finalize(self, semh, final_eng="gpsimd"):
        record = _PBMODE[0] == "record"
        self._waits(final_eng)
        own = OWNSEM[final_eng]
        if own is not None:
            self.q[final_eng].append(("w", own, self.cnt[own]))
        with self.nc.Block() as blk:
            for ename, meth in (("sync", blk.sync), ("tensor", blk.tensor),
                                ("vector", blk.vector), ("scalar", blk.scalar),
                                ("gpsimd", blk.gpsimd)):
                items = self.q[ename]
                if not items:
                    continue

                def body(e, items=items):
                    for it in items:
                        if it[0] == "w":
                            e.wait_ge(semh[it[1]], it[2])
                        elif it[0] == "d":
                            e.drain()
                        else:
                            _, fn, sem, n, idx, eng, cafter = it
                            ins = fn(e)
                            if record:
                                mi = getattr(ins, "ins", None)
                                rd, wr = [], []
                                if mi is not None:
                                    try:
                                        _i = mi.ins() if callable(mi.ins) else mi.ins
                                        _o = mi.outs() if callable(mi.outs) else mi.outs
                                        rd = [_ap_ranges(o) for o in _i]
                                        wr = [_ap_ranges(o) for o in _o]
                                    except Exception:
                                        rd, wr = [("ALL", 0, 1 << 60)],                                                  [("ALL", 0, 1 << 60)]
                                    nm = type(mi).__name__
                                    if "Collective" in nm:
                                        rd = [("ALL", 0, 1 << 60)]
                                        wr = [("ALL", 0, 1 << 60)]
                                _REC.append((idx, eng, sem, cafter,
                                             [x for x in rd if x],
                                             [x for x in wr if x]))
                            ins.then_inc(semh[sem], n)
                meth(body)


def _compute_deps(rec):
    rec = sorted(rec, key=lambda r: r[0])
    state = {}
    bar = {}
    deps = {}
    _DRAINS.clear()
    dirty = {}

    def keys_of(m, a, b):
        if m == "ALL":
            return None
        if b - a > 40960:
            return [(m, "W")]
        return [(m, i) for i in range(a // SEMBLK, b // SEMBLK + 1)]

    for (idx, eng, sem, cafter, rd, wr) in rec:
        need = dict(bar)

        def want(ps, pc):
            if ps == sem:
                return
            if need.get(ps, 0) < pc:
                need[ps] = pc

        def check_key(m, k, for_write):
            mm = state.get(m)
            if not mm:
                return
            tgts = list(mm.items()) if k == "W" else                 [(k, mm.get(k)), ("W", mm.get("W"))]
            for kk, st in tgts:
                if not st:
                    continue
                if st.get("w"):
                    want(*st["w"])
                if for_write:
                    for en, cn in st.get("r", {}).items():
                        want(en, cn)

        def upd_read(m, k):
            mm = state.setdefault(m, {})
            st = mm.setdefault(k, {"w": None, "r": {}})
            st["r"][sem] = cafter

        def upd_write(m, k):
            mm = state.setdefault(m, {})
            if k == "W":
                mm.clear()
            st = mm.setdefault(k, {"w": None, "r": {}})
            st["w"] = (sem, cafter)
            st["r"] = {}

        barrier = False
        rks, wks = [], []
        for (m, a, b) in rd:
            ks = keys_of(m, a, b)
            if ks is None:
                barrier = True
            else:
                rks += ks
        for (m, a, b) in wr:
            ks = keys_of(m, a, b)
            if ks is None:
                barrier = True
            else:
                wks += ks
        if barrier:
            for m, mm in state.items():
                for kk, st in mm.items():
                    if st.get("w"):
                        want(*st["w"])
                    for en, cn in st.get("r", {}).items():
                        want(en, cn)
            state.clear()
            bar = dict(bar)
            bar[sem] = cafter
        for (m, k) in rks:
            check_key(m, k, False)
        for (m, k) in wks:
            check_key(m, k, True)
        for (m, k) in rks:
            upd_read(m, k)
        for (m, k) in wks:
            upd_write(m, k)
        if sem not in ("ld", "gd"):
            dd = dirty.setdefault(sem, set())
            touched = set(rks) | set(wks)
            hit = False
            for (m, k) in touched:
                if (m, k) in dd or (m, "W") in dd:
                    hit = True
                    break
            if not hit:
                wholes = {m for (m, k) in touched if k == "W"}
                if wholes and any(m in wholes for (m, _) in dd):
                    hit = True
            if barrier or hit:
                _DRAINS.add(idx)
                dd.clear()
            dd.update(wks)
        if need:
            deps[idx] = sorted(need.items())
    return deps


def fm_dram_slab(src, c0, n):
    return src.rearrange("(c p) s -> p c s", p=128)[:, :, c0:c0 + n]


def sb_slab(t, off, n):
    return t[:, off:off + NCH * n].rearrange("p (c n) -> p c n", n=n)


class Alloc:

    def __init__(self, t, cols):
        self.t = t
        self.cols = cols
        self.off = 0
        self.marks = []

    def get(self, n):
        assert self.off + n <= self.cols, (self.off, n, self.cols)
        off = self.off
        self.off += n
        return off

    def push(self):
        self.marks.append(self.off)

    def pop(self):
        self.off = self.marks.pop()



def quant_fp4(P, scr, x_off, out_off, np_, nblk, prange=128):
    t = scr
    pr = slice(0, prange)
    ax_o = P._q4ax; ac_o = P._q4ac; qq_o = P._q4qq
    bm_o = P._q4bm; rs_o = P._q4rs
    seg = lambda off: t[pr, off:off + np_].rearrange("p (g e) -> p g e", e=16)
    bseg = lambda off: t[pr, off:off + nblk].rearrange(
        "p (g e) -> p g e", e=1).broadcast_to((prange, nblk, 16))
    X = t[pr, x_off:x_off + np_]
    AXs = t[pr, ax_o:ax_o + np_]
    AC = t[pr, ac_o:ac_o + np_]
    QQ = t[pr, qq_o:qq_o + np_]
    BM = t[pr, bm_o:bm_o + nblk]
    RS = t[pr, rs_o:rs_o + nblk]
    OUT = t[pr, out_off:out_off + np_]
    P.op("scalar", lambda e: e.activation(AXs, X, AF.Abs))
    P.op("vector", lambda e: e.tensor_reduce(BM, seg(ax_o), AX.X, ALU.max))
    P.op("vector", lambda e: e.tensor_scalar(BM, BM, 1.0 / 6.0, 1.0e-12,
                                             ALU.mult, ALU.max))
    P.op("vector", lambda e: e.reciprocal(RS, BM))
    P.op("vector", lambda e: e.tensor_tensor(seg(ax_o), seg(ax_o), bseg(rs_o),
                                             ALU.mult))
    P.op("vector", lambda e: e.tensor_scalar(AC, AXs, 0.25, None, ALU.is_gt))
    for mid in (0.75, 1.25, 1.75):
        P.op("vector", lambda e, mid=mid: e.scalar_tensor_tensor(
            AC, AXs, mid, AC, ALU.is_gt, ALU.add))
    P.op("vector", lambda e: e.tensor_scalar(QQ, AXs, 2.5, None, ALU.is_gt))
    for mid in (3.5, 5.0, 5.0):
        P.op("vector", lambda e, mid=mid: e.scalar_tensor_tensor(
            QQ, AXs, mid, QQ, ALU.is_gt, ALU.add))
    P.op("vector", lambda e: e.scalar_tensor_tensor(QQ, AC, 0.5, QQ,
                                                    ALU.mult, ALU.add))
    P.op("scalar", lambda e: e.activation(AC, X, AF.Sign))
    P.op("vector", lambda e: e.tensor_tensor(QQ, QQ, AC, ALU.mult))
    P.op("vector", lambda e: e.tensor_tensor(seg(out_off), seg(qq_o),
                                             bseg(bm_o), ALU.mult))



def build_l0(args_shapes_only=None):
    nc = bass.Bass("TRN2", target_bir_lowering=False)
    dram = lambda n, sh, dt=F32, kind="ExternalInput": nc.dram_tensor(
        n, sh, dt, kind=kind).ap()
    xg = dram("xg", [D, S])
    xo = dram("xo", [D, NSLOT * QT])
    w_in = dram("w_in", [D, 3 * D])
    b_in = dram("b_in", [3 * D, 1])
    w_out = dram("w_out", [D, D])
    b_out = dram("b_out", [128, NCH])
    w_gate = dram("w_gate", [I0, D])
    w_up = dram("w_up", [I0, D])
    w_down = dram("w_down", [D, I0])
    C = {k: dram("c_" + k, list(v_.shape)) for k, v_ in _host_consts(0).items()}
    t2own = dram("t2own", [D, NSLOT * QT], kind="ExternalOutput")
    yf_hbm = dram("yf_hbm", [D, NSLOT * QT],
                  kind="ExternalOutput" if DEBUG else "Internal")
    dbg = dram("dbg", [128, 4096], kind="ExternalOutput") if DEBUG else None
    dbg2 = dram("dbg2", [128, 4096], kind="ExternalOutput") if DEBUG else None
    wgqT = dram("wgqT", [D, I0], kind="Internal")
    wuqT = dram("wuqT", [D, I0], kind="Internal")
    wdqT = dram("wdqT", [I0, D], kind="Internal")

    with ExitStack() as ctx:
        SB = lambda n, sh, dt=F32: ctx.enter_context(nc.sbuf_tensor(n, sh, dt))
        scr_t = SB("scr", [128, 40960])
        cst = {k: SB("s_" + k, list(v_.shape)) for k, v_ in _host_consts(0).items()}
        bout_sb = SB("bout", [128, NCH])
        ps = [ctx.enter_context(nc.psum_tensor(f"pb{i}", [128, 512], F32))
              for i in range(8)]
        semh = {s: ctx.enter_context(nc.semaphore(s)) for s in ALLSEMS}
        P = PB(nc)
        al = Alloc(scr_t, 40960)

        for k in C:
            P.op("sync", lambda e, k=k: e.dma_start(out=cst[k][:], in_=C[k]),
                 dma=True)
        P.op("sync", lambda e: e.dma_start(
            out=bout_sb[:], in_=b_out), dma=True)

        al.push()
        acc_o = al.get(NCH * 2048)
        al.push()
        k_o = al.get(4096)
        q_o = al.get(2048)
        vtm_o = al.get(32 * 130)
        xt_o = al.get(4096)
        w3_o = al.get(3 * 1024)
        wo_o = al.get(1024)
        b3_o = al.get(4)
        vf_o = al.get(512)
        et_o = al.get(2 * 512)
        etmp_o = al.get(512)
        ao_o = al.get(512)
        z_o = al.get(512)
        rz_o = al.get(512)

        T = scr_t
        P.op("gpsimd", lambda e: e.memset(T[:, acc_o:acc_o + NCH * 2048], 0.0))
        P.op("gpsimd", lambda e: e.memset(
            T[:, vtm_o:vtm_o + 32 * 130].rearrange(
                "p (t c) -> p t c", c=130)[:, :, 64:65], 1.0))
        P.op("gpsimd", lambda e: e.memset(
            T[:, vtm_o:vtm_o + 32 * 130].rearrange(
                "p (t c) -> p t c", c=130)[:, :, 129:130], 1.0))

        def production(P, wsrc_cols, bias_rows, x_src, x_ncols, psum, dst_off,
                       wofs, add_bias=True):
            wt, wc0 = wsrc_cols
            P.op("sync", lambda e: e.dma_start(
                out=sb_slab(T, wofs, 128), in_=fm_dram_slab(wt, wc0, 128)),
                dma=True)
            if add_bias:
                P.op("sync", lambda e: e.dma_start(
                    out=T[:, b3_o + bias_rows[1]:b3_o + bias_rows[1] + 1],
                    in_=bias_rows[0][bias_rows[2]:bias_rows[2] + 128, 0:1]),
                    dma=True)
            ntt = x_ncols // 512
            for tt in range(ntt):
                P.op("sync", lambda e, tt=tt: e.dma_start(
                    out=sb_slab(T, xt_o, 512),
                    in_=fm_dram_slab(x_src, tt * 512, 512)), dma=True)
                for ch in range(NCH):
                    P.op("tensor", lambda e, ch=ch: e.matmul(
                        psum[:],
                        T[:, wofs + ch * 128: wofs + ch * 128 + 128],
                        T[:, xt_o + ch * 512: xt_o + ch * 512 + 512],
                        start=(ch == 0), stop=(ch == NCH - 1)))
                if add_bias:
                    P.op("vector", lambda e, tt=tt: e.tensor_scalar(
                        T[:, dst_off + tt * 512: dst_off + tt * 512 + 512],
                        psum[:], T[:, b3_o + bias_rows[1]:b3_o + bias_rows[1] + 1],
                        None, ALU.add))
                else:
                    P.op("vector", lambda e, tt=tt: e.tensor_copy(
                        T[:, dst_off + tt * 512: dst_off + tt * 512 + 512],
                        psum[:]))

        def make_ln(al2):
            lnscr = dict(m=al2.get(512), t=al2.get(512), rs=al2.get(512),
                         sq=al2.get(512), y=al2.get(512))

            def ln_tile(pre_off, out_fn):
                for oc in range(NCH):
                    src = T[:, pre_off + oc * 512: pre_off + oc * 512 + 512]
                    P.op("tensor", lambda e, src=src, oc=oc: e.matmul(
                        ps[6][:], cst["onesM"][:], src,
                        start=(oc == 0), stop=(oc == NCH - 1)))
                    P.op("scalar", lambda e, src=src: e.activation(
                        T[:, lnscr["sq"]:lnscr["sq"] + 512], src, AF.Square))
                    P.op("tensor", lambda e, oc=oc: e.matmul(
                        ps[7][:], cst["onesM"][:],
                        T[:, lnscr["sq"]:lnscr["sq"] + 512],
                        start=(oc == 0), stop=(oc == NCH - 1)))
                M_ = T[:, lnscr["m"]:lnscr["m"] + 512]
                T_ = T[:, lnscr["t"]:lnscr["t"] + 512]
                R_ = T[:, lnscr["rs"]:lnscr["rs"] + 512]
                SQ = T[:, lnscr["sq"]:lnscr["sq"] + 512]
                Y_ = T[:, lnscr["y"]:lnscr["y"] + 512]
                P.op("vector", lambda e: e.tensor_copy(M_, ps[6][:]))
                P.op("vector", lambda e: e.tensor_tensor(T_, ps[6][:], M_, ALU.mult))
                P.op("vector", lambda e: e.tensor_tensor(T_, ps[7][:], T_,
                                                         ALU.subtract))
                P.op("vector", lambda e: e.tensor_scalar(T_, T_, 1.0e-5, None,
                                                         ALU.add))
                P.op("scalar", lambda e: e.activation(R_, T_, AF.Sqrt))
                P.op("vector", lambda e: e.reciprocal(R_, R_))
                for oc in range(NCH):
                    src = T[:, pre_off + oc * 512: pre_off + oc * 512 + 512]
                    P.op("vector", lambda e, src=src: e.tensor_tensor(
                        SQ, src, M_, ALU.subtract))
                    P.op("vector", lambda e: e.tensor_tensor(Y_, SQ, R_, ALU.mult))
                    out_fn(oc)
            return ln_tile, lnscr

        for p in range(NPAIR):
            production(P, (w_in, 128 * p), (b_in, 0, 128 * p), xo,
                       NSLOT * QT, ps[0], q_o, w3_o)
            production(P, (w_in, D + 128 * p), (b_in, 1, D + 128 * p), xg,
                       S, ps[1], k_o, w3_o + 1024)
            P.op("sync", lambda e, p=p: e.dma_start(
                out=sb_slab(T, w3_o + 2048, 128),
                in_=fm_dram_slab(w_in, 2 * D + 128 * p, 128)), dma=True)
            P.op("sync", lambda e, p=p: e.dma_start(
                out=T[:, b3_o + 2:b3_o + 3],
                in_=b_in[2 * D + 128 * p:2 * D + 128 * p + 128, 0:1]), dma=True)
            for tt in range(8):
                P.op("sync", lambda e, tt=tt: e.dma_start(
                    out=sb_slab(T, xt_o, 512),
                    in_=fm_dram_slab(xg, tt * 512, 512)), dma=True)
                for ch in range(NCH):
                    P.op("tensor", lambda e, ch=ch: e.matmul(
                        ps[2][:],
                        T[:, w3_o + 2048 + ch * 128: w3_o + 2048 + ch * 128 + 128],
                        T[:, xt_o + ch * 512: xt_o + ch * 512 + 512],
                        start=(ch == 0), stop=(ch == NCH - 1)))
                P.op("vector", lambda e: e.tensor_scalar(
                    T[:, vf_o:vf_o + 512], ps[2][:],
                    T[:, b3_o + 2:b3_o + 3], None, ALU.add))
                for j in range(4):
                    P.op("tensor", lambda e, j=j: e.transpose(
                        ps[7][:, 0:128],
                        T[:, vf_o + j * 128: vf_o + j * 128 + 128],
                        cst["ident"][:]))
                    base = vtm_o + (tt * 4 + j) * 130
                    P.op("vector", lambda e, base=base: e.tensor_copy(
                        T[:, base:base + 64], ps[7][:, 0:64]))
                    P.op("vector", lambda e, base=base: e.tensor_copy(
                        T[:, base + 65:base + 129], ps[7][:, 64:128]))

            P.op("sync", lambda e, p=p: e.dma_start(
                out=T[:, wo_o:wo_o + 1024].rearrange("q (c n) -> q c n", n=128),
                in_=w_out[128 * p:128 * p + 128, :].rearrange(
                    "q (c n) -> q c n", n=128)), dma=True)
            for s in range(NSLOT):
                nkt = NKT[s]
                for hd in range(2):
                    gh = 2 * p + hd
                    off = 64 * hd
                    pav = ps[2 + hd]
                    pss = ps[hd]
                    for i in range(nkt):
                        P.op("tensor", lambda e, i=i, off=off, pss=pss, s=s:
                             e.matmul(
                                 pss[:],
                                 T[off:off + 64, k_o + 128 * i: k_o + 128 * i + 128],
                                 T[off:off + 64, q_o + s * 512: q_o + s * 512 + 512],
                                 start=True, stop=True,
                                 tile_position=(off, 0)))
                        kvidx = (gh * NSLOT + s) * 32 + i
                        if gh < ASPL:
                            P.op("vector", lambda e, pss=pss, gh=gh:
                                 e.scalar_tensor_tensor(
                                     T[:, etmp_o:etmp_o + 512], pss[:], 0.125,
                                     cst["qbias"][:, gh * QT:(gh + 1) * QT],
                                     ALU.mult, ALU.add))
                            P.op("vector", lambda e, kvidx=kvidx:
                                 e.tensor_scalar(
                                     T[:, etmp_o:etmp_o + 512],
                                     T[:, etmp_o:etmp_o + 512],
                                     cst["kvec0"][:, kvidx:kvidx + 1], 30.0,
                                     ALU.add, ALU.min))
                            P.op("scalar", lambda e, hd=hd:
                                 e.activation(
                                     T[:, et_o + hd * 512: et_o + hd * 512 + 512],
                                     T[:, etmp_o:etmp_o + 512], AF.Exp))
                        else:
                            P.op("scalar", lambda e, kvidx=kvidx, hd=hd, pss=pss:
                                 e.activation(
                                     T[:, et_o + hd * 512: et_o + hd * 512 + 512],
                                     pss[:], AF.Exp,
                                     bias=cst["kvec0"][:, kvidx:kvidx + 1],
                                     scale=0.125))
                        if i >= nkt - 8:
                            rel = i - (nkt - 8)
                            P.op("vector", lambda e, rel=rel, hd=hd:
                                 e.tensor_tensor(
                                     T[:, et_o + hd * 512: et_o + hd * 512 + 512],
                                     T[:, et_o + hd * 512: et_o + hd * 512 + 512],
                                     cst["maskstrip"][:, rel * QT:(rel + 1) * QT],
                                     ALU.mult))
                        vbase = vtm_o + i * 130 + 65 * hd
                        P.op("tensor", lambda e, vbase=vbase, i=i, hd=hd,
                             pav=pav, nkt=nkt: e.matmul(
                                 pav[0:65, :], T[:, vbase:vbase + 65],
                                 T[:, et_o + hd * 512: et_o + hd * 512 + 512],
                                 start=(i == 0), stop=(i == nkt - 1)))
                    P.op("vector", lambda e, pav=pav: e.tensor_copy(
                        T[0:1, z_o:z_o + 512], pav[64:65, :]))
                    P.op("tensor", lambda e: e.matmul(
                        ps[4][0:64, :], cst["onesrow"][0:1, 0:64],
                        T[0:1, z_o:z_o + 512], start=True, stop=True))
                    P.op("vector", lambda e: e.reciprocal(
                        T[0:64, rz_o:rz_o + 512], ps[4][0:64, :]))
                    P.op("vector", lambda e, pav=pav, off=off: e.tensor_tensor(
                        T[off:off + 64, ao_o:ao_o + 512], pav[0:64, :],
                        T[0:64, rz_o:rz_o + 512], ALU.mult))
                if DEBUG and p == 0 and s == 0:
                    for nm, ofs in (("q", q_o), ("k", k_o), ("e", et_o),
                                    ("ao", ao_o)):
                        pass
                    P.op("gpsimd", lambda e: e.dma_start(
                        out=dbg[:, 0:512], in_=T[:, q_o:q_o + 512]), dma=True)
                    P.op("gpsimd", lambda e: e.dma_start(
                        out=dbg[:, 512:1024], in_=T[:, k_o:k_o + 512]), dma=True)
                    P.op("gpsimd", lambda e: e.dma_start(
                        out=dbg[:, 1024:1154], in_=T[:, vtm_o:vtm_o + 130]),
                        dma=True)
                    P.op("gpsimd", lambda e: e.dma_start(
                        out=dbg[:, 1536:2048], in_=T[:, et_o:et_o + 512]),
                        dma=True)
                    P.op("gpsimd", lambda e: e.dma_start(
                        out=dbg[:, 2048:2560], in_=T[:, ao_o:ao_o + 512]),
                        dma=True)
                    P.op("vector", lambda e: e.tensor_copy(
                        T[:, etmp_o:etmp_o + 512], ps[4][0:128, :]))
                    P.op("gpsimd", lambda e: e.dma_start(
                        out=dbg[:, 2560:3072], in_=T[:, etmp_o:etmp_o + 512]),
                        dma=True)
                for oc in range(NCH):
                    P.op("tensor", lambda e, oc=oc: e.matmul(
                        ps[5][:], T[:, wo_o + oc * 128: wo_o + oc * 128 + 128],
                        T[:, ao_o:ao_o + 512], start=True, stop=True))
                    dst = acc_o + oc * 2048 + s * 512
                    P.op("vector", lambda e, dst=dst: e.tensor_tensor(
                        T[:, dst:dst + 512], T[:, dst:dst + 512], ps[5][:],
                        ALU.add))

        al.pop()
        al.push()
        pre_o = al.get(4096)
        xr_o = al.get(4096)
        ln1, lnscr1 = make_ln(al)

        for s in range(NSLOT):
            P.op("sync", lambda e, s=s: e.dma_start(
                out=sb_slab(T, xr_o, 512), in_=fm_dram_slab(xo, s * 512, 512)),
                dma=True)
            for oc in range(NCH):
                P.op("vector", lambda e, s=s, oc=oc: e.scalar_tensor_tensor(
                    T[:, pre_o + oc * 512: pre_o + oc * 512 + 512],
                    T[:, acc_o + oc * 2048 + s * 512:
                      acc_o + oc * 2048 + s * 512 + 512],
                    bout_sb[:, oc:oc + 1],
                    T[:, xr_o + oc * 512: xr_o + oc * 512 + 512],
                    ALU.add, ALU.add))
            if DEBUG and s == 0:
                P.op("gpsimd", lambda e: e.dma_start(
                    out=dbg[:, 3072:3584], in_=T[:, acc_o:acc_o + 512]),
                    dma=True)
                P.op("gpsimd", lambda e: e.dma_start(
                    out=dbg[:, 3584:4096], in_=T[:, pre_o:pre_o + 512]),
                    dma=True)

            def wr(oc, s=s):
                yo = lnscr1["y"]
                P.op("gpsimd", lambda e, oc=oc, s=s, yo=yo: e.dma_start(
                    out=yf_hbm.rearrange("(c p) s -> p c s", p=128)[
                        :, oc:oc + 1, s * 512:(s + 1) * 512].squeeze(1),
                    in_=T[:, yo:yo + 512]), dma=True)
            ln1(pre_o, wr)
        al.pop()
        al.pop()

        al.push()
        wt_o = al.get(2816)
        wq_o = al.get(2816)
        P._q4ax = al.get(2816)
        P._q4ac = al.get(2816)
        P._q4qq = al.get(2816)
        P._q4bm = al.get(176)
        P._q4rs = al.get(176)
        tb_o = al.get(128)
        for (wsrc, wdst, rows, cols) in ((w_gate, wgqT, I0, D),
                                         (w_up, wuqT, I0, D),
                                         (w_down, wdqT, D, I0)):
            nrt = rows // 128
            for rt in range(nrt):
                P.op("sync", lambda e, wsrc=wsrc, rt=rt, cols=cols: e.dma_start(
                    out=T[:, wt_o:wt_o + cols],
                    in_=wsrc[rt * 128:(rt + 1) * 128, :]), dma=True)
                quant_fp4(P, T, wt_o, wq_o, cols, cols // 16)
                for j in range(cols // 128):
                    P.op("tensor", lambda e, j=j: e.transpose(
                        ps[7][:, 0:128],
                        T[:, wq_o + j * 128: wq_o + j * 128 + 128],
                        cst["ident"][:]))
                    P.op("vector", lambda e: e.tensor_copy(
                        T[:, tb_o:tb_o + 128], ps[7][:, 0:128]))
                    P.op("gpsimd", lambda e, wdst=wdst, j=j, rt=rt: e.dma_start(
                        out=wdst[j * 128:(j + 1) * 128,
                                 rt * 128:(rt + 1) * 128],
                        in_=T[:, tb_o:tb_o + 128]), dma=True)
        al.pop()

        al.push()
        yf_o = al.get(4096)
        xtm_o = al.get(1024)
        xqtm_o = al.get(1024)
        xq_o = al.get(4096)
        hA_o = al.get(22 * 512)
        hB_o = al.get(11 * 128)
        hBq_o = al.get(11 * 128)
        wg2_o = al.get(1024)
        wu2_o = al.get(1024)
        wd2_o = al.get(1024)
        sg_o = al.get(512)
        P._q4ax = al.get(1408)
        P._q4ac = al.get(1408)
        P._q4qq = al.get(1408)
        P._q4bm = al.get(88)
        P._q4rs = al.get(88)
        pre2_o = al.get(4096)
        ln2, lnscr2 = make_ln(al)
        assert al.off <= 40960, al.off

        for s in range(NSLOT):
            P.op("sync", lambda e, s=s: e.dma_start(
                out=sb_slab(T, yf_o, 512), in_=fm_dram_slab(yf_hbm, s * 512, 512)),
                dma=True)
            for sub in range(4):
                for ch in range(NCH):
                    P.op("tensor", lambda e, ch=ch, sub=sub: e.transpose(
                        ps[7][:, 0:128],
                        T[:, yf_o + ch * 512 + sub * 128:
                          yf_o + ch * 512 + sub * 128 + 128],
                        cst["ident"][:]))
                    P.op("vector", lambda e, ch=ch: e.tensor_copy(
                        T[:, xtm_o + ch * 128: xtm_o + ch * 128 + 128],
                        ps[7][:, 0:128]))
                quant_fp4(P, T, xtm_o, xqtm_o, 1024, 64)
                for ch in range(NCH):
                    P.op("tensor", lambda e, ch=ch: e.transpose(
                        ps[7][:, 0:128],
                        T[:, xqtm_o + ch * 128: xqtm_o + ch * 128 + 128],
                        cst["ident"][:]))
                    P.op("vector", lambda e, ch=ch, sub=sub: e.tensor_copy(
                        T[:, xq_o + ch * 512 + sub * 128:
                          xq_o + ch * 512 + sub * 128 + 128],
                        ps[7][:, 0:128]))
            if DEBUG and s == 0:
                P.op("gpsimd", lambda e: e.dma_start(
                    out=dbg2[:, 0:512], in_=T[:, xq_o:xq_o + 512]), dma=True)
                P.op("gpsimd", lambda e: e.dma_start(
                    out=dbg2[:, 2048:3072], in_=T[:, yf_o:yf_o + 1024]), dma=True)
            for ic in range(22):
                P.op("sync", lambda e, ic=ic: e.dma_start(
                    out=sb_slab(T, wg2_o, 128),
                    in_=fm_dram_slab(wgqT, ic * 128, 128)), dma=True)
                P.op("sync", lambda e, ic=ic: e.dma_start(
                    out=sb_slab(T, wu2_o, 128),
                    in_=fm_dram_slab(wuqT, ic * 128, 128)), dma=True)
                for ch in range(NCH):
                    P.op("tensor", lambda e, ch=ch: e.matmul(
                        ps[0][:], T[:, wg2_o + ch * 128: wg2_o + ch * 128 + 128],
                        T[:, xq_o + ch * 512: xq_o + ch * 512 + 512],
                        start=(ch == 0), stop=(ch == NCH - 1)))
                for ch in range(NCH):
                    P.op("tensor", lambda e, ch=ch: e.matmul(
                        ps[1][:], T[:, wu2_o + ch * 128: wu2_o + ch * 128 + 128],
                        T[:, xq_o + ch * 512: xq_o + ch * 512 + 512],
                        start=(ch == 0), stop=(ch == NCH - 1)))
                P.op("scalar", lambda e: e.activation(
                    T[:, sg_o:sg_o + 512], ps[0][:], AF.Silu))
                P.op("vector", lambda e, ic=ic: e.tensor_tensor(
                    T[:, hA_o + ic * 512: hA_o + ic * 512 + 512],
                    T[:, sg_o:sg_o + 512], ps[1][:], ALU.mult))
            if DEBUG and s == 0:
                P.op("gpsimd", lambda e: e.dma_start(
                    out=dbg2[:, 512:1024], in_=T[:, hA_o:hA_o + 512]), dma=True)
            for sub in range(4):
                for g in range(2):
                    for jj in range(11):
                        ic = g * 11 + jj
                        P.op("tensor", lambda e, ic=ic, sub=sub: e.transpose(
                            ps[7][:, 0:128],
                            T[:, hA_o + ic * 512 + sub * 128:
                              hA_o + ic * 512 + sub * 128 + 128],
                            cst["ident"][:]))
                        P.op("vector", lambda e, jj=jj: e.tensor_copy(
                            T[:, hB_o + jj * 128: hB_o + jj * 128 + 128],
                            ps[7][:, 0:128]))
                    quant_fp4(P, T, hB_o, hBq_o, 1408, 88)
                    for jj in range(11):
                        ic = g * 11 + jj
                        P.op("tensor", lambda e, jj=jj: e.transpose(
                            ps[7][:, 0:128],
                            T[:, hBq_o + jj * 128: hBq_o + jj * 128 + 128],
                            cst["ident"][:]))
                        P.op("vector", lambda e, ic=ic, sub=sub: e.tensor_copy(
                            T[:, hA_o + ic * 512 + sub * 128:
                              hA_o + ic * 512 + sub * 128 + 128],
                            ps[7][:, 0:128]))
            if DEBUG and s == 0:
                P.op("gpsimd", lambda e: e.dma_start(
                    out=dbg2[:, 1024:1536], in_=T[:, hA_o:hA_o + 512]), dma=True)
                P.op("gpsimd", lambda e: e.dma_start(
                    out=dbg2[:, 3072:4096], in_=T[:, wg2_o:wg2_o + 1024]), dma=True)
            for ic in range(22):
                P.op("sync", lambda e, ic=ic: e.dma_start(
                    out=T[:, wd2_o:wd2_o + 1024],
                    in_=wdqT[ic * 128:(ic + 1) * 128, :]), dma=True)
                for hc in range(NCH):
                    P.op("tensor", lambda e, ic=ic, hc=hc: e.matmul(
                        ps[hc][:],
                        T[:, wd2_o + hc * 128: wd2_o + hc * 128 + 128],
                        T[:, hA_o + ic * 512: hA_o + ic * 512 + 512],
                        start=(ic == 0), stop=(ic == 21)))
            for hc in range(NCH):
                P.op("vector", lambda e, hc=hc: e.tensor_tensor(
                    T[:, pre2_o + hc * 512: pre2_o + hc * 512 + 512],
                    ps[hc][:], T[:, yf_o + hc * 512: yf_o + hc * 512 + 512],
                    ALU.add))
            if DEBUG and s == 0:
                P.op("gpsimd", lambda e: e.dma_start(
                    out=dbg2[:, 1536:2048], in_=T[:, pre2_o:pre2_o + 512]),
                    dma=True)

            def wr2(oc, s=s):
                yo = lnscr2["y"]
                P.op("gpsimd", lambda e, oc=oc, s=s, yo=yo: e.dma_start(
                    out=t2own.rearrange("(c p) s -> p c s", p=128)[
                        :, oc:oc + 1, s * 512:(s + 1) * 512].squeeze(1),
                    in_=T[:, yo:yo + 512]), dma=True)
            ln2(pre2_o, wr2)
        al.pop()
        P.finalize(semh)
    return nc



def build_l1a():
    nc = bass.Bass("TRN2", target_bir_lowering=False)
    dram = lambda n, sh, dt=F32, kind="ExternalInput": nc.dram_tensor(
        n, sh, dt, kind=kind).ap()
    xg = dram("xg", [D, S])
    xo = dram("xo", [D, NSLOT * QT])
    wqT = dram("wqT", [D, D])
    wkT = dram("wkT", [D, D])
    wvT = dram("wvT", [D, D])
    woT = dram("woT", [D, D])
    bq = dram("bq", [D, 1])
    bk = dram("bk", [D, 1])
    bv = dram("bv", [D, 1])
    bo = dram("bo", [128, NCH])
    C = {k: dram("c_" + k, list(v_.shape)) for k, v_ in _host_consts(0).items()
         if k in ("maskstrip", "kvec1", "ident", "onesM", "onesD", "onesrow", "sel")}
    t3own = dram("t3own", [D, NSLOT * QT], kind="ExternalOutput")

    with ExitStack() as ctx:
        SB = lambda n, sh, dt=F32: ctx.enter_context(nc.sbuf_tensor(n, sh, dt))
        scr_t = SB("scr", [128, 45056])
        cst = {k: SB("s_" + k, list(v_.shape)) for k, v_ in _host_consts(0).items()
               if k in ("maskstrip", "kvec1", "ident", "onesM", "onesD", "onesrow", "sel")}
        bo_sb = SB("bo_sb", [128, NCH])
        ps = [ctx.enter_context(nc.psum_tensor(f"pb{i}", [128, 512], F32))
              for i in range(8)]
        semh = {s: ctx.enter_context(nc.semaphore(s)) for s in ALLSEMS}
        P = PB(nc)
        al = Alloc(scr_t, 45056)
        T = scr_t

        for k in C:
            P.op("sync", lambda e, k=k: e.dma_start(out=cst[k][:], in_=C[k]),
                 dma=True)
        P.op("sync", lambda e: e.dma_start(
            out=bo_sb[:], in_=bo), dma=True)

        acc_o = al.get(NCH * 2048)
        al.push()
        k_o = al.get(4096)
        q_o = al.get(2048)
        vtm_o = al.get(32 * 130)
        xt_o = al.get(4096)
        w3_o = al.get(3 * 1024)
        wo_o = al.get(1024)
        b3_o = al.get(4)
        vf_o = al.get(512)
        e1_o = al.get(8 * 512)
        pw_o = al.get(512)
        rd_o = al.get(2 * 512)
        ao_o = al.get(512)
        z_o = al.get(512)
        rz_o = al.get(512)
        bv_o = al.get(8 * 130)
        sfx_o = al.get(4 * 130)

        P.op("gpsimd", lambda e: e.memset(T[:, acc_o:acc_o + NCH * 2048], 0.0))
        P.op("gpsimd", lambda e: e.memset(
            T[:, vtm_o:vtm_o + 32 * 130].rearrange(
                "p (t c) -> p t c", c=130)[:, :, 64:65], 1.0))
        P.op("gpsimd", lambda e: e.memset(
            T[:, vtm_o:vtm_o + 32 * 130].rearrange(
                "p (t c) -> p t c", c=130)[:, :, 129:130], 1.0))

        def prod(wsrc, bsrc, col0, x_src, x_ncols, psum, dst_off, wofs, bslot):
            P.op("sync", lambda e: e.dma_start(
                out=sb_slab(T, wofs, 128), in_=fm_dram_slab(wsrc, col0, 128)),
                dma=True)
            P.op("sync", lambda e: e.dma_start(
                out=T[:, b3_o + bslot:b3_o + bslot + 1],
                in_=bsrc[col0:col0 + 128, 0:1]), dma=True)
            for tt in range(x_ncols // 512):
                P.op("sync", lambda e, tt=tt: e.dma_start(
                    out=sb_slab(T, xt_o, 512),
                    in_=fm_dram_slab(x_src, tt * 512, 512)), dma=True)
                for ch in range(NCH):
                    P.op("tensor", lambda e, ch=ch: e.matmul(
                        psum[:], T[:, wofs + ch * 128: wofs + ch * 128 + 128],
                        T[:, xt_o + ch * 512: xt_o + ch * 512 + 512],
                        start=(ch == 0), stop=(ch == NCH - 1)))
                if dst_off is not None:
                    P.op("vector", lambda e, tt=tt: e.tensor_scalar(
                        T[:, dst_off + tt * 512: dst_off + tt * 512 + 512],
                        psum[:], T[:, b3_o + bslot:b3_o + bslot + 1], None,
                        ALU.add))
                else:
                    P.op("vector", lambda e: e.tensor_scalar(
                        T[:, vf_o:vf_o + 512], psum[:],
                        T[:, b3_o + bslot:b3_o + bslot + 1], None, ALU.add))
                    for j in range(4):
                        P.op("tensor", lambda e, j=j: e.transpose(
                            ps[7][:, 0:128],
                            T[:, vf_o + j * 128: vf_o + j * 128 + 128],
                            cst["ident"][:]))
                        base = vtm_o + (tt * 4 + j) * 130
                        P.op("vector", lambda e, base=base: e.tensor_copy(
                            T[:, base:base + 64], ps[7][:, 0:64]))
                        P.op("vector", lambda e, base=base: e.tensor_copy(
                            T[:, base + 65:base + 129], ps[7][:, 64:128]))

        for p in range(NPAIR):
            prod(wqT, bq, 128 * p, xo, NSLOT * QT, ps[0], q_o, w3_o, 0)
            prod(wkT, bk, 128 * p, xg, S, ps[1], k_o, w3_o + 1024, 1)
            prod(wvT, bv, 128 * p, xg, S, ps[2], None, w3_o + 2048, 2)
            for blk in range(8):
                for t4 in range(4):
                    P.op("tensor", lambda e, blk=blk, t4=t4: e.matmul(
                        ps[6][0:1, 0:130], cst["onesD"][:, 0:1],
                        T[:, vtm_o + (blk * 4 + t4) * 130:
                          vtm_o + (blk * 4 + t4) * 130 + 130],
                        start=(t4 == 0), stop=(t4 == 3)))
                P.op("vector", lambda e, blk=blk: e.tensor_copy(
                    T[0:1, bv_o + blk * 130: bv_o + blk * 130 + 130],
                    ps[6][0:1, 0:130]))
            for s in range(NSLOT):
                for b in range(8):
                    sap = cst["sel"][0:1, s * 8 + b:s * 8 + b + 1]
                    if b == 0:
                        P.op("vector", lambda e, s=s, b=b, sap=sap: e.tensor_scalar(
                            T[0:1, sfx_o + s * 130: sfx_o + s * 130 + 130],
                            T[0:1, bv_o + b * 130: bv_o + b * 130 + 130],
                            sap, None, ALU.mult))
                    else:
                        P.op("vector", lambda e, s=s, b=b, sap=sap:
                             e.scalar_tensor_tensor(
                                 T[0:1, sfx_o + s * 130: sfx_o + s * 130 + 130],
                                 T[0:1, bv_o + b * 130: bv_o + b * 130 + 130],
                                 sap,
                                 T[0:1, sfx_o + s * 130: sfx_o + s * 130 + 130],
                                 ALU.mult, ALU.add))
            P.op("sync", lambda e, p=p: e.dma_start(
                out=T[:, wo_o:wo_o + 1024].rearrange("q (c n) -> q c n", n=128),
                in_=woT[128 * p:128 * p + 128, :].rearrange(
                    "q (c n) -> q c n", n=128)), dma=True)

            for s in range(NSLOT):
                nkt = NKT[s]
                for blk in range(nkt // 4):
                    for j4 in range(4):
                        i = blk * 4 + j4
                        for hd in range(2):
                            off = 64 * hd
                            P.op("tensor", lambda e, i=i, off=off, hd=hd, s=s:
                                 e.matmul(
                                     ps[hd][:],
                                     T[off:off + 64,
                                       k_o + 128 * i: k_o + 128 * i + 128],
                                     T[off:off + 64,
                                       q_o + s * 512: q_o + s * 512 + 512],
                                     start=True, stop=True,
                                     tile_position=(off, 0)))
                            kvidx = s * 32 + i
                            eoff = e1_o + (j4 * 2 + hd) * 512
                            P.op("scalar", lambda e, kvidx=kvidx, eoff=eoff,
                                 hd=hd: e.activation(
                                     T[:, eoff:eoff + 512], ps[hd][:], AF.Exp,
                                     bias=cst["kvec1"][:, kvidx:kvidx + 1],
                                     scale=0.125))
                            if i >= nkt - 8:
                                rel = i - (nkt - 8)
                                P.op("vector", lambda e, rel=rel, eoff=eoff:
                                     e.tensor_tensor(
                                         T[:, eoff:eoff + 512],
                                         T[:, eoff:eoff + 512],
                                         cst["maskstrip"][:, rel * QT:(rel + 1) * QT],
                                         ALU.mult))
                            P.op("tensor", lambda e, eoff=eoff, hd=hd, j4=j4:
                                 e.matmul(
                                     ps[2 + hd][:], cst["onesD"][:],
                                     T[:, eoff:eoff + 512],
                                     start=(j4 == 0), stop=(j4 == 3)))
                    for hd in range(2):
                        P.op("vector", lambda e, hd=hd: e.tensor_scalar(
                            T[:, rd_o + hd * 512: rd_o + hd * 512 + 512],
                            ps[2 + hd][:], 1.0e-30, None, ALU.max))
                        P.op("vector", lambda e, hd=hd: e.reciprocal(
                            T[:, rd_o + hd * 512: rd_o + hd * 512 + 512],
                            T[:, rd_o + hd * 512: rd_o + hd * 512 + 512]))
                        for j4 in range(4):
                            i = blk * 4 + j4
                            eoff = e1_o + (j4 * 2 + hd) * 512
                            P.op("vector", lambda e, eoff=eoff, hd=hd:
                                 e.tensor_tensor(
                                     T[:, pw_o:pw_o + 512], T[:, eoff:eoff + 512],
                                     T[:, rd_o + hd * 512: rd_o + hd * 512 + 512],
                                     ALU.mult))
                            P.op("scalar", lambda e, eoff=eoff: e.activation(
                                T[:, eoff:eoff + 512], T[:, pw_o:pw_o + 512],
                                AF.Exp))
                            vbase = vtm_o + i * 130 + 65 * hd
                            P.op("tensor", lambda e, vbase=vbase, eoff=eoff,
                                 hd=hd, i=i: e.matmul(
                                     ps[4 + hd][0:65, :],
                                     T[:, vbase:vbase + 65],
                                     T[:, eoff:eoff + 512],
                                     start=(i == 0), stop=False))
                for hd in range(2):
                    off = 64 * hd
                    P.op("tensor", lambda e, hd=hd, s=s: e.matmul(
                        ps[4 + hd][0:65, :],
                        T[0:1, sfx_o + s * 130 + 65 * hd:
                          sfx_o + s * 130 + 65 * hd + 65],
                        cst["onesrow"][0:1, :], start=False, stop=True))
                    P.op("vector", lambda e, hd=hd: e.tensor_copy(
                        T[0:1, z_o:z_o + 512], ps[4 + hd][64:65, :]))
                    P.op("tensor", lambda e: e.matmul(
                        ps[6][0:64, :], cst["onesrow"][0:1, 0:64],
                        T[0:1, z_o:z_o + 512], start=True, stop=True))
                    P.op("vector", lambda e: e.reciprocal(
                        T[0:64, rz_o:rz_o + 512], ps[6][0:64, :]))
                    P.op("vector", lambda e, hd=hd, off=off: e.tensor_tensor(
                        T[off:off + 64, ao_o:ao_o + 512], ps[4 + hd][0:64, :],
                        T[0:64, rz_o:rz_o + 512], ALU.mult))
                for oc in range(NCH):
                    P.op("tensor", lambda e, oc=oc: e.matmul(
                        ps[6][:], T[:, wo_o + oc * 128: wo_o + oc * 128 + 128],
                        T[:, ao_o:ao_o + 512], start=True, stop=True))
                    dst = acc_o + oc * 2048 + s * 512
                    P.op("vector", lambda e, dst=dst: e.tensor_tensor(
                        T[:, dst:dst + 512], T[:, dst:dst + 512], ps[6][:],
                        ALU.add))
        al.pop()

        al.push()
        pre_o = al.get(4096)
        xr_o = al.get(4096)
        lnm_o = al.get(512)
        lnt_o = al.get(512)
        lnr_o = al.get(512)
        lnq_o = al.get(512)
        lny_o = al.get(512)

        def ln_tile(pre_off, out_fn):
            for oc in range(NCH):
                src = T[:, pre_off + oc * 512: pre_off + oc * 512 + 512]
                P.op("tensor", lambda e, src=src, oc=oc: e.matmul(
                    ps[6][:], cst["onesM"][:], src,
                    start=(oc == 0), stop=(oc == NCH - 1)))
                P.op("scalar", lambda e, src=src: e.activation(
                    T[:, lnq_o:lnq_o + 512], src, AF.Square))
                P.op("tensor", lambda e, oc=oc: e.matmul(
                    ps[7][:], cst["onesM"][:], T[:, lnq_o:lnq_o + 512],
                    start=(oc == 0), stop=(oc == NCH - 1)))
            P.op("vector", lambda e: e.tensor_copy(T[:, lnm_o:lnm_o + 512], ps[6][:]))
            P.op("vector", lambda e: e.tensor_tensor(
                T[:, lnt_o:lnt_o + 512], ps[6][:], T[:, lnm_o:lnm_o + 512], ALU.mult))
            P.op("vector", lambda e: e.tensor_tensor(
                T[:, lnt_o:lnt_o + 512], ps[7][:], T[:, lnt_o:lnt_o + 512],
                ALU.subtract))
            P.op("vector", lambda e: e.tensor_scalar(
                T[:, lnt_o:lnt_o + 512], T[:, lnt_o:lnt_o + 512], 1.0e-5, None,
                ALU.add))
            P.op("scalar", lambda e: e.activation(
                T[:, lnr_o:lnr_o + 512], T[:, lnt_o:lnt_o + 512], AF.Sqrt))
            P.op("vector", lambda e: e.reciprocal(
                T[:, lnr_o:lnr_o + 512], T[:, lnr_o:lnr_o + 512]))
            for oc in range(NCH):
                src = T[:, pre_off + oc * 512: pre_off + oc * 512 + 512]
                P.op("vector", lambda e, src=src: e.tensor_tensor(
                    T[:, lnq_o:lnq_o + 512], src, T[:, lnm_o:lnm_o + 512],
                    ALU.subtract))
                P.op("vector", lambda e: e.tensor_tensor(
                    T[:, lny_o:lny_o + 512], T[:, lnq_o:lnq_o + 512],
                    T[:, lnr_o:lnr_o + 512], ALU.mult))
                out_fn(oc)

        for s in range(NSLOT):
            P.op("sync", lambda e, s=s: e.dma_start(
                out=sb_slab(T, xr_o, 512), in_=fm_dram_slab(xo, s * 512, 512)),
                dma=True)
            for oc in range(NCH):
                P.op("vector", lambda e, s=s, oc=oc: e.scalar_tensor_tensor(
                    T[:, pre_o + oc * 512: pre_o + oc * 512 + 512],
                    T[:, acc_o + oc * 2048 + s * 512:
                      acc_o + oc * 2048 + s * 512 + 512],
                    bo_sb[:, oc:oc + 1],
                    T[:, xr_o + oc * 512: xr_o + oc * 512 + 512],
                    ALU.add, ALU.add))

            def wr(oc, s=s):
                P.op("gpsimd", lambda e, oc=oc, s=s: e.dma_start(
                    out=t3own.rearrange("(c p) s -> p c s", p=128)[
                        :, oc:oc + 1, s * 512:(s + 1) * 512].squeeze(1),
                    in_=T[:, lny_o:lny_o + 512]), dma=True)
            ln_tile(pre_o, wr)
        al.pop()
        P.finalize(semh)
    return nc



def build_l1c():
    nc = bass.Bass("TRN2", target_bir_lowering=False)
    dram = lambda n, sh, dt=F32, kind="ExternalInput": nc.dram_tensor(
        n, sh, dt, kind=kind).ap()
    xo = dram("xo", [D, NSLOT * QT])
    xe = dram("xe", [D, 8])
    w1T = dram("w1T", [D, I1])
    w2T = dram("w2T", [I1, D])
    convw = dram("convw", [I1, 3])
    convb = dram("convb", [128, 32])
    C = {k: dram("c_" + k, list(v_.shape))
         for k, v_ in _host_consts(0).items() if k in ("ident", "onesM", "onesD")}
    out = dram("out", [D, NSLOT * QT], kind="ExternalOutput")
    w1q_hbm = dram("w1q_hbm", [D, I1], BF16, kind="Internal")
    w2q_hbm = dram("w2q_hbm", [I1, D], BF16, kind="Internal")

    with ExitStack() as ctx:
        SB = lambda n, sh, dt=F32: ctx.enter_context(nc.sbuf_tensor(n, sh, dt))
        T = SB("scr", [128, 36864])
        TB = SB("scrb", [128, 22656], BF16)
        cst = {k: SB("s_" + k, list(v_.shape)) for k, v_ in _host_consts(0).items()
               if k in ("ident", "onesM", "onesD")}
        identb = SB("identb", [128, 128], BF16)
        cw_sb = SB("cw_sb", [128, 32 * 3])
        cb_sb = SB("cb_sb", [128, 32])
        ps = [ctx.enter_context(nc.psum_tensor(f"pb{i}", [128, 512], F32))
              for i in range(8)]
        semh = {s: ctx.enter_context(nc.semaphore(s)) for s in ALLSEMS}
        P = PB(nc)
        al = Alloc(T, 36864)
        alb = Alloc(TB, 22656)

        for k in cst:
            P.op("sync", lambda e, k=k: e.dma_start(out=cst[k][:], in_=C[k]),
                 dma=True)
        P.op("vector", lambda e: e.tensor_copy(identb[:], cst["ident"][:]))
        P.op("sync", lambda e: e.dma_start(
            out=cw_sb[:].rearrange("p (c k) -> p c k", k=3),
            in_=convw.rearrange("(c p) k -> p c k", p=128)), dma=True)
        P.op("sync", lambda e: e.dma_start(
            out=cb_sb[:], in_=convb), dma=True)

        cs_o = al.get(8)
        c1_o = al.get(1)
        c2_o = al.get(1)
        al.push()
        wt_o = al.get(4096)
        wi_o = al.get(4096)
        alb.push()
        wtb_o = alb.get(4096)
        wi32 = ctx.enter_context(nc.sbuf_tensor("wi32", [128, 2048], I32))

        def ternary(wsrc, wdst, rows, cols, c_off):
            nrt = rows // 128
            P.op("gpsimd", lambda e: e.memset(T[:, cs_o:cs_o + 1], 0.0))
            for rt in range(nrt):
                P.op("sync", lambda e, rt=rt: e.dma_start(
                    out=T[:, wt_o:wt_o + cols],
                    in_=wsrc[rt * 128:(rt + 1) * 128, :]), dma=True)
                P.op("scalar", lambda e: e.activation(
                    T[:, wi_o:wi_o + cols], T[:, wt_o:wt_o + cols], AF.Abs))
                P.op("vector", lambda e: e.tensor_reduce(
                    T[:, cs_o + 1:cs_o + 2], T[:, wi_o:wi_o + cols], AX.X,
                    ALU.add))
                P.op("vector", lambda e: e.tensor_tensor(
                    T[:, cs_o:cs_o + 1], T[:, cs_o:cs_o + 1],
                    T[:, cs_o + 1:cs_o + 2], ALU.add))
            P.op("tensor", lambda e: e.matmul(
                ps[7][0:1, 0:1], T[:, cs_o:cs_o + 1], cst["onesD"][:, 0:1],
                start=True, stop=True))
            P.op("vector", lambda e: e.tensor_copy(T[0:1, cs_o + 2:cs_o + 3],
                                                   ps[7][0:1, 0:1]))
            P.op("tensor", lambda e: e.matmul(
                ps[7][0:128, 0:1], cst["onesD"][0:1, :],
                T[0:1, cs_o + 2:cs_o + 3], start=True, stop=True))
            P.op("vector", lambda e, rows=rows, cols=cols: e.tensor_scalar(
                T[:, c_off:c_off + 1], ps[7][0:128, 0:1],
                1.0 / float(rows * cols), 1.0e-5, ALU.mult, ALU.max))
            P.op("vector", lambda e: e.reciprocal(T[:, cs_o + 3:cs_o + 4],
                                                  T[:, c_off:c_off + 1]))
            for rt in range(nrt):
                P.op("sync", lambda e, rt=rt: e.dma_start(
                    out=T[:, wt_o:wt_o + cols],
                    in_=wsrc[rt * 128:(rt + 1) * 128, :]), dma=True)
                P.op("vector", lambda e: e.tensor_scalar(
                    T[:, wi_o:wi_o + cols], T[:, wt_o:wt_o + cols],
                    T[:, cs_o + 3:cs_o + 4], None, ALU.mult))
                for c0 in range(0, cols, 2048):
                    cn = min(2048, cols - c0)
                    P.op("vector", lambda e, c0=c0, cn=cn: e.tensor_copy(
                        wi32[:, 0:cn], T[:, wi_o + c0:wi_o + c0 + cn]))
                    P.op("vector", lambda e, c0=c0, cn=cn: e.tensor_copy(
                        T[:, wi_o + c0:wi_o + c0 + cn], wi32[:, 0:cn]))
                P.op("vector", lambda e: e.tensor_scalar(
                    T[:, wi_o:wi_o + cols], T[:, wi_o:wi_o + cols], 1.0, -1.0,
                    ALU.min, ALU.max))
                P.op("vector", lambda e: e.tensor_copy(
                    TB[:, wtb_o:wtb_o + cols], T[:, wi_o:wi_o + cols]))
                P.op("gpsimd", lambda e, rt=rt: e.dma_start(
                    out=wdst[rt * 128:(rt + 1) * 128, :],
                    in_=TB[:, wtb_o:wtb_o + cols]), dma=True)

        ternary(w1T, w1q_hbm, D, I1, c1_o)
        ternary(w2T, w2q_hbm, I1, D, c2_o)
        al.pop()
        alb.pop()

        te_o = al.get(64)
        etm_o = al.get(1024)
        erow_o = al.get(16)
        xqe_o = alb.get(64)
        qi_o = al.get(4096)
        febc_o = al.get(8)
        P.op("sync", lambda e: e.dma_start(
            out=sb_slab(T, te_o, 8), in_=fm_dram_slab(xe, 0, 8)), dma=True)
        for ch in range(NCH):
            P.op("tensor", lambda e, ch=ch: e.transpose(
                ps[7][0:8, 0:128], T[:, te_o + ch * 8: te_o + ch * 8 + 8],
                cst["ident"][:]))
            P.op("vector", lambda e, ch=ch: e.tensor_copy(
                T[0:8, etm_o + ch * 128: etm_o + ch * 128 + 128],
                ps[7][0:8, 0:128]))
        P.op("vector", lambda e: e.tensor_reduce(
            T[0:8, erow_o:erow_o + 1], T[0:8, etm_o:etm_o + 1024], AX.X,
            ALU.max, apply_absolute_value=True))
        P.op("vector", lambda e: e.tensor_scalar(
            T[0:8, erow_o:erow_o + 1], T[0:8, erow_o:erow_o + 1], 1.0e-5, None,
            ALU.max))
        P.op("vector", lambda e: e.reciprocal(
            T[0:8, erow_o + 1:erow_o + 2], T[0:8, erow_o:erow_o + 1]))
        P.op("vector", lambda e: e.tensor_scalar(
            T[0:8, etm_o:etm_o + 1024], T[0:8, etm_o:etm_o + 1024],
            T[0:8, erow_o + 1:erow_o + 2], 127.0, ALU.mult, ALU.mult))
        ei32 = wi32
        P.op("vector", lambda e: e.tensor_copy(ei32[0:8, 0:1024],
                                               T[0:8, etm_o:etm_o + 1024]))
        P.op("vector", lambda e: e.tensor_copy(
            T[0:8, etm_o:etm_o + 1024], ei32[0:8, 0:1024]))
        for ch in range(NCH):
            P.op("tensor", lambda e, ch=ch: e.transpose(
                ps[7][0:128, 0:8],
                T[0:8, etm_o + ch * 128: etm_o + ch * 128 + 128],
                cst["ident"][0:8, 0:8]))
            P.op("vector", lambda e, ch=ch: e.tensor_copy(
                TB[:, xqe_o + ch * 8: xqe_o + ch * 8 + 8], ps[7][0:128, 0:8]))
        P.op("vector", lambda e: e.tensor_scalar(
            T[0:8, erow_o + 2:erow_o + 3], T[0:8, erow_o:erow_o + 1],
            T[0:8, c1_o:c1_o + 1], 1.0 / 127.0, ALU.mult, ALU.mult))
        P.op("tensor", lambda e: e.transpose(
            ps[7][0:1, 0:8],
            T[0:8, erow_o + 2:erow_o + 3], cst["ident"][0:8, 0:8]))
        P.op("vector", lambda e: e.tensor_copy(T[0:1, erow_o + 4:erow_o + 12],
                                               ps[7][0:1, 0:8]))
        P.op("tensor", lambda e: e.matmul(
            ps[7][0:128, 0:8], cst["onesD"][0:1, :],
            T[0:1, erow_o + 4:erow_o + 12], start=True, stop=True))
        P.op("vector", lambda e: e.tensor_copy(T[:, febc_o:febc_o + 8],
                                               ps[7][0:128, 0:8]))

        xt_o = al.get(4096)
        xtm_o = al.get(1024)
        ctok_o = al.get(16)
        f_o = al.get(4)
        frow_o = al.get(512)
        fbc_o = al.get(512)
        fbc2_o = al.get(512)
        stm_o = al.get(4 * 4096)
        hx_o = al.get(514)
        pre_o = al.get(4096)
        tmp_o = al.get(512)
        lnm_o = al.get(512)
        lnt_o = al.get(512)
        lnr_o = al.get(512)
        lnq_o = al.get(512)
        lny_o = al.get(512)
        xqs_o = alb.get(4096)
        w1t_o = alb.get(1024)
        w2t_o = alb.get(1024)
        xq2_o = alb.get(32 * 512)
        assert al.off <= 36864, al.off
        assert alb.off <= 22656, alb.off

        def int8_quant_tm(src_off, qi_off, n, sub, rbuf_off, c_scale_off):
            P.op("vector", lambda e: e.tensor_reduce(
                T[:, ctok_o:ctok_o + 1], T[:, src_off:src_off + n], AX.X,
                ALU.max, apply_absolute_value=True))
            P.op("vector", lambda e: e.tensor_scalar(
                T[:, ctok_o:ctok_o + 1], T[:, ctok_o:ctok_o + 1], 1.0e-5, None,
                ALU.max))
            P.op("vector", lambda e: e.reciprocal(
                T[:, ctok_o + 1:ctok_o + 2], T[:, ctok_o:ctok_o + 1]))
            P.op("vector", lambda e, n=n: e.tensor_scalar(
                T[:, rbuf_off:rbuf_off + n], T[:, src_off:src_off + n],
                T[:, ctok_o + 1:ctok_o + 2], 127.0, ALU.mult, ALU.mult))
            for c0 in range(0, n, 2048):
                cn = min(2048, n - c0)
                P.op("vector", lambda e, c0=c0, cn=cn: e.tensor_copy(
                    wi32[:, 0:cn], T[:, rbuf_off + c0:rbuf_off + c0 + cn]))
                P.op("vector", lambda e, c0=c0, cn=cn: e.tensor_copy(
                    T[:, qi_off + c0:qi_off + c0 + cn], wi32[:, 0:cn]))
            P.op("vector", lambda e: e.tensor_scalar(
                T[:, f_o + sub:f_o + sub + 1], T[:, ctok_o:ctok_o + 1],
                T[:, c_scale_off:c_scale_off + 1], 1.0 / 127.0,
                ALU.mult, ALU.mult))

        def factor_bcast(dst_off):
            for sub in range(4):
                P.op("tensor", lambda e, sub=sub: e.transpose(
                    ps[7][0:1, 0:128],
                    T[:, f_o + sub:f_o + sub + 1], cst["ident"][:]))
                P.op("vector", lambda e, sub=sub: e.tensor_copy(
                    T[0:1, frow_o + sub * 128: frow_o + sub * 128 + 128],
                    ps[7][0:1, 0:128]))
            P.op("tensor", lambda e: e.matmul(
                ps[7][0:128, 0:512], cst["onesD"][0:1, :],
                T[0:1, frow_o:frow_o + 512], start=True, stop=True))
            P.op("vector", lambda e, dst_off=dst_off: e.tensor_copy(
                T[:, dst_off:dst_off + 512], ps[7][0:128, 0:512]))

        for s in range(NSLOT):
            P.op("sync", lambda e, s=s: e.dma_start(
                out=sb_slab(T, xt_o, 512), in_=fm_dram_slab(xo, s * 512, 512)),
                dma=True)
            for sub in range(4):
                for ch in range(NCH):
                    P.op("tensor", lambda e, ch=ch, sub=sub: e.transpose(
                        ps[7][:, 0:128],
                        T[:, xt_o + ch * 512 + sub * 128:
                          xt_o + ch * 512 + sub * 128 + 128],
                        cst["ident"][:]))
                    P.op("vector", lambda e, ch=ch: e.tensor_copy(
                        T[:, xtm_o + ch * 128: xtm_o + ch * 128 + 128],
                        ps[7][:, 0:128]))
                int8_quant_tm(xtm_o, qi_o, 1024, sub, pre_o, c1_o)
                for ch in range(NCH):
                    P.op("tensor", lambda e, ch=ch: e.transpose(
                        ps[7][:, 0:128],
                        T[:, qi_o + ch * 128: qi_o + ch * 128 + 128],
                        cst["ident"][:]))
                    P.op("vector", lambda e, ch=ch, sub=sub: e.tensor_copy(
                        TB[:, xqs_o + ch * 512 + sub * 128:
                           xqs_o + ch * 512 + sub * 128 + 128],
                        ps[7][:, 0:128]))
            factor_bcast(fbc_o)
            for ich in range(32):
                P.op("sync", lambda e, ich=ich: e.dma_start(
                    out=sb_slab(TB, w1t_o, 128),
                    in_=fm_dram_slab(w1q_hbm, ich * 128, 128)), dma=True)
                for ch in range(NCH):
                    P.op("tensor", lambda e, ch=ch: e.matmul(
                        ps[0][:], TB[:, w1t_o + ch * 128: w1t_o + ch * 128 + 128],
                        TB[:, xqs_o + ch * 512: xqs_o + ch * 512 + 512],
                        start=(ch == 0), stop=(ch == NCH - 1)))
                for ch in range(NCH):
                    P.op("tensor", lambda e, ch=ch, s=s: e.matmul(
                        ps[1][:, 0:2],
                        TB[:, w1t_o + ch * 128: w1t_o + ch * 128 + 128],
                        TB[:, xqe_o + ch * 8 + 2 * s: xqe_o + ch * 8 + 2 * s + 2],
                        start=(ch == 0), stop=(ch == NCH - 1)))
                P.op("vector", lambda e: e.tensor_tensor(
                    T[:, hx_o + 1:hx_o + 513], ps[0][:], T[:, fbc_o:fbc_o + 512],
                    ALU.mult))
                P.op("vector", lambda e, s=s: e.tensor_scalar(
                    T[:, hx_o:hx_o + 1], ps[1][:, 0:1],
                    T[:, febc_o + 2 * s:febc_o + 2 * s + 1], None, ALU.mult))
                P.op("vector", lambda e, s=s: e.tensor_scalar(
                    T[:, hx_o + 513:hx_o + 514], ps[1][:, 1:2],
                    T[:, febc_o + 2 * s + 1:febc_o + 2 * s + 2], None, ALU.mult))
                P.op("vector", lambda e, ich=ich: e.tensor_scalar(
                    T[:, tmp_o:tmp_o + 512], T[:, hx_o:hx_o + 512],
                    cw_sb[:, ich * 3:ich * 3 + 1], cb_sb[:, ich:ich + 1],
                    ALU.mult, ALU.add))
                P.op("vector", lambda e, ich=ich: e.scalar_tensor_tensor(
                    T[:, tmp_o:tmp_o + 512], T[:, hx_o + 1:hx_o + 513],
                    cw_sb[:, ich * 3 + 1:ich * 3 + 2], T[:, tmp_o:tmp_o + 512],
                    ALU.mult, ALU.add))
                P.op("vector", lambda e, ich=ich: e.scalar_tensor_tensor(
                    T[:, tmp_o:tmp_o + 512], T[:, hx_o + 2:hx_o + 514],
                    cw_sb[:, ich * 3 + 2:ich * 3 + 3], T[:, tmp_o:tmp_o + 512],
                    ALU.mult, ALU.add))
                P.op("scalar", lambda e: e.activation(
                    T[:, tmp_o:tmp_o + 512], T[:, tmp_o:tmp_o + 512], AF.Silu))
                for sub in range(4):
                    P.op("tensor", lambda e, sub=sub: e.transpose(
                        ps[7][:, 0:128],
                        T[:, tmp_o + sub * 128: tmp_o + sub * 128 + 128],
                        cst["ident"][:]))
                    P.op("vector", lambda e, sub=sub, ich=ich: e.tensor_copy(
                        T[:, stm_o + sub * 4096 + ich * 128:
                          stm_o + sub * 4096 + ich * 128 + 128],
                        ps[7][:, 0:128]))
            for sub in range(4):
                int8_quant_tm(stm_o + sub * 4096, qi_o, 4096, sub, pre_o, c2_o)
                for ich in range(32):
                    P.op("tensor", lambda e, ich=ich: e.transpose(
                        ps[7][:, 0:128],
                        T[:, qi_o + ich * 128: qi_o + ich * 128 + 128],
                        cst["ident"][:]))
                    P.op("vector", lambda e, ich=ich, sub=sub: e.tensor_copy(
                        TB[:, xq2_o + ich * 512 + sub * 128:
                           xq2_o + ich * 512 + sub * 128 + 128],
                        ps[7][:, 0:128]))
            factor_bcast(fbc2_o)
            for ich in range(32):
                P.op("sync", lambda e, ich=ich: e.dma_start(
                    out=TB[:, w2t_o:w2t_o + 1024],
                    in_=w2q_hbm[ich * 128:(ich + 1) * 128, :]), dma=True)
                for hc in range(NCH):
                    P.op("tensor", lambda e, ich=ich, hc=hc: e.matmul(
                        ps[hc][:],
                        TB[:, w2t_o + hc * 128: w2t_o + hc * 128 + 128],
                        TB[:, xq2_o + ich * 512: xq2_o + ich * 512 + 512],
                        start=(ich == 0), stop=(ich == 31)))
            for hc in range(NCH):
                P.op("vector", lambda e, hc=hc: e.tensor_tensor(
                    T[:, tmp_o:tmp_o + 512], ps[hc][:], T[:, fbc2_o:fbc2_o + 512],
                    ALU.mult))
                P.op("vector", lambda e, hc=hc: e.tensor_tensor(
                    T[:, pre_o + hc * 512: pre_o + hc * 512 + 512],
                    T[:, tmp_o:tmp_o + 512],
                    T[:, xt_o + hc * 512: xt_o + hc * 512 + 512], ALU.add))
            for oc in range(NCH):
                src = T[:, pre_o + oc * 512: pre_o + oc * 512 + 512]
                P.op("tensor", lambda e, src=src, oc=oc: e.matmul(
                    ps[6][:], cst["onesM"][:], src,
                    start=(oc == 0), stop=(oc == NCH - 1)))
                P.op("scalar", lambda e, src=src: e.activation(
                    T[:, lnq_o:lnq_o + 512], src, AF.Square))
                P.op("tensor", lambda e, oc=oc: e.matmul(
                    ps[7][:], cst["onesM"][:], T[:, lnq_o:lnq_o + 512],
                    start=(oc == 0), stop=(oc == NCH - 1)))
            P.op("vector", lambda e: e.tensor_copy(T[:, lnm_o:lnm_o + 512], ps[6][:]))
            P.op("vector", lambda e: e.tensor_tensor(
                T[:, lnt_o:lnt_o + 512], ps[6][:], T[:, lnm_o:lnm_o + 512],
                ALU.mult))
            P.op("vector", lambda e: e.tensor_tensor(
                T[:, lnt_o:lnt_o + 512], ps[7][:], T[:, lnt_o:lnt_o + 512],
                ALU.subtract))
            P.op("vector", lambda e: e.tensor_scalar(
                T[:, lnt_o:lnt_o + 512], T[:, lnt_o:lnt_o + 512], 1.0e-5, None,
                ALU.add))
            P.op("scalar", lambda e: e.activation(
                T[:, lnr_o:lnr_o + 512], T[:, lnt_o:lnt_o + 512], AF.Sqrt))
            P.op("vector", lambda e: e.reciprocal(
                T[:, lnr_o:lnr_o + 512], T[:, lnr_o:lnr_o + 512]))
            for oc in range(NCH):
                src = T[:, pre_o + oc * 512: pre_o + oc * 512 + 512]
                P.op("vector", lambda e, src=src: e.tensor_tensor(
                    T[:, lnq_o:lnq_o + 512], src, T[:, lnm_o:lnm_o + 512],
                    ALU.subtract))
                P.op("vector", lambda e: e.tensor_tensor(
                    T[:, lny_o:lny_o + 512], T[:, lnq_o:lnq_o + 512],
                    T[:, lnr_o:lnr_o + 512], ALU.mult))
                P.op("gpsimd", lambda e, oc=oc, s=s: e.dma_start(
                    out=out.rearrange("(c p) s -> p c s", p=128)[
                        :, oc:oc + 1, s * 512:(s + 1) * 512].squeeze(1),
                    in_=T[:, lny_o:lny_o + 512]), dma=True)
        P.finalize(semh)
    return nc


_PROGS = {}


def _get_prog(name, builder):
    if name not in _PROGS:
        _PROGS[name] = builder()
    return _PROGS[name]


def kernel(**inp):
    f32 = lambda a: np.ascontiguousarray(a, dtype=np.float32)
    cols = {h: _own_cols(h) for h in (0, 1)}
    consts = {h: _host_consts(h) for h in (0, 1)}
    xs = [f32(inp["L0_attn_x"][b].T) for b in range(B)]

    maps = []
    for c in range(8):
        b, h = c % 4, c // 4
        m = {("c_" + k): v for k, v in consts[h].items()}
        m["c_selE"] = _host_selE(h)
        m.update(dict(
            xg=xs[b], xo=np.ascontiguousarray(xs[b][:, cols[h]]),
            w_in=f32(inp["L0_attn_w_in"]),
            b_in=f32(inp["L0_attn_b_in"]).reshape(3 * D, 1),
            w_out=f32(inp["L0_attn_w_out"]),
            b_out=f32(inp["L0_attn_b_out"]).reshape(NCH, 128).T.copy(),
            w_gate=f32(inp["L0_ffn_w_gate"]), w_up=f32(inp["L0_ffn_w_up"]),
            w_down=f32(inp["L0_ffn_w_down"]),
            wqT=f32(inp["L1_attn_Wq_w"].T), wkT=f32(inp["L1_attn_Wk_w"].T),
            wvT=f32(inp["L1_attn_Wv_w"].T), woT=f32(inp["L1_attn_Wo_w"].T),
            bq=f32(inp["L1_attn_Wq_b"]).reshape(D, 1),
            bk=f32(inp["L1_attn_Wk_b"]).reshape(D, 1),
            bv=f32(inp["L1_attn_Wv_b"]).reshape(D, 1),
            bo=f32(inp["L1_attn_Wo_b"]).reshape(NCH, 128).T.copy(),
            w1T=f32(inp["L1_ffn_w1"].T), w2T=f32(inp["L1_ffn_w2"].T),
            convw=f32(inp["L1_ffn_conv_w"]).reshape(I1, 3),
            convb=f32(inp["L1_ffn_conv_b"]).reshape(32, 128).T.copy()))
        maps.append(m)

    nc = _get_prog("fused", build_fused)
    r = run_bass_kernel_spmd(nc, maps, core_ids=list(range(8)))

    dt0 = np.asarray(r.results[0]["out"]).dtype
    outp = np.zeros((B, S, D), dtype=dt0)
    for c in range(8):
        b, h = c % 4, c // 4
        outp[b][cols[h], :] = np.asarray(r.results[c]["out"]).T
    return outp.astype(np.float32) if dt0 == np.float32 else outp


CC_GROUPS = [[0, 4], [1, 5], [2, 6], [3, 7]]


def _host_selE(half):
    selE = np.zeros((1, 8 * 16), dtype=np.float32)
    for s in range(NSLOT):
        g = _own_qtiles(half)[s]
        if g - 1 >= 0:
            blk = 0 if ((g - 1) % 2 == 1) else 1
            s2 = (g - 1) // 2
            selE[0, (2 * s) * 16 + blk * 8 + 2 * s2 + 1] = 1.0
        if g + 1 < S // QT:
            blk = 0 if ((g + 1) % 2 == 1) else 1
            s2 = (g + 1) // 2
            selE[0, (2 * s + 1) * 16 + blk * 8 + 2 * s2] = 1.0
    return np.tile(selE, (128, 1))


def build_fused():
    global _DEP
    _PBMODE[0] = "record"
    _REC.clear()
    _DEP = {}
    _build_fused_once()
    _DEP = _compute_deps(_REC)
    _PBMODE[0] = "dep"
    try:
        nc = _build_fused_once()
    finally:
        _PBMODE[0] = "strict"
    return nc


def _build_fused_once():
    nc = bass.Bass("TRN2", target_bir_lowering=False)
    dram = lambda n, sh, dt=F32, kind="ExternalInput": nc.dram_tensor(
        n, sh, dt, kind=kind).ap()
    xg = dram("xg", [D, S])
    xo = dram("xo", [D, NSLOT * QT])
    w_in = dram("w_in", [D, 3 * D])
    b_in = dram("b_in", [3 * D, 1])
    w_out = dram("w_out", [D, D])
    b_out = dram("b_out", [128, NCH])
    w_gate = dram("w_gate", [I0, D])
    w_up = dram("w_up", [I0, D])
    w_down = dram("w_down", [D, I0])
    wqT = dram("wqT", [D, D])
    wkT = dram("wkT", [D, D])
    wvT = dram("wvT", [D, D])
    woT = dram("woT", [D, D])
    bq = dram("bq", [D, 1])
    bk = dram("bk", [D, 1])
    bv = dram("bv", [D, 1])
    bo = dram("bo", [128, NCH])
    w1T = dram("w1T", [D, I1])
    w2T = dram("w2T", [I1, D])
    convw = dram("convw", [I1, 3])
    convb = dram("convb", [128, 32])
    CC0 = _host_consts(0)
    RES_CONSTS = ("maskstrip", "kvec1", "ident", "onesM", "onesD", "onesrow",
                  "sel")
    C = {k: dram("c_" + k, list(CC0[k].shape)) for k in RES_CONSTS}
    C["kvec0"] = dram("c_kvec0", list(CC0["kvec0"].shape))
    C["qbias"] = dram("c_qbias", list(CC0["qbias"].shape))
    C["selE"] = dram("c_selE", [128, 128])
    out = dram("out", [D, NSLOT * QT], kind="ExternalOutput")
    IK = "ExternalOutput" if DEBUG else "Internal"
    yf_hbm = dram("yf_hbm", [D, NSLOT * QT], kind=IK)
    ao_hbm = dram("ao_hbm", [D, NSLOT * QT], kind=IK)
    t2o_hbm = dram("t2o_hbm", [D, NSLOT * QT], kind=IK)
    t3o_hbm = dram("t3o_hbm", [D, NSLOT * QT], kind=IK)
    cc_in = dram("cc_in", [D, NSLOT * QT], kind="Internal")
    cc_out = dram("cc_out", [2 * D, NSLOT * QT], kind="Internal")
    ccE_in = dram("ccE_in", [D, 8], kind="Internal")
    ccE_out = dram("ccE_out", [2 * D, 8], kind="Internal")
    wgqT = dram("wgqT", [D, I0], kind="Internal")
    wuqT = dram("wuqT", [D, I0], kind="Internal")
    wdqT = dram("wdqT", [I0, D], kind="Internal")
    w1q_hbm = dram("w1q_hbm", [D, I1], BF16, kind="Internal")
    w2q_hbm = dram("w2q_hbm", [I1, D], BF16, kind="Internal")

    with ExitStack() as ctx:
        SB = lambda n, sh, dt=F32: ctx.enter_context(nc.sbuf_tensor(n, sh, dt))
        T = SB("scr", [128, 33536])
        TB = SB("scrb", [128, 22656], BF16)
        wi32 = ctx.enter_context(nc.sbuf_tensor("wi32", [128, 1024], I32))
        cst = {k: SB("s_" + k, list(CC0[k].shape)) for k in RES_CONSTS}
        cst["selE"] = SB("s_selE", [128, 128])
        bout_sb = SB("bout", [128, NCH])
        bo_sb = SB("bo_sb", [128, NCH])
        cw_sb = SB("cw_sb", [128, 32 * 3])
        cb_sb = SB("cb_sb", [128, 32])
        ps = [ctx.enter_context(nc.psum_tensor(f"pb{i}", [128, 512], F32))
              for i in range(8)]
        semh = {s: ctx.enter_context(nc.semaphore(s)) for s in ALLSEMS}
        P = PB(nc)
        al = Alloc(T, 33536)
        alb = Alloc(TB, 22656)

        for k in RES_CONSTS:
            P.op("sync", lambda e, k=k: e.dma_start(out=cst[k][:], in_=C[k]),
                 dma=True)
        P.op("sync", lambda e: e.dma_start(out=cst["selE"][:], in_=C["selE"]),
             dma=True)
        P.op("sync", lambda e: e.dma_start(out=bout_sb[:], in_=b_out), dma=True)
        P.op("sync", lambda e: e.dma_start(out=bo_sb[:], in_=bo), dma=True)
        P.op("sync", lambda e: e.dma_start(
            out=cw_sb[:].rearrange("p (c k) -> p c k", k=3),
            in_=convw.rearrange("(c p) k -> p c k", p=128)), dma=True)
        P.op("sync", lambda e: e.dma_start(out=cb_sb[:], in_=convb), dma=True)

        att = {}

        def alloc_attn(l0=True):
            att["k"] = al.get(4096)
            att["q"] = al.get(2048)
            att["vtm"] = al.get(32 * 130)
            att["xt"] = al.get(8192)
            att["w3"] = al.get(1024)
            att["b3"] = al.get(4)
            att["vf"] = al.get(1024)
            att["ao"] = al.get(512)
            att["z"] = al.get(512)
            att["rz"] = al.get(512)
            if l0:
                att["et"] = al.get(4 * 512)
                att["etmp"] = al.get(1024)
                att["kv0"] = al.get(256)
                att["qb"] = al.get(1024)

        def memset_vtm():
            P.op("gpsimd", lambda e: e.memset(
                T[:, att["vtm"]:att["vtm"] + 32 * 130].rearrange(
                    "p (t c) -> p t c", c=130)[:, :, 64:65], 1.0))
            P.op("gpsimd", lambda e: e.memset(
                T[:, att["vtm"]:att["vtm"] + 32 * 130].rearrange(
                    "p (t c) -> p t c", c=130)[:, :, 129:130], 1.0))

        def production(wsrc, col0, bsrc, brow0, bslot, src_fn, ntt, dst, kind):
            w3, b3, vf, vtm = (att["w3"], att["b3"], att["vf"], att["vtm"])
            P.op("sync", lambda e: e.dma_start(
                out=sb_slab(T, w3, 128), in_=fm_dram_slab(wsrc, col0, 128)),
                dma=True)
            P.op("sync", lambda e: e.dma_start(
                out=T[:, b3 + bslot:b3 + bslot + 1],
                in_=bsrc[brow0:brow0 + 128, 0:1]), dma=True)
            for tt in range(ntt):
                xt = att["xt"] + 4096 * (tt % 2)
                pp = ps[0] if tt % 2 == 0 else ps[5]
                P.op("sync", lambda e, tt=tt, xt=xt: e.dma_start(
                    out=sb_slab(T, xt, 512), in_=src_fn(tt)), dma=True)
                for ch in range(NCH):
                    P.op("tensor", lambda e, ch=ch, xt=xt, pp=pp: e.matmul(
                        pp[:], T[:, w3 + ch * 128: w3 + ch * 128 + 128],
                        T[:, xt + ch * 512: xt + ch * 512 + 512],
                        start=(ch == 0), stop=(ch == NCH - 1)))
                if kind == "fm":
                    P.op("vector", lambda e, tt=tt, pp=pp: e.tensor_scalar(
                        T[:, dst + tt * 512: dst + tt * 512 + 512], pp[:],
                        T[:, b3 + bslot:b3 + bslot + 1], None, ALU.add))
                else:
                    vfo = vf + 512 * (tt % 2)
                    P.op("vector", lambda e, pp=pp, vfo=vfo: e.tensor_scalar(
                        T[:, vfo:vfo + 512], pp[:],
                        T[:, b3 + bslot:b3 + bslot + 1], None, ALU.add))
                    for j in range(4):
                        tp = ps[6] if j % 2 == 0 else ps[7]
                        P.op("tensor", lambda e, j=j, vfo=vfo, tp=tp: e.transpose(
                            tp[:, 0:128],
                            T[:, vfo + j * 128: vfo + j * 128 + 128],
                            cst["ident"][:]))
                        base = vtm + (tt * 4 + j) * 130
                        P.op("vector", lambda e, base=base, tp=tp: e.tensor_copy(
                            T[:, base:base + 64], tp[:, 0:64]))
                        P.op("vector", lambda e, base=base, tp=tp: e.tensor_copy(
                            T[:, base + 65:base + 129], tp[:, 64:128]))

        def attn_l0(p, filler=None):
            k_o, q_o, vtm_o = att["k"], att["q"], att["vtm"]
            et_o, etmp_o, ao_o, z_o, rz_o = (att["et"], att["etmp"], att["ao"],
                                             att["z"], att["rz"])
            kv0, qb = att["kv0"], att["qb"]
            SBANK = {(0, 0): ps[0], (0, 1): ps[6], (1, 0): ps[1], (1, 1): ps[7]}
            P.op("sync", lambda e, p=p: e.dma_start(
                out=T[:, kv0:kv0 + 256],
                in_=C["kvec0"][:, 2 * p * NSLOT * 32:(2 * p + 2) * NSLOT * 32]),
                dma=True)
            if 2 * p < ASPL:
                n_qb = min(ASPL - 2 * p, 2)
                P.op("sync", lambda e, p=p, n_qb=n_qb: e.dma_start(
                    out=T[:, qb:qb + n_qb * 512],
                    in_=C["qbias"][:, 2 * p * QT:(2 * p + n_qb) * QT]), dma=True)
            for s in range(NSLOT):
                nkt = NKT[s]

                def emit_av(i):
                    for hd in range(2):
                        pav = ps[2 + hd]
                        eo = et_o + (hd * 2 + i % 2) * 512
                        vbase = vtm_o + i * 130 + 65 * hd
                        P.op("tensor", lambda e, vbase=vbase, eo=eo, pav=pav,
                             i=i, nkt=nkt: e.matmul(
                                 pav[0:65, :], T[:, vbase:vbase + 65],
                                 T[:, eo:eo + 512],
                                 start=(i == 0), stop=(i == nkt - 1)))

                for i in range(nkt):
                    for hd in range(2):
                        off = 64 * hd
                        pss = SBANK[(hd, i % 2)]
                        P.op("tensor", lambda e, i=i, off=off, pss=pss, s=s:
                             e.matmul(
                                 pss[:],
                                 T[off:off + 64, k_o + 128 * i: k_o + 128 * i + 128],
                                 T[off:off + 64, q_o + s * 512: q_o + s * 512 + 512],
                                 start=True, stop=True, tile_position=(off, 0)))
                    if i > 0:
                        emit_av(i - 1)
                    for hd in range(2):
                        gh = 2 * p + hd
                        pss = SBANK[(hd, i % 2)]
                        eo = et_o + (hd * 2 + i % 2) * 512
                        em = etmp_o + hd * 512
                        kvidx = kv0 + (hd * NSLOT + s) * 32 + i
                        if gh < ASPL:
                            P.op("vector", lambda e, pss=pss, hd=hd, em=em:
                                 e.scalar_tensor_tensor(
                                     T[:, em:em + 512], pss[:], 0.125,
                                     T[:, qb + hd * QT: qb + (hd + 1) * QT],
                                     ALU.mult, ALU.add))
                            P.op("vector", lambda e, kvidx=kvidx, em=em:
                                 e.tensor_scalar(
                                     T[:, em:em + 512], T[:, em:em + 512],
                                     T[:, kvidx:kvidx + 1], 30.0,
                                     ALU.add, ALU.min))
                            P.op("scalar", lambda e, eo=eo, em=em: e.activation(
                                T[:, eo:eo + 512], T[:, em:em + 512], AF.Exp))
                        else:
                            P.op("scalar", lambda e, kvidx=kvidx, eo=eo, pss=pss:
                                 e.activation(
                                     T[:, eo:eo + 512], pss[:], AF.Exp,
                                     bias=T[:, kvidx:kvidx + 1], scale=0.125))
                        if i >= nkt - 8:
                            rel = i - (nkt - 8)
                            P.op("vector", lambda e, rel=rel, eo=eo:
                                 e.tensor_tensor(
                                     T[:, eo:eo + 512], T[:, eo:eo + 512],
                                     cst["maskstrip"][:, rel * QT:(rel + 1) * QT],
                                     ALU.mult))
                emit_av(nkt - 1)
                for hd in range(2):
                    off = 64 * hd
                    pav = ps[2 + hd]
                    P.op("vector", lambda e, pav=pav: e.tensor_copy(
                        T[0:1, z_o:z_o + 512], pav[64:65, :]))
                    P.op("tensor", lambda e: e.matmul(
                        ps[6][0:64, :], cst["onesrow"][0:1, 0:64],
                        T[0:1, z_o:z_o + 512], start=True, stop=True))
                    P.op("vector", lambda e: e.reciprocal(
                        T[0:64, rz_o:rz_o + 512], ps[6][0:64, :]))
                    P.op("vector", lambda e, pav=pav, off=off: e.tensor_tensor(
                        T[off:off + 64, ao_o:ao_o + 512], pav[0:64, :],
                        T[0:64, rz_o:rz_o + 512], ALU.mult))
                P.op("gpsimd", lambda e, p=p, s=s: e.dma_start(
                    out=ao_hbm[128 * p:128 * p + 128, s * 512:(s + 1) * 512],
                    in_=T[:, att["ao"]:att["ao"] + 512]), dma=True)
                if filler is not None:
                    filler()

        def attn_l1(p):
            k_o, q_o, vtm_o = att["k"], att["q"], att["vtm"]
            et_o = att["e1"]
            pw_o, rd_o, ao_o, z_o, rz_o = (att["pw"], att["rd"], att["ao"],
                                           att["z"], att["rz"])
            bv_o, sfx_o = att["bv"], att["sfx"]
            SBANK = {(0, 0): ps[0], (0, 1): ps[6], (1, 0): ps[1], (1, 1): ps[7]}
            for blk in range(8):
                for t4 in range(4):
                    P.op("tensor", lambda e, blk=blk, t4=t4: e.matmul(
                        ps[6][0:1, 0:130], cst["onesD"][:, 0:1],
                        T[:, vtm_o + (blk * 4 + t4) * 130:
                          vtm_o + (blk * 4 + t4) * 130 + 130],
                        start=(t4 == 0), stop=(t4 == 3)))
                P.op("vector", lambda e, blk=blk: e.tensor_copy(
                    T[0:1, bv_o + blk * 130: bv_o + blk * 130 + 130],
                    ps[6][0:1, 0:130]))
            for s in range(NSLOT):
                for b in range(8):
                    sap = cst["sel"][0:1, s * 8 + b:s * 8 + b + 1]
                    if b == 0:
                        P.op("vector", lambda e, s=s, sap=sap, b=b:
                             e.tensor_scalar(
                                 T[0:1, sfx_o + s * 130: sfx_o + s * 130 + 130],
                                 T[0:1, bv_o + b * 130: bv_o + b * 130 + 130],
                                 sap, None, ALU.mult))
                    else:
                        P.op("vector", lambda e, s=s, sap=sap, b=b:
                             e.scalar_tensor_tensor(
                                 T[0:1, sfx_o + s * 130: sfx_o + s * 130 + 130],
                                 T[0:1, bv_o + b * 130: bv_o + b * 130 + 130],
                                 sap,
                                 T[0:1, sfx_o + s * 130: sfx_o + s * 130 + 130],
                                 ALU.mult, ALU.add))
            for s in range(NSLOT):
                nkt = NKT[s]
                def emit_d(j4):
                    for hd in range(2):
                        eoff = et_o + (j4 * 2 + hd) * 512
                        P.op("tensor", lambda e, eoff=eoff, hd=hd, j4=j4:
                             e.matmul(
                                 ps[2 + hd][:], cst["onesD"][:],
                                 T[:, eoff:eoff + 512],
                                 start=(j4 == 0), stop=(j4 == 3)))

                def emit_av1(j4, blk):
                    i = blk * 4 + j4
                    for hd in range(2):
                        eoff = et_o + (j4 * 2 + hd) * 512
                        vbase = vtm_o + i * 130 + 65 * hd
                        P.op("tensor", lambda e, vbase=vbase, eoff=eoff,
                             hd=hd, i=i: e.matmul(
                                 ps[4 + hd][0:65, :],
                                 T[:, vbase:vbase + 65],
                                 T[:, eoff:eoff + 512],
                                 start=(i == 0), stop=False))

                for blk in range(nkt // 4):
                    for j4 in range(4):
                        i = blk * 4 + j4
                        for hd in range(2):
                            off = 64 * hd
                            pss = SBANK[(hd, j4 % 2)]
                            P.op("tensor", lambda e, i=i, off=off, s=s, pss=pss:
                                 e.matmul(
                                     pss[:],
                                     T[off:off + 64,
                                       k_o + 128 * i: k_o + 128 * i + 128],
                                     T[off:off + 64,
                                       q_o + s * 512: q_o + s * 512 + 512],
                                     start=True, stop=True,
                                     tile_position=(off, 0)))
                        if j4 > 0:
                            emit_d(j4 - 1)
                        for hd in range(2):
                            off = 64 * hd
                            pss = SBANK[(hd, j4 % 2)]
                            kvidx = s * 32 + i
                            eoff = et_o + (j4 * 2 + hd) * 512
                            P.op("scalar", lambda e, kvidx=kvidx, eoff=eoff,
                                 pss=pss: e.activation(
                                     T[:, eoff:eoff + 512], pss[:], AF.Exp,
                                     bias=cst["kvec1"][:, kvidx:kvidx + 1],
                                     scale=0.125))
                            if i >= nkt - 8:
                                rel = i - (nkt - 8)
                                P.op("vector", lambda e, rel=rel, eoff=eoff:
                                     e.tensor_tensor(
                                         T[:, eoff:eoff + 512],
                                         T[:, eoff:eoff + 512],
                                         cst["maskstrip"][:, rel * QT:(rel + 1) * QT],
                                         ALU.mult))
                    emit_d(3)
                    for hd in range(2):
                        P.op("vector", lambda e, hd=hd: e.tensor_scalar(
                            T[:, rd_o + hd * 512: rd_o + hd * 512 + 512],
                            ps[2 + hd][:], 1.0e-30, None, ALU.max))
                        P.op("vector", lambda e, hd=hd: e.reciprocal(
                            T[:, rd_o + hd * 512: rd_o + hd * 512 + 512],
                            T[:, rd_o + hd * 512: rd_o + hd * 512 + 512]))
                    for j4 in range(4):
                        for hd in range(2):
                            eoff = et_o + (j4 * 2 + hd) * 512
                            pw = pw_o + hd * 512
                            P.op("vector", lambda e, eoff=eoff, hd=hd, pw=pw:
                                 e.tensor_tensor(
                                     T[:, pw:pw + 512], T[:, eoff:eoff + 512],
                                     T[:, rd_o + hd * 512: rd_o + hd * 512 + 512],
                                     ALU.mult))
                            P.op("scalar", lambda e, eoff=eoff, pw=pw:
                                 e.activation(
                                     T[:, eoff:eoff + 512], T[:, pw:pw + 512],
                                     AF.Exp))
                        if j4 > 0:
                            emit_av1(j4 - 1, blk)
                    emit_av1(3, blk)
                for hd in range(2):
                    off = 64 * hd
                    P.op("tensor", lambda e, hd=hd, s=s: e.matmul(
                        ps[4 + hd][0:65, :],
                        T[0:1, sfx_o + s * 130 + 65 * hd:
                          sfx_o + s * 130 + 65 * hd + 65],
                        cst["onesrow"][0:1, :], start=False, stop=True))
                    P.op("vector", lambda e, hd=hd: e.tensor_copy(
                        T[0:1, z_o:z_o + 512], ps[4 + hd][64:65, :]))
                    P.op("tensor", lambda e: e.matmul(
                        ps[6][0:64, :], cst["onesrow"][0:1, 0:64],
                        T[0:1, z_o:z_o + 512], start=True, stop=True))
                    P.op("vector", lambda e: e.reciprocal(
                        T[0:64, rz_o:rz_o + 512], ps[6][0:64, :]))
                    P.op("vector", lambda e, hd=hd, off=off: e.tensor_tensor(
                        T[off:off + 64, ao_o:ao_o + 512], ps[4 + hd][0:64, :],
                        T[0:64, rz_o:rz_o + 512], ALU.mult))
                P.op("gpsimd", lambda e, p=p, s=s: e.dma_start(
                    out=ao_hbm[128 * p:128 * p + 128, s * 512:(s + 1) * 512],
                    in_=T[:, att["ao"]:att["ao"] + 512]), dma=True)

        P._fused = dict(att=att, alloc_attn=alloc_attn, memset_vtm=memset_vtm,
                        production=production, attn_l0=attn_l0, attn_l1=attn_l1)
        _fused_part2(nc, P, al, alb, T, TB, wi32, cst, ps, semh, locals())
    return nc


def _fused_part2(nc, P, al, alb, T, TB, wi32, cst, ps, semh, env):
    att = P._fused["att"]
    alloc_attn = P._fused["alloc_attn"]
    memset_vtm = P._fused["memset_vtm"]
    production = P._fused["production"]
    attn_l0 = P._fused["attn_l0"]
    attn_l1 = P._fused["attn_l1"]
    g = lambda n: env[n]
    xg, xo, w_in, b_in, w_out = g("xg"), g("xo"), g("w_in"), g("b_in"), g("w_out")
    w_gate, w_up, w_down = g("w_gate"), g("w_up"), g("w_down")
    wqT, wkT, wvT, woT = g("wqT"), g("wkT"), g("wvT"), g("woT")
    bq, bk, bv = g("bq"), g("bk"), g("bv")
    w1T, w2T = g("w1T"), g("w2T")
    bout_sb, bo_sb = g("bout_sb"), g("bo_sb")
    yf_hbm, ao_hbm = g("yf_hbm"), g("ao_hbm")
    t2o_hbm, t3o_hbm = g("t2o_hbm"), g("t3o_hbm")
    cc_in, cc_out, ccE_in, ccE_out = (g("cc_in"), g("cc_out"), g("ccE_in"),
                                      g("ccE_out"))
    wgqT, wuqT, wdqT = g("wgqT"), g("wuqT"), g("wdqT")

    def make_ln():
        lnscr = dict(m=al.get(512), t=al.get(512), rs=al.get(512),
                     sq=al.get(512), y=al.get(512))

        def ln_tile(pre_off, out_fn):
            for oc in range(NCH):
                src = T[:, pre_off + oc * 512: pre_off + oc * 512 + 512]
                P.op("tensor", lambda e, src=src, oc=oc: e.matmul(
                    ps[6][:], cst["onesM"][:], src,
                    start=(oc == 0), stop=(oc == NCH - 1)))
                P.op("scalar", lambda e, src=src: e.activation(
                    T[:, lnscr["sq"]:lnscr["sq"] + 512], src, AF.Square))
                P.op("tensor", lambda e, oc=oc: e.matmul(
                    ps[7][:], cst["onesM"][:],
                    T[:, lnscr["sq"]:lnscr["sq"] + 512],
                    start=(oc == 0), stop=(oc == NCH - 1)))
            M_ = T[:, lnscr["m"]:lnscr["m"] + 512]
            T_ = T[:, lnscr["t"]:lnscr["t"] + 512]
            R_ = T[:, lnscr["rs"]:lnscr["rs"] + 512]
            SQ = T[:, lnscr["sq"]:lnscr["sq"] + 512]
            Y_ = T[:, lnscr["y"]:lnscr["y"] + 512]
            P.op("vector", lambda e: e.tensor_copy(M_, ps[6][:]))
            P.op("vector", lambda e: e.tensor_tensor(T_, ps[6][:], M_, ALU.mult))
            P.op("vector", lambda e: e.tensor_tensor(T_, ps[7][:], T_,
                                                     ALU.subtract))
            P.op("vector", lambda e: e.tensor_scalar(T_, T_, 1.0e-5, None,
                                                     ALU.add))
            P.op("scalar", lambda e: e.activation(R_, T_, AF.Sqrt))
            P.op("vector", lambda e: e.reciprocal(R_, R_))
            for oc in range(NCH):
                src = T[:, pre_off + oc * 512: pre_off + oc * 512 + 512]
                P.op("vector", lambda e, src=src: e.tensor_tensor(
                    SQ, src, M_, ALU.subtract))
                P.op("vector", lambda e: e.tensor_tensor(Y_, SQ, R_, ALU.mult))
                out_fn(oc)
        return ln_tile, lnscr

    def attn_epilogue(wsrc, bias_sb, resid_src, write_fn):
        aos = al.get(4096)
        wo2 = al.get(1024)
        pre = al.get(4096)
        xr = al.get(4096)
        ln_tile, lnscr = make_ln()
        for s in range(NSLOT):
            P.op("sync", lambda e, s=s: e.dma_start(
                out=sb_slab(T, aos, 512), in_=fm_dram_slab(ao_hbm, s * 512, 512)),
                dma=True)
            P.op("sync", lambda e, s=s: e.dma_start(
                out=sb_slab(T, xr, 512),
                in_=fm_dram_slab(resid_src, s * 512, 512)), dma=True)
            for oc in range(NCH):
                P.op("sync", lambda e, oc=oc: e.dma_start(
                    out=sb_slab(T, wo2, 128),
                    in_=fm_dram_slab(wsrc, oc * 128, 128)), dma=True)
                for ch in range(NCH):
                    P.op("tensor", lambda e, ch=ch: e.matmul(
                        ps[5][:], T[:, wo2 + ch * 128: wo2 + ch * 128 + 128],
                        T[:, aos + ch * 512: aos + ch * 512 + 512],
                        start=(ch == 0), stop=(ch == NCH - 1)))
                P.op("vector", lambda e, oc=oc: e.scalar_tensor_tensor(
                    T[:, pre + oc * 512: pre + oc * 512 + 512], ps[5][:],
                    bias_sb[:, oc:oc + 1],
                    T[:, xr + oc * 512: xr + oc * 512 + 512],
                    ALU.add, ALU.add))
            write_fn(s, ln_tile, lnscr, pre)

    al.push()
    alloc_attn()
    memset_vtm()
    wt_o = al.get(1024)
    wq_o = al.get(1024)
    P._q4ax = al.get(1024)
    P._q4ac = al.get(1024)
    P._q4qq = al.get(1024)
    P._q4bm = al.get(64)
    P._q4rs = al.get(64)
    tb_o = al.get(1024)

    def emit_qunit(wsrc, wdst, rt, c0, w):
        P.op("sync", lambda e: e.dma_start(
            out=T[:, wt_o:wt_o + w],
            in_=wsrc[rt * 128:(rt + 1) * 128, c0:c0 + w]), dma=True)
        quant_fp4(P, T, wt_o, wq_o, w, w // 16)
        nj = w // 128
        for j in range(nj):
            pslot = 128 * (j % 4)
            P.op("tensor", lambda e, j=j, pslot=pslot: e.transpose(
                ps[4][:, pslot:pslot + 128],
                T[:, wq_o + j * 128: wq_o + j * 128 + 128],
                cst["ident"][:]))
            P.op("vector", lambda e, j=j, pslot=pslot: e.tensor_copy(
                T[:, tb_o + j * 128: tb_o + j * 128 + 128],
                ps[4][:, pslot:pslot + 128]))
        P.op("gpsimd", lambda e: e.dma_start(
            out=wdst[c0:c0 + w, rt * 128:(rt + 1) * 128].rearrange(
                "(j p) c -> p j c", p=128),
            in_=T[:, tb_o:tb_o + w].rearrange("p (j c) -> p j c", c=128)),
            dma=True)

    qunits = []
    for (wsrc, wdst, rows, cols) in ((w_gate, wgqT, I0, D), (w_up, wuqT, I0, D),
                                     (w_down, wdqT, D, I0)):
        for rt in range(rows // 128):
            for c0 in range(0, cols, 1024):
                qunits.append((wsrc, wdst, rt, c0, min(1024, cols - c0)))
    qstate = {"pts": NPAIR * 4 + (NPAIR - 3) * NSLOT}

    def drip():
        if qunits:
            k = -(-len(qunits) // max(1, qstate["pts"]))
            for _ in range(min(k, len(qunits))):
                emit_qunit(*qunits.pop(0))
        qstate["pts"] -= 1

    for p in range(NPAIR):
        production(w_in, 128 * p, b_in, 128 * p, 0,
                   lambda tt: fm_dram_slab(xo, tt * 512, 512), NSLOT,
                   att["q"], "fm")
        drip()
        production(w_in, D + 128 * p, b_in, D + 128 * p, 1,
                   lambda tt: fm_dram_slab(xg, tt * 512, 512), 8,
                   att["k"], "fm")
        drip()
        production(w_in, 2 * D + 128 * p, b_in, 2 * D + 128 * p, 2,
                   lambda tt: fm_dram_slab(xg, tt * 512, 512), 8,
                   None, "v")
        drip()
        attn_l0(p, filler=(drip if p >= 3 else None))
        drip()
    while qunits:
        emit_qunit(*qunits.pop(0))
    al.pop()
    al.push()

    def wr_t1(s, ln_tile, lnscr, pre):
        def wfn(oc, s=s):
            yo = lnscr["y"]
            P.op("gpsimd", lambda e, oc=oc, s=s, yo=yo: e.dma_start(
                out=yf_hbm.rearrange("(c p) s -> p c s", p=128)[
                    :, oc:oc + 1, s * 512:(s + 1) * 512].squeeze(1),
                in_=T[:, yo:yo + 512]), dma=True)
        ln_tile(pre, wfn)
    attn_epilogue(w_out, bout_sb, xo, wr_t1)
    al.pop()

    al.push()
    yf_o = al.get(4096)
    xq_o = al.get(4096)
    hA_o = al.get(22 * 512)
    hB_o = al.get(11 * 128)
    hBq_o = al.get(11 * 128)
    wg2_o = al.get(1024)
    wu2_o = al.get(1024)
    sg_o = al.get(1024)
    yfr_o = al.get(1024)
    P._q4ax = al.get(1408)
    P._q4ac = al.get(1408)
    P._q4qq = al.get(1408)
    P._q4bm = al.get(88)
    P._q4rs = al.get(88)
    ln2, lnscr2 = make_ln()
    xqtm_o = hB_o
    xtm_o = hBq_o
    pre2_o = hA_o
    assert al.off <= 33536, al.off

    def emit_xquant(s):
        P.op("sync", lambda e, s=s: e.dma_start(
            out=sb_slab(T, yf_o, 512), in_=fm_dram_slab(yf_hbm, s * 512, 512)),
            dma=True)
        for sub in range(4):
            for ch in range(NCH):
                tp = ps[7] if ch % 2 == 0 else ps[6]
                P.op("tensor", lambda e, ch=ch, sub=sub, tp=tp: e.transpose(
                    tp[:, 0:128],
                    T[:, yf_o + ch * 512 + sub * 128:
                      yf_o + ch * 512 + sub * 128 + 128], cst["ident"][:]))
                P.op("vector", lambda e, ch=ch, tp=tp: e.tensor_copy(
                    T[:, xtm_o + ch * 128: xtm_o + ch * 128 + 128],
                    tp[:, 0:128]))
            quant_fp4(P, T, xtm_o, xqtm_o, 1024, 64)
            for ch in range(NCH):
                tp = ps[7] if ch % 2 == 0 else ps[6]
                P.op("tensor", lambda e, ch=ch, tp=tp: e.transpose(
                    tp[:, 0:128],
                    T[:, xqtm_o + ch * 128: xqtm_o + ch * 128 + 128],
                    cst["ident"][:]))
                P.op("vector", lambda e, ch=ch, sub=sub, tp=tp: e.tensor_copy(
                    T[:, xq_o + ch * 512 + sub * 128:
                      xq_o + ch * 512 + sub * 128 + 128], tp[:, 0:128]))

    emit_xquant(0)
    for s in range(NSLOT):
        for ic in range(22):
            P.op("sync", lambda e, ic=ic: e.dma_start(
                out=sb_slab(T, wg2_o, 128),
                in_=fm_dram_slab(wgqT, ic * 128, 128)), dma=True)
            P.op("sync", lambda e, ic=ic: e.dma_start(
                out=sb_slab(T, wu2_o, 128),
                in_=fm_dram_slab(wuqT, ic * 128, 128)), dma=True)
            pg = ps[0] if ic % 2 == 0 else ps[2]
            pu = ps[1] if ic % 2 == 0 else ps[3]
            sg2 = sg_o + 512 * (ic % 2)
            for ch in range(NCH):
                P.op("tensor", lambda e, ch=ch, pg=pg: e.matmul(
                    pg[:], T[:, wg2_o + ch * 128: wg2_o + ch * 128 + 128],
                    T[:, xq_o + ch * 512: xq_o + ch * 512 + 512],
                    start=(ch == 0), stop=(ch == NCH - 1)))
            for ch in range(NCH):
                P.op("tensor", lambda e, ch=ch, pu=pu: e.matmul(
                    pu[:], T[:, wu2_o + ch * 128: wu2_o + ch * 128 + 128],
                    T[:, xq_o + ch * 512: xq_o + ch * 512 + 512],
                    start=(ch == 0), stop=(ch == NCH - 1)))
            P.op("scalar", lambda e, pg=pg, sg2=sg2: e.activation(
                T[:, sg2:sg2 + 512], pg[:], AF.Silu))
            P.op("vector", lambda e, ic=ic, pu=pu, sg2=sg2: e.tensor_tensor(
                T[:, hA_o + ic * 512: hA_o + ic * 512 + 512],
                T[:, sg2:sg2 + 512], pu[:], ALU.mult))
        for gg in range(2):
            for sub in range(4):
                for jj in range(11):
                    ic = gg * 11 + jj
                    tp = ps[7] if jj % 2 == 0 else ps[6]
                    P.op("tensor", lambda e, ic=ic, sub=sub, tp=tp: e.transpose(
                        tp[:, 0:128],
                        T[:, hA_o + ic * 512 + sub * 128:
                          hA_o + ic * 512 + sub * 128 + 128], cst["ident"][:]))
                    P.op("vector", lambda e, jj=jj, tp=tp: e.tensor_copy(
                        T[:, hB_o + jj * 128: hB_o + jj * 128 + 128],
                        tp[:, 0:128]))
                quant_fp4(P, T, hB_o, hBq_o, 1408, 88)
                for jj in range(11):
                    ic = gg * 11 + jj
                    tp = ps[7] if jj % 2 == 0 else ps[6]
                    P.op("tensor", lambda e, jj=jj, tp=tp: e.transpose(
                        tp[:, 0:128],
                        T[:, hBq_o + jj * 128: hBq_o + jj * 128 + 128],
                        cst["ident"][:]))
                    P.op("vector", lambda e, ic=ic, sub=sub, tp=tp: e.tensor_copy(
                        T[:, hA_o + ic * 512 + sub * 128:
                          hA_o + ic * 512 + sub * 128 + 128], tp[:, 0:128]))
        if s + 1 < NSLOT:
            emit_xquant(s + 1)
        for ic in range(22):
            wd2 = wg2_o if ic % 2 == 0 else wu2_o
            P.op("sync", lambda e, ic=ic, wd2=wd2: e.dma_start(
                out=T[:, wd2:wd2 + 1024],
                in_=wdqT[ic * 128:(ic + 1) * 128, :]), dma=True)
            for hc in range(NCH):
                P.op("tensor", lambda e, ic=ic, hc=hc, wd2=wd2: e.matmul(
                    ps[hc][:], T[:, wd2 + hc * 128: wd2 + hc * 128 + 128],
                    T[:, hA_o + ic * 512: hA_o + ic * 512 + 512],
                    start=(ic == 0), stop=(ic == 21)))
        for hc in range(NCH):
            yfr = yfr_o + 512 * (hc % 2)
            P.op("sync", lambda e, hc=hc, s=s, yfr=yfr: e.dma_start(
                out=T[:, yfr:yfr + 512],
                in_=fm_dram_slab(yf_hbm, s * 512, 512)[
                    :, hc:hc + 1, :].squeeze(1)), dma=True)
            P.op("vector", lambda e, hc=hc, yfr=yfr: e.tensor_tensor(
                T[:, pre2_o + hc * 512: pre2_o + hc * 512 + 512], ps[hc][:],
                T[:, yfr:yfr + 512], ALU.add))

        def wr2(oc, s=s):
            yo = lnscr2["y"]
            for dst in (t2o_hbm, cc_in):
                P.op("gpsimd", lambda e, oc=oc, s=s, yo=yo, dst=dst:
                     e.dma_start(
                         out=dst.rearrange("(c p) s -> p c s", p=128)[
                             :, oc:oc + 1, s * 512:(s + 1) * 512].squeeze(1),
                         in_=T[:, yo:yo + 512]), dma=True)
        ln2(pre2_o, wr2)
    al.pop()

    P.op("gpsimd", lambda e: e.collective_compute(
        "AllGather", ALU.bypass, replica_groups=CC_GROUPS,
        ins=[cc_in], outs=[cc_out]))

    def l1_src(tt):
        blk = 1 - (tt & 1)
        sc = tt // 2
        return cc_out.rearrange("(b c p) s -> p b c s", p=128, c=NCH)[
            :, blk, :, sc * 512:(sc + 1) * 512]

    al.push()
    alloc_attn(l0=False)
    att["e1"] = al.get(8 * 512)
    att["pw"] = al.get(1024)
    att["rd"] = al.get(1024)
    att["bv"] = al.get(8 * 130)
    att["sfx"] = al.get(4 * 130)
    memset_vtm()
    for p in range(NPAIR):
        production(wqT, 128 * p, bq, 128 * p, 0,
                   lambda tt: fm_dram_slab(t2o_hbm, tt * 512, 512), NSLOT,
                   att["q"], "fm")
        production(wkT, 128 * p, bk, 128 * p, 1, l1_src, 8, att["k"], "fm")
        production(wvT, 128 * p, bv, 128 * p, 2, l1_src, 8, None, "v")
        attn_l1(p)
    al.pop()
    al.push()

    def wr_t3(s, ln_tile, lnscr, pre):
        def wfn(oc, s=s):
            yo = lnscr["y"]
            P.op("gpsimd", lambda e, oc=oc, s=s, yo=yo: e.dma_start(
                out=t3o_hbm.rearrange("(c p) s -> p c s", p=128)[
                    :, oc:oc + 1, s * 512:(s + 1) * 512].squeeze(1),
                in_=T[:, yo:yo + 512]), dma=True)
            P.op("gpsimd", lambda e, oc=oc, s=s, yo=yo: e.dma_start(
                out=ccE_in[oc * 128:(oc + 1) * 128, 2 * s:2 * s + 1],
                in_=T[:, yo:yo + 1],
                allow_slow_non_contiguous=True), dma=True)
            P.op("gpsimd", lambda e, oc=oc, s=s, yo=yo: e.dma_start(
                out=ccE_in[oc * 128:(oc + 1) * 128, 2 * s + 1:2 * s + 2],
                in_=T[:, yo + 511:yo + 512],
                allow_slow_non_contiguous=True), dma=True)
        ln_tile(pre, wfn)
    attn_epilogue(woT, bo_sb, t2o_hbm, wr_t3)
    al.pop()

    P.op("gpsimd", lambda e: e.collective_compute(
        "AllGather", ALU.bypass, replica_groups=CC_GROUPS,
        ins=[ccE_in], outs=[ccE_out]))

    _fused_l1c(nc, P, al, alb, T, TB, wi32, cst, ps, semh, env)


def _fused_l1c(nc, P, al, alb, T, TB, wi32, cst, ps, semh, env):
    g = lambda n: env[n]
    w1T, w2T = g("w1T"), g("w2T")
    cw_sb, cb_sb = g("cw_sb"), g("cb_sb")
    t3o_hbm, ccE_out = g("t3o_hbm"), g("ccE_out")
    w1q_hbm, w2q_hbm = g("w1q_hbm"), g("w2q_hbm")
    out = g("out")
    selE = cst["selE"]

    al.push()
    alb.push()
    cs_o = al.get(8)
    c1_o = al.get(1)
    c2_o = al.get(1)
    al.push()
    wt_o = al.get(4096)
    wi_o = al.get(4096)
    alb.push()
    wtb_o = alb.get(4096)

    def i32rt(buf_off, n, pr=128):
        for c0 in range(0, n, 1024):
            cn = min(1024, n - c0)
            P.op("vector", lambda e, c0=c0, cn=cn, pr=pr: e.tensor_copy(
                wi32[0:pr, 0:cn], T[0:pr, buf_off + c0:buf_off + c0 + cn]))
            P.op("vector", lambda e, c0=c0, cn=cn, pr=pr: e.tensor_copy(
                T[0:pr, buf_off + c0:buf_off + c0 + cn], wi32[0:pr, 0:cn]))

    def ternary(wsrc, wdst, rows, cols, c_off):
        nrt = rows // 128
        P.op("gpsimd", lambda e: e.memset(T[:, cs_o:cs_o + 1], 0.0))
        for rt in range(nrt):
            P.op("sync", lambda e, rt=rt, wsrc=wsrc, cols=cols: e.dma_start(
                out=T[:, wt_o:wt_o + cols],
                in_=wsrc[rt * 128:(rt + 1) * 128, :]), dma=True)
            P.op("scalar", lambda e, cols=cols: e.activation(
                T[:, wi_o:wi_o + cols], T[:, wt_o:wt_o + cols], AF.Abs))
            P.op("vector", lambda e, cols=cols: e.tensor_reduce(
                T[:, cs_o + 1:cs_o + 2], T[:, wi_o:wi_o + cols], AX.X, ALU.add))
            P.op("vector", lambda e: e.tensor_tensor(
                T[:, cs_o:cs_o + 1], T[:, cs_o:cs_o + 1],
                T[:, cs_o + 1:cs_o + 2], ALU.add))
        P.op("tensor", lambda e: e.matmul(
            ps[7][0:1, 0:1], T[:, cs_o:cs_o + 1], cst["onesD"][:, 0:1],
            start=True, stop=True))
        P.op("vector", lambda e: e.tensor_copy(T[0:1, cs_o + 2:cs_o + 3],
                                               ps[7][0:1, 0:1]))
        P.op("tensor", lambda e: e.matmul(
            ps[7][0:128, 0:1], cst["onesD"][0:1, :], T[0:1, cs_o + 2:cs_o + 3],
            start=True, stop=True))
        P.op("vector", lambda e, rows=rows, cols=cols: e.tensor_scalar(
            T[:, c_off:c_off + 1], ps[7][0:128, 0:1],
            1.0 / float(rows * cols), 1.0e-5, ALU.mult, ALU.max))
        P.op("vector", lambda e: e.reciprocal(T[:, cs_o + 3:cs_o + 4],
                                              T[:, c_off:c_off + 1]))
        for rt in range(nrt):
            P.op("sync", lambda e, rt=rt, wsrc=wsrc, cols=cols: e.dma_start(
                out=T[:, wt_o:wt_o + cols],
                in_=wsrc[rt * 128:(rt + 1) * 128, :]), dma=True)
            P.op("vector", lambda e, cols=cols: e.tensor_scalar(
                T[:, wi_o:wi_o + cols], T[:, wt_o:wt_o + cols],
                T[:, cs_o + 3:cs_o + 4], None, ALU.mult))
            i32rt(wi_o, cols)
            P.op("vector", lambda e, cols=cols: e.tensor_scalar(
                T[:, wi_o:wi_o + cols], T[:, wi_o:wi_o + cols], 1.0, -1.0,
                ALU.min, ALU.max))
            P.op("vector", lambda e, cols=cols: e.tensor_copy(
                TB[:, wtb_o:wtb_o + cols], T[:, wi_o:wi_o + cols]))
            P.op("gpsimd", lambda e, rt=rt, wdst=wdst, cols=cols: e.dma_start(
                out=wdst[rt * 128:(rt + 1) * 128, :],
                in_=TB[:, wtb_o:wtb_o + cols]), dma=True)

    ternary(w1T, w1q_hbm, D, I1, c1_o)
    ternary(w2T, w2q_hbm, I1, D, c2_o)
    al.pop()
    alb.pop()

    cand_o = al.get(128)
    te_o = al.get(64)
    etm_o = al.get(1024)
    erow_o = al.get(16)
    febc_o = al.get(8)
    xqe_o = alb.get(64)
    for blk in range(2):
        P.op("sync", lambda e, blk=blk: e.dma_start(
            out=T[:, cand_o:cand_o + 128].rearrange(
                "p (c j) -> p c j", j=16)[:, :, blk * 8:(blk + 1) * 8],
            in_=ccE_out.rearrange("(b c p) s -> p b c s", p=128, c=NCH)[
                :, blk, :, :]), dma=True)
    for ed in range(8):
        for j in range(16):
            sap = selE[:, ed * 16 + j:ed * 16 + j + 1]
            src = T[:, cand_o:cand_o + 128].rearrange(
                "p (c j) -> p c j", j=16)[:, :, j:j + 1]
            dst = T[:, te_o:te_o + 64].rearrange(
                "p (c x) -> p c x", x=8)[:, :, ed:ed + 1]
            if j == 0:
                P.op("vector", lambda e, src=src, dst=dst, sap=sap:
                     e.tensor_scalar(dst, src, sap, None, ALU.mult))
            else:
                P.op("vector", lambda e, src=src, dst=dst, sap=sap:
                     e.scalar_tensor_tensor(dst, src, sap, dst,
                                            ALU.mult, ALU.add))
    for ch in range(NCH):
        P.op("tensor", lambda e, ch=ch: e.transpose(
            ps[7][0:8, 0:128], T[:, te_o + ch * 8: te_o + ch * 8 + 8],
            cst["ident"][:]))
        P.op("vector", lambda e, ch=ch: e.tensor_copy(
            T[0:8, etm_o + ch * 128: etm_o + ch * 128 + 128], ps[7][0:8, 0:128]))
    P.op("vector", lambda e: e.tensor_reduce(
        T[0:8, erow_o:erow_o + 1], T[0:8, etm_o:etm_o + 1024], AX.X, ALU.max,
        apply_absolute_value=True))
    P.op("vector", lambda e: e.tensor_scalar(
        T[0:8, erow_o:erow_o + 1], T[0:8, erow_o:erow_o + 1], 1.0e-5, None,
        ALU.max))
    P.op("vector", lambda e: e.reciprocal(
        T[0:8, erow_o + 1:erow_o + 2], T[0:8, erow_o:erow_o + 1]))
    P.op("vector", lambda e: e.tensor_scalar(
        T[0:8, etm_o:etm_o + 1024], T[0:8, etm_o:etm_o + 1024],
        T[0:8, erow_o + 1:erow_o + 2], 127.0, ALU.mult, ALU.mult))
    i32rt(etm_o, 1024, pr=8)
    for ch in range(NCH):
        P.op("tensor", lambda e, ch=ch: e.transpose(
            ps[7][0:128, 0:8], T[0:8, etm_o + ch * 128: etm_o + ch * 128 + 128],
            cst["ident"][0:8, 0:8]))
        P.op("vector", lambda e, ch=ch: e.tensor_copy(
            TB[:, xqe_o + ch * 8: xqe_o + ch * 8 + 8], ps[7][0:128, 0:8]))
    P.op("vector", lambda e: e.tensor_scalar(
        T[0:8, erow_o + 2:erow_o + 3], T[0:8, erow_o:erow_o + 1],
        T[0:8, c1_o:c1_o + 1], 1.0 / 127.0, ALU.mult, ALU.mult))
    P.op("tensor", lambda e: e.transpose(
        ps[7][0:1, 0:8], T[0:8, erow_o + 2:erow_o + 3], cst["ident"][0:8, 0:8]))
    P.op("vector", lambda e: e.tensor_copy(T[0:1, erow_o + 4:erow_o + 12],
                                           ps[7][0:1, 0:8]))
    P.op("tensor", lambda e: e.matmul(
        ps[7][0:128, 0:8], cst["onesD"][0:1, :], T[0:1, erow_o + 4:erow_o + 12],
        start=True, stop=True))
    P.op("vector", lambda e: e.tensor_copy(T[:, febc_o:febc_o + 8],
                                           ps[7][0:128, 0:8]))

    xt_o = al.get(4096)
    xtm_o = al.get(1024)
    ctok_o = al.get(16)
    f_o = al.get(4)
    frow_o = al.get(512)
    fbc_o = al.get(512)
    fbc2_o = al.get(512)
    stm_o = al.get(4 * 4096)
    hx_o = al.get(514)
    pre_o = al.get(4096)
    tmp_o = al.get(512)
    lnm_o = al.get(512)
    lnt_o = al.get(512)
    lnr_o = al.get(512)
    lnq_o = al.get(512)
    lny_o = al.get(512)
    xqs_o = alb.get(4096)
    w1t_o = alb.get(1024)
    w2t_o = alb.get(1024)
    xq2_o = alb.get(32 * 512)
    assert al.off <= 33536, al.off
    assert alb.off <= 22656, alb.off

    def int8_quant_tm(src_off, n, sub, c_scale_off):
        P.op("vector", lambda e, n=n: e.tensor_reduce(
            T[:, ctok_o:ctok_o + 1], T[:, src_off:src_off + n], AX.X, ALU.max,
            apply_absolute_value=True))
        P.op("vector", lambda e: e.tensor_scalar(
            T[:, ctok_o:ctok_o + 1], T[:, ctok_o:ctok_o + 1], 1.0e-5, None,
            ALU.max))
        P.op("vector", lambda e: e.reciprocal(
            T[:, ctok_o + 1:ctok_o + 2], T[:, ctok_o:ctok_o + 1]))
        P.op("vector", lambda e, n=n: e.tensor_scalar(
            T[:, pre_o:pre_o + n], T[:, src_off:src_off + n],
            T[:, ctok_o + 1:ctok_o + 2], 127.0, ALU.mult, ALU.mult))
        i32rt(pre_o, n)
        P.op("vector", lambda e, sub=sub: e.tensor_scalar(
            T[:, f_o + sub:f_o + sub + 1], T[:, ctok_o:ctok_o + 1],
            T[:, c_scale_off:c_scale_off + 1], 1.0 / 127.0, ALU.mult, ALU.mult))

    def factor_bcast(dst_off):
        for sub in range(4):
            P.op("tensor", lambda e, sub=sub: e.transpose(
                ps[7][0:1, 0:128], T[:, f_o + sub:f_o + sub + 1],
                cst["ident"][:]))
            P.op("vector", lambda e, sub=sub: e.tensor_copy(
                T[0:1, frow_o + sub * 128: frow_o + sub * 128 + 128],
                ps[7][0:1, 0:128]))
        P.op("tensor", lambda e: e.matmul(
            ps[7][0:128, 0:512], cst["onesD"][0:1, :],
            T[0:1, frow_o:frow_o + 512], start=True, stop=True))
        P.op("vector", lambda e, dst_off=dst_off: e.tensor_copy(
            T[:, dst_off:dst_off + 512], ps[7][0:128, 0:512]))

    for s in range(NSLOT):
        P.op("sync", lambda e, s=s: e.dma_start(
            out=sb_slab(T, xt_o, 512), in_=fm_dram_slab(t3o_hbm, s * 512, 512)),
            dma=True)
        for sub in range(4):
            for ch in range(NCH):
                tp = ps[7] if ch % 2 == 0 else ps[6]
                P.op("tensor", lambda e, ch=ch, sub=sub, tp=tp: e.transpose(
                    tp[:, 0:128],
                    T[:, xt_o + ch * 512 + sub * 128:
                      xt_o + ch * 512 + sub * 128 + 128], cst["ident"][:]))
                P.op("vector", lambda e, ch=ch, tp=tp: e.tensor_copy(
                    T[:, xtm_o + ch * 128: xtm_o + ch * 128 + 128],
                    tp[:, 0:128]))
            int8_quant_tm(xtm_o, 1024, sub, c1_o)
            for ch in range(NCH):
                tp = ps[7] if ch % 2 == 0 else ps[6]
                P.op("tensor", lambda e, ch=ch, tp=tp: e.transpose(
                    tp[:, 0:128],
                    T[:, pre_o + ch * 128: pre_o + ch * 128 + 128],
                    cst["ident"][:]))
                P.op("vector", lambda e, ch=ch, sub=sub, tp=tp: e.tensor_copy(
                    TB[:, xqs_o + ch * 512 + sub * 128:
                       xqs_o + ch * 512 + sub * 128 + 128], tp[:, 0:128]))
        factor_bcast(fbc_o)
        for ich in range(32):
            P.op("sync", lambda e, ich=ich: e.dma_start(
                out=sb_slab(TB, w1t_o, 128),
                in_=fm_dram_slab(w1q_hbm, ich * 128, 128)), dma=True)
            ph = ps[0] if ich % 2 == 0 else ps[2]
            pe2 = ps[1] if ich % 2 == 0 else ps[3]
            for ch in range(NCH):
                P.op("tensor", lambda e, ch=ch, ph=ph: e.matmul(
                    ph[:], TB[:, w1t_o + ch * 128: w1t_o + ch * 128 + 128],
                    TB[:, xqs_o + ch * 512: xqs_o + ch * 512 + 512],
                    start=(ch == 0), stop=(ch == NCH - 1)))
            for ch in range(NCH):
                P.op("tensor", lambda e, ch=ch, s=s, pe2=pe2: e.matmul(
                    pe2[:, 0:2],
                    TB[:, w1t_o + ch * 128: w1t_o + ch * 128 + 128],
                    TB[:, xqe_o + ch * 8 + 2 * s: xqe_o + ch * 8 + 2 * s + 2],
                    start=(ch == 0), stop=(ch == NCH - 1)))
            P.op("vector", lambda e, ph=ph: e.tensor_tensor(
                T[:, hx_o + 1:hx_o + 513], ph[:], T[:, fbc_o:fbc_o + 512],
                ALU.mult))
            P.op("vector", lambda e, s=s, pe2=pe2: e.tensor_scalar(
                T[:, hx_o:hx_o + 1], pe2[:, 0:1],
                T[:, febc_o + 2 * s:febc_o + 2 * s + 1], None, ALU.mult))
            P.op("vector", lambda e, s=s, pe2=pe2: e.tensor_scalar(
                T[:, hx_o + 513:hx_o + 514], pe2[:, 1:2],
                T[:, febc_o + 2 * s + 1:febc_o + 2 * s + 2], None, ALU.mult))
            P.op("vector", lambda e, ich=ich: e.tensor_scalar(
                T[:, tmp_o:tmp_o + 512], T[:, hx_o:hx_o + 512],
                cw_sb[:, ich * 3:ich * 3 + 1], cb_sb[:, ich:ich + 1],
                ALU.mult, ALU.add))
            P.op("vector", lambda e, ich=ich: e.scalar_tensor_tensor(
                T[:, tmp_o:tmp_o + 512], T[:, hx_o + 1:hx_o + 513],
                cw_sb[:, ich * 3 + 1:ich * 3 + 2], T[:, tmp_o:tmp_o + 512],
                ALU.mult, ALU.add))
            P.op("vector", lambda e, ich=ich: e.scalar_tensor_tensor(
                T[:, tmp_o:tmp_o + 512], T[:, hx_o + 2:hx_o + 514],
                cw_sb[:, ich * 3 + 2:ich * 3 + 3], T[:, tmp_o:tmp_o + 512],
                ALU.mult, ALU.add))
            P.op("scalar", lambda e: e.activation(
                T[:, tmp_o:tmp_o + 512], T[:, tmp_o:tmp_o + 512], AF.Silu))
            for sub in range(4):
                tp = ps[7] if sub % 2 == 0 else ps[6]
                P.op("tensor", lambda e, sub=sub, tp=tp: e.transpose(
                    tp[:, 0:128],
                    T[:, tmp_o + sub * 128: tmp_o + sub * 128 + 128],
                    cst["ident"][:]))
                P.op("vector", lambda e, sub=sub, ich=ich, tp=tp: e.tensor_copy(
                    T[:, stm_o + sub * 4096 + ich * 128:
                      stm_o + sub * 4096 + ich * 128 + 128], tp[:, 0:128]))
        for sub in range(4):
            int8_quant_tm(stm_o + sub * 4096, 4096, sub, c2_o)
            for ich in range(32):
                tp = ps[7] if ich % 2 == 0 else ps[6]
                P.op("tensor", lambda e, ich=ich, tp=tp: e.transpose(
                    tp[:, 0:128],
                    T[:, pre_o + ich * 128: pre_o + ich * 128 + 128],
                    cst["ident"][:]))
                P.op("vector", lambda e, ich=ich, sub=sub, tp=tp: e.tensor_copy(
                    TB[:, xq2_o + ich * 512 + sub * 128:
                       xq2_o + ich * 512 + sub * 128 + 128], tp[:, 0:128]))
        factor_bcast(fbc2_o)
        for ich in range(32):
            P.op("sync", lambda e, ich=ich: e.dma_start(
                out=TB[:, w2t_o:w2t_o + 1024],
                in_=w2q_hbm[ich * 128:(ich + 1) * 128, :]), dma=True)
            for hc in range(NCH):
                P.op("tensor", lambda e, ich=ich, hc=hc: e.matmul(
                    ps[hc][:], TB[:, w2t_o + hc * 128: w2t_o + hc * 128 + 128],
                    TB[:, xq2_o + ich * 512: xq2_o + ich * 512 + 512],
                    start=(ich == 0), stop=(ich == 31)))
        for hc in range(NCH):
            P.op("vector", lambda e, hc=hc: e.tensor_tensor(
                T[:, tmp_o:tmp_o + 512], ps[hc][:], T[:, fbc2_o:fbc2_o + 512],
                ALU.mult))
            P.op("vector", lambda e, hc=hc: e.tensor_tensor(
                T[:, pre_o + hc * 512: pre_o + hc * 512 + 512],
                T[:, tmp_o:tmp_o + 512],
                T[:, xt_o + hc * 512: xt_o + hc * 512 + 512], ALU.add))
        for oc in range(NCH):
            src = T[:, pre_o + oc * 512: pre_o + oc * 512 + 512]
            P.op("tensor", lambda e, src=src, oc=oc: e.matmul(
                ps[6][:], cst["onesM"][:], src,
                start=(oc == 0), stop=(oc == NCH - 1)))
            P.op("scalar", lambda e, src=src: e.activation(
                T[:, lnq_o:lnq_o + 512], src, AF.Square))
            P.op("tensor", lambda e, oc=oc: e.matmul(
                ps[7][:], cst["onesM"][:], T[:, lnq_o:lnq_o + 512],
                start=(oc == 0), stop=(oc == NCH - 1)))
        P.op("vector", lambda e: e.tensor_copy(T[:, lnm_o:lnm_o + 512], ps[6][:]))
        P.op("vector", lambda e: e.tensor_tensor(
            T[:, lnt_o:lnt_o + 512], ps[6][:], T[:, lnm_o:lnm_o + 512], ALU.mult))
        P.op("vector", lambda e: e.tensor_tensor(
            T[:, lnt_o:lnt_o + 512], ps[7][:], T[:, lnt_o:lnt_o + 512],
            ALU.subtract))
        P.op("vector", lambda e: e.tensor_scalar(
            T[:, lnt_o:lnt_o + 512], T[:, lnt_o:lnt_o + 512], 1.0e-5, None,
            ALU.add))
        P.op("scalar", lambda e: e.activation(
            T[:, lnr_o:lnr_o + 512], T[:, lnt_o:lnt_o + 512], AF.Sqrt))
        P.op("vector", lambda e: e.reciprocal(
            T[:, lnr_o:lnr_o + 512], T[:, lnr_o:lnr_o + 512]))
        for oc in range(NCH):
            src = T[:, pre_o + oc * 512: pre_o + oc * 512 + 512]
            P.op("vector", lambda e, src=src: e.tensor_tensor(
                T[:, lnq_o:lnq_o + 512], src, T[:, lnm_o:lnm_o + 512],
                ALU.subtract))
            P.op("vector", lambda e: e.tensor_tensor(
                T[:, lny_o:lny_o + 512], T[:, lnq_o:lnq_o + 512],
                T[:, lnr_o:lnr_o + 512], ALU.mult))
            P.op("gpsimd", lambda e, oc=oc, s=s: e.dma_start(
                out=out.rearrange("(c p) s -> p c s", p=128)[
                    :, oc:oc + 1, s * 512:(s + 1) * 512].squeeze(1),
                in_=T[:, lny_o:lny_o + 512]), dma=True)
    al.pop()
    alb.pop()
    P.finalize(semh)
